# version 03
# baseline (speedup 1.0000x reference)
import contextlib
import numpy as np
import concourse.bass as bass
import concourse.mybir as mybir
from concourse.bass_utils import run_bass_kernel_spmd

F32 = mybir.dt.float32
BF16 = mybir.dt.bfloat16
AF = mybir.ActivationFunctionType
ALU = mybir.AluOpType
AX = mybir.AxisListType

D = 2048
T = 2048
NCH = 16
NTT = 16
NCORES = 8

E = 16
FI = 1408
SI = 2816
TOPK = 6
TBF = 1024
NBF = T // TBF

HA, NA, LCH = 32, 64, 128
HB, NOPE, ROPE, VHD, QKD = 16, 128, 64, 128, 192

DEBUG_TAPS = False
TENS, VEC, SCAL, POOL, SYNC = "tensor", "vector", "scalar", "gpsimd", "sync"
ENGINES = [TENS, VEC, SCAL, POOL, SYNC]



def _relayout(W):
    K, M = W.shape
    assert K % 128 == 0 and M % 128 == 0, (K, M)
    nK, nM = K // 128, M // 128
    return np.ascontiguousarray(
        W.reshape(nK, 128, nM, 128).transpose(2, 1, 0, 3).reshape(nM, 128, nK * 128))


def _pad(W, k=None, m=None):
    K, M = W.shape
    k, m = k or K, m or M
    if (k, m) == (K, M):
        return np.ascontiguousarray(W)
    out = np.zeros((k, m), W.dtype)
    out[:K, :M] = W
    return out



class Tok:
    __slots__ = ("eng", "val", "sem")
    def __init__(self, eng, val, sem=None):
        self.eng, self.val, self.sem = eng, val, sem


class Prog:
    def __init__(self, nc, ctx):
        self.nc = nc
        self.ctx = ctx
        self.steps = []
        self.esem = {e: ctx.enter_context(nc.semaphore("es_" + e)) for e in ENGINES}
        self.ecnt = {e: 0 for e in ENGINES}
        self.dsems = {}
        self.guard = []

    def _dsem(self, name):
        if name not in self.dsems:
            self.dsems[name] = [self.ctx.enter_context(self.nc.semaphore("ds_" + name)), 0]
        return self.dsems[name]

    def do(self, eng, fn, after=()):
        self.ecnt[eng] += 1
        tok = Tok(eng, self.ecnt[eng])
        deps = tuple(t for t in after if t is not None) + tuple(self.guard)
        self.steps.append((eng, fn, deps, None))
        return tok

    def dma(self, eng, sem_name, fn, after=()):
        rec = self._dsem(sem_name)
        prev = rec[1]
        rec[1] += 16
        tok = Tok(None, rec[1], sem_name)
        deps = tuple(t for t in after if t is not None) + tuple(self.guard)
        if prev > 0:
            deps = deps + (Tok(None, prev, sem_name),)
        self.steps.append((eng, fn, deps, sem_name))
        return tok

    def set_guard(self):
        self.guard = [Tok(e, self.ecnt[e]) for e in ENGINES if self.ecnt[e] > 0] + \
                     [Tok(None, rec[1], n) for n, rec in self.dsems.items()]

    def emit(self, block):
        seen = {e: {} for e in ENGINES}

        def emit_for(eng):
            def body(E_):
                for s_eng, fn, deps, dsem in self.steps:
                    if s_eng != eng:
                        continue
                    for d in deps:
                        if d.sem is None:
                            if d.eng == eng and d.val >= self.ecnt[eng]:
                                continue
                            tgt = self.esem[d.eng]
                        else:
                            tgt = self.dsems[d.sem][0]
                        key = id(tgt)
                        if seen[eng].get(key, -1) < d.val:
                            E_.wait_ge(tgt, d.val)
                            seen[eng][key] = d.val
                    instr = fn(E_)
                    if dsem is not None:
                        instr.then_inc(self.dsems[dsem][0], 16)
                    else:
                        instr.then_inc(self.esem[eng], 1)
            return body

        used = {s[0] for s in self.steps}
        for eng in ENGINES:
            if eng in used:
                getattr(block, eng)(emit_for(eng))

    def n_steps(self):
        return len(self.steps)



class Build:
    def __init__(self):
        self.ctx = contextlib.ExitStack()
        self.nc = bass.Bass("TRN2", target_bir_lowering=False)
        self.p = Prog(self.nc, self.ctx)
        self.consts = {}
        self.const_ap = {}
        self.phase_ctx = None
        self.com = {}

    def dram_in(self, name, shape, dtype=F32):
        return self.nc.dram_tensor(name, list(shape), dtype, kind="ExternalInput").ap()

    def dram_out(self, name, shape, dtype=F32):
        return self.nc.dram_tensor(name, list(shape), dtype, kind="ExternalOutput").ap()

    def dram_tmp(self, name, shape, dtype=F32):
        return self.nc.dram_tensor(name, list(shape), dtype, kind="Internal").ap()

    def sb(self, name, shape, dtype):
        stk = self.phase_ctx if self.phase_ctx is not None else self.ctx
        return stk.enter_context(self.nc.sbuf_tensor(name, list(shape), dtype))

    def sbp(self, name, shape, dtype):
        return self.ctx.enter_context(
            self.nc.sbuf_tensor(name, list(shape), dtype, side="right"))

    def ps(self, name, shape, dtype=F32):
        stk = self.phase_ctx if self.phase_ctx is not None else self.ctx
        return stk.enter_context(self.nc.psum_tensor(name, list(shape), dtype))

    def const(self, name, arr):
        if name not in self.const_ap:
            self.consts["cst_" + name] = np.ascontiguousarray(arr)
            dt = {"float32": F32, "bfloat16": BF16}[str(arr.dtype)]
            self.const_ap[name] = self.dram_in("cst_" + name, arr.shape, dt)
        return self.const_ap[name]

    def const_tile(self, name, arr):
        key = "ct_" + name
        if key not in self.com:
            ap = self.const(name, arr)
            dt = {"float32": F32, "bfloat16": BF16}[str(arr.dtype)]
            sbt = self.ctx.enter_context(
                self.nc.sbuf_tensor(key, list(arr.shape), dt, side="right"))
            tok = self.p.dma(SYNC, "cst",
                             lambda e, ap=ap, sbt=sbt: e.dma_start(out=sbt[:], in_=ap))
            self.p.guard = list(self.p.guard) + [tok]
            self.com[key] = sbt
        return self.com[key]

    @contextlib.contextmanager
    def phase(self, name):
        assert self.phase_ctx is None
        self.phase_ctx = contextlib.ExitStack()
        try:
            yield
        finally:
            self.phase_ctx.close()
            self.phase_ctx = None
            self.p.set_guard()


def bf16c(b):
    import ml_dtypes
    return ml_dtypes.bfloat16



def emit_ln(b, xsrc, xln, pfx):
    p = b.p
    bf = bf16c(b)
    ones_b = b.const_tile("ones128_b", np.ones((128, 1), bf))
    ones_f = b.const_tile("ones128_f", np.ones((1, 128), np.float32))
    assert b.phase_ctx is None, "emit_ln manages its own phases"
    outer = contextlib.ExitStack()
    r1 = outer.enter_context(b.nc.sbuf_tensor(pfx + "r1", [1, 2048], F32))
    r2 = outer.enter_context(b.nc.sbuf_tensor(pfx + "r2", [1, 2048], F32))
    b.phase_ctx = contextlib.ExitStack()
    stats = b.ps(pfx + "st", [128, 4096], F32)
    xsq = b.sb(pfx + "xsq", [128, 1024], BF16)

    ld = [p.dma(POOL, pfx + "x%d" % (c % 2),
                lambda e, c=c: e.dma_start(out=xln[:, c * T:(c + 1) * T],
                                           in_=xsrc[c * 128:(c + 1) * 128, :]))
          for c in range(NCH)]

    NNB = T // 512
    sq, mmt = {}, []
    for c in range(NCH):
        for nb in range(NNB):
            i = c * NNB + nb
            deps = [ld[c]] + ([mmt[i - 2]] if i >= 2 else [])
            sq[i] = p.do(SCAL, lambda e, c=c, nb=nb, i=i: e.activation(
                out=xsq[:, (i % 2) * 512:(i % 2) * 512 + 512],
                in_=xln[:, c * T + nb * 512:c * T + nb * 512 + 512],
                func=AF.Square), after=deps)
            def mm(e, c=c, nb=nb, i=i):
                e.matmul(stats[0:1, nb * 512:nb * 512 + 512], ones_b[:],
                         xln[:, c * T + nb * 512:c * T + nb * 512 + 512],
                         start=(c == 0), stop=(c == NCH - 1))
                return e.matmul(stats[0:1, 2048 + nb * 512:2048 + nb * 512 + 512], ones_b[:],
                                xsq[:, (i % 2) * 512:(i % 2) * 512 + 512],
                                start=(c == 0), stop=(c == NCH - 1))
            mmt.append(p.do(TENS, mm, after=[ld[c], sq[i]]))

    t0 = None
    for base, dstr in ((0, r1), (2048, r2)):
        for q in range(T // 512):
            t0 = p.do(VEC, lambda e, q=q, base=base, dstr=dstr: e.tensor_copy(
                out=dstr[0:1, q * 512:(q + 1) * 512],
                in_=stats[0:1, base + q * 512:base + (q + 1) * 512]),
                after=[mmt[-1]])
    t1 = p.do(VEC, lambda e: e.tensor_scalar(out=r1[0:1, 0:T], in0=r1[0:1, 0:T],
                                             scalar1=1.0 / D, scalar2=None, op0=ALU.mult),
              after=[t0])
    t2 = p.do(VEC, lambda e: e.tensor_scalar(out=r2[0:1, 0:T], in0=r2[0:1, 0:T],
                                             scalar1=1.0 / D, scalar2=None, op0=ALU.mult),
              after=[t0])
    t3 = p.do(VEC, lambda e: e.tensor_tensor(out=stats[0:1, 0:T], in0=r1[0:1, 0:T],
                                             in1=r1[0:1, 0:T], op=ALU.mult), after=[t1])
    t4 = p.do(VEC, lambda e: e.tensor_tensor(out=r2[0:1, 0:T], in0=r2[0:1, 0:T],
                                             in1=stats[0:1, 0:T], op=ALU.subtract),
              after=[t2, t3])
    t4b = p.do(VEC, lambda e: e.tensor_scalar(out=r2[0:1, 0:T], in0=r2[0:1, 0:T],
                                              scalar1=1e-5, scalar2=None, op0=ALU.add),
               after=[t4])
    t5 = p.do(SCAL, lambda e: e.activation(out=stats[0:1, 0:T], in_=r2[0:1, 0:T],
                                           func=AF.Sqrt), after=[t4b])
    t6 = p.do(VEC, lambda e: e.reciprocal(out=r2[0:1, 0:T], in_=stats[0:1, 0:T]),
              after=[t5])
    t7 = p.do(VEC, lambda e: e.tensor_tensor(out=r1[0:1, 0:T], in0=r1[0:1, 0:T],
                                             in1=r2[0:1, 0:T], op=ALU.mult), after=[t6])
    t8 = p.do(VEC, lambda e: e.tensor_scalar(out=r1[0:1, 0:T], in0=r1[0:1, 0:T],
                                             scalar1=-1.0, scalar2=None, op0=ALU.mult),
              after=[t7])
    t9 = t8
    b.phase_ctx.close(); b.phase_ctx = None; p.set_guard()
    b.phase_ctx = contextlib.ExitStack()
    bcAB = b.ps(pfx + "bc", [128, 4096], F32)

    bc = []
    for nb in range(T // 512):
        def mm(e, nb=nb):
            e.matmul(bcAB[:, nb * 512:nb * 512 + 512], ones_f[:],
                     r2[0:1, nb * 512:nb * 512 + 512], start=True, stop=True)
            return e.matmul(bcAB[:, 2048 + nb * 512:2048 + nb * 512 + 512], ones_f[:],
                            r1[0:1, nb * 512:nb * 512 + 512],
                            start=True, stop=True)
        bc.append(p.do(TENS, mm, after=[t8, t9]))

    out_toks = []
    for c in range(NCH):
        for nb in range(T // 512):
            sl = slice(c * T + nb * 512, c * T + nb * 512 + 512)
            a = p.do(VEC, lambda e, sl=sl, nb=nb: e.tensor_tensor(
                out=xln[:, sl], in0=xln[:, sl],
                in1=bcAB[:, nb * 512:nb * 512 + 512], op=ALU.mult), after=[bc[nb], mmt[-1]])
            out_toks.append(p.do(VEC, lambda e, sl=sl, nb=nb: e.tensor_tensor(
                out=xln[:, sl], in0=xln[:, sl],
                in1=bcAB[:, 2048 + nb * 512:2048 + nb * 512 + 512], op=ALU.add), after=[a]))
    b.phase_ctx.close(); b.phase_ctx = None
    outer.close()
    p.set_guard()
    return out_toks



def emit_ffn(b, xsrc, dst, W, pfx):
    p = b.p
    bf = bf16c(b)
    ffn_outer = contextlib.ExitStack()
    cT = ffn_outer.enter_context(b.nc.sbuf_tensor(pfx + "cT", [16, 2048], F32))
    if "xln" not in b.com:
        b.com["xln"] = b.sbp("g_xln", [128, NCH * T], BF16)
    xln = b.com["xln"]
    ln_done = emit_ln(b, xsrc, xln, pfx + "l")

    esel_np = np.zeros((16, 16 * 128), np.float32)
    for e_ in range(16):
        esel_np[e_, e_ * 128:(e_ + 1) * 128] = 1.0
    with b.phase(pfx + "gate"):
        gwT = b.sb(pfx + "gw", [128, NCH * 16], BF16)
        ldg = p.dma(POOL, pfx + "gw", lambda e: e.dma_start(out=gwT[:], in_=W["f_gate"]))
        logit = b.ps(pfx + "lg", [128, 2048], F32)
        cps = b.ps(pfx + "cps", [128, 1024], F32)
        comb = b.sb(pfx + "cmb", [128, NTT * 16], F32)
        srow = b.sb(pfx + "srw", [128, 48], F32)
        mrow = b.sb(pfx + "mrw", [128, 2], F32)
        identf = b.const_tile("identf", np.eye(128, dtype=np.float32))

        slot_free = {}
        comb_done = {}
        for tt in range(NTT):
            deps = [ldg] + ([slot_free[tt - 4]] if tt >= 4 else [])
            def mm(e, tt=tt):
                for kc in range(NCH):
                    last = e.matmul(logit[:, (tt % 4) * 512:(tt % 4) * 512 + 16],
                                    xln[:, kc * T + tt * 128:kc * T + tt * 128 + 128],
                                    gwT[:, kc * 16:(kc + 1) * 16],
                                    start=(kc == 0), stop=(kc == NCH - 1))
                return last
            g0 = p.do(TENS, mm, after=deps)
            sl = logit[:, (tt % 4) * 512:(tt % 4) * 512 + 16]
            a0 = p.do(VEC, lambda e, sl=sl: e.reduce_max(out=mrow[:, 0:1], in_=sl, axis=AX.X),
                      after=[g0])
            a1 = p.do(VEC, lambda e: e.tensor_scalar(out=mrow[:, 1:2], in0=mrow[:, 0:1],
                                                     scalar1=-1.0, scalar2=None, op0=ALU.mult),
                      after=[a0])
            a2 = p.do(SCAL, lambda e, sl=sl: e.activation(out=srow[:, 0:16], in_=sl,
                                                          func=AF.Exp, bias=mrow[:, 1:2]),
                      after=[a1])
            slot_free[tt] = a2
            a3 = p.do(VEC, lambda e: e.reduce_sum(out=mrow[:, 0:1], in_=srow[:, 0:16],
                                                  axis=AX.X), after=[a2])
            a4 = p.do(VEC, lambda e: e.reciprocal(out=mrow[:, 1:2], in_=mrow[:, 0:1]),
                      after=[a3])
            cur = p.do(VEC, lambda e: e.tensor_scalar(out=srow[:, 0:16], in0=srow[:, 0:16],
                                                      scalar1=mrow[:, 1:2], scalar2=None,
                                                      op0=ALU.mult), after=[a4])
            cur = p.do(VEC, lambda e: e.tensor_copy(out=srow[:, 16:32], in_=srow[:, 0:16]),
                       after=[cur])
            for it in range(TOPK - 1):
                m0 = p.do(VEC, lambda e: e.reduce_max(out=mrow[:, 0:1], in_=srow[:, 16:32],
                                                      axis=AX.X), after=[cur])
                m1 = p.do(VEC, lambda e: e.tensor_scalar(out=srow[:, 32:48],
                                                         in0=srow[:, 16:32],
                                                         scalar1=mrow[:, 0:1], scalar2=None,
                                                         op0=ALU.is_lt), after=[m0])
                cur = p.do(VEC, lambda e: e.tensor_tensor(out=srow[:, 16:32],
                                                          in0=srow[:, 16:32],
                                                          in1=srow[:, 32:48], op=ALU.mult),
                           after=[m1])
            m0 = p.do(VEC, lambda e: e.reduce_max(out=mrow[:, 0:1], in_=srow[:, 16:32],
                                                  axis=AX.X), after=[cur])
            m1 = p.do(VEC, lambda e: e.tensor_scalar(out=srow[:, 32:48], in0=srow[:, 0:16],
                                                     scalar1=mrow[:, 0:1], scalar2=None,
                                                     op0=ALU.is_ge), after=[m0])
            m2 = p.do(VEC, lambda e: e.tensor_tensor(out=srow[:, 16:32], in0=srow[:, 0:16],
                                                     in1=srow[:, 32:48], op=ALU.mult),
                      after=[m1])
            m3 = p.do(VEC, lambda e: e.reduce_sum(out=mrow[:, 0:1], in_=srow[:, 16:32],
                                                  axis=AX.X), after=[m2])
            m4 = p.do(VEC, lambda e: e.tensor_scalar(out=mrow[:, 0:1], in0=mrow[:, 0:1],
                                                     scalar1=1e-20, scalar2=None, op0=ALU.add),
                      after=[m3])
            m5 = p.do(VEC, lambda e: e.reciprocal(out=mrow[:, 1:2], in_=mrow[:, 0:1]),
                      after=[m4])
            comb_done[tt] = p.do(VEC, lambda e, tt=tt: e.tensor_scalar(
                out=comb[:, tt * 16:(tt + 1) * 16], in0=srow[:, 16:32],
                scalar1=mrow[:, 1:2], scalar2=None, op0=ALU.mult), after=[m5])

        prev_par = {}
        for tt in range(NTT):
            def tr(e, tt=tt):
                return e.transpose(cps[0:16, (tt % 2) * 512:(tt % 2) * 512 + 128],
                                   comb[:, tt * 16:(tt + 1) * 16], identf[:])
            trt = p.do(TENS, tr, after=[comb_done[tt], prev_par.get(tt % 2)])
            prev_par[tt % 2] = p.do(VEC, lambda e, tt=tt: e.tensor_copy(
                out=cT[:, tt * 128:(tt + 1) * 128],
                in_=cps[0:16, (tt % 2) * 512:(tt % 2) * 512 + 128]), after=[trt])

    with b.phase(pfx + "ex"):
        acc = b.sb(pfx + "acc", [128, NCH * TBF], F32)
        nsil = max(11 * TBF, 22 * 512)
        sil = b.sb(pfx + "sil", [128, 2 * TBF], BF16)
        hbuf = b.sb(pfx + "h", [128, nsil], BF16)
        w13b = b.sb(pfx + "w13", [128, 2 * 2048], BF16)
        w2b = b.sb(pfx + "w2", [128, 2 * SI], BF16)
        xrt = b.sb(pfx + "xr", [128, TBF], F32)
        mps = b.ps(pfx + "mps", [128, 3072], F32)
        cbps = b.ps(pfx + "cb", [128, 1024], F32)
        eselt = b.const_tile("esel", esel_np)

        g = {
            "w13_user": {0: None, 1: None},
            "w2_user": {0: None, 1: None},
            "slot": {0: None, 1: None, 2: None},
            "cb_readers": [],
            "sil_guard": [],
            "h_guard": None,
            "acc_writer": {m: None for m in range(NCH)},
            "acc_store": {m: None for m in range(NCH)},
            "xrt": {0: None, 1: None},
        }

        def expert_pass(tb, kind, e_idx):
            if kind == "e":
                w13_src, w2_src = W["f_w13"][e_idx], W["f_w2"][e_idx]
                nm13, nk13, nm2, nk2, ni = 22, NCH, 16, 11, 11
                toff, tlen = tb * TBF, TBF
            else:
                w13_src, w2_src = W["f_sw13"], W["f_sw2"]
                nm13, nk13, nm2, nk2, ni = 44, NCH, 16, 22, 22
                toff, tlen = tb * TBF + e_idx * 512, 512
            nnb = tlen // 512
            acc_off = toff - tb * TBF

            cb_tok = None
            if kind == "e":
                def cbmm(e):
                    for nb in range(nnb):
                        last = e.matmul(cbps[:, nb * 512:nb * 512 + 512],
                                        eselt[:, e_idx * 128:(e_idx + 1) * 128],
                                        cT[:, toff + nb * 512:toff + nb * 512 + 512],
                                        start=True, stop=True)
                    return last
                cb_tok = p.do(TENS, cbmm, after=g["cb_readers"])
                g["cb_readers"] = []

            mm13, ev13 = {}, {}
            sil_consumers = []
            m_order = [x for i in range(ni) for x in (i, ni + i)]
            for mi, m in enumerate(m_order):
                par, slot = mi % 2, mi % 3
                wt = p.dma(POOL, pfx + "w13p%d" % par,
                           lambda e, m=m, par=par: e.dma_start(
                               out=w13b[:, par * 2048:par * 2048 + nk13 * 128],
                               in_=w13_src[m]),
                           after=[g["w13_user"][par]])
                def mmf(e, m=m, par=par, slot=slot, nnb=nnb, nk13=nk13, toff=toff):
                    for nb in range(nnb):
                        for kc in range(nk13):
                            last = e.matmul(
                                mps[:, slot * 1024 + nb * 512:slot * 1024 + nb * 512 + 512],
                                w13b[:, par * 2048 + kc * 128:par * 2048 + kc * 128 + 128],
                                xln[:, kc * T + toff + nb * 512:
                                     kc * T + toff + nb * 512 + 512],
                                start=(kc == 0), stop=(kc == nk13 - 1))
                    return last
                mm13[m] = p.do(TENS, mmf, after=[wt, g["slot"][slot]])
                g["w13_user"][par] = mm13[m]
                if m < ni:
                    deps = [mm13[m]] + ([ev13[ni + m - 2]] if m >= 2 else g["sil_guard"])
                    for q in range(tlen // 512):
                        ev13[m] = p.do(SCAL, lambda e, m=m, slot=slot, tlen=tlen, q=q:
                            e.activation(
                                out=sil[:, (m % 2) * tlen + q * 512:
                                         (m % 2) * tlen + q * 512 + 512],
                                in_=mps[:, slot * 1024 + q * 512:slot * 1024 + q * 512 + 512],
                                func=AF.Silu),
                            after=deps)
                else:
                    gm = m - ni
                    deps = [mm13[m], ev13[gm], g["h_guard"]]
                    t1 = None
                    for q in range(tlen // 512):
                        t1 = p.do(VEC, lambda e, gm=gm, slot=slot, tlen=tlen, q=q:
                            e.tensor_tensor(
                                out=hbuf[:, gm * tlen + q * 512:gm * tlen + q * 512 + 512],
                                in0=sil[:, (gm % 2) * tlen + q * 512:
                                        (gm % 2) * tlen + q * 512 + 512],
                                in1=mps[:, slot * 1024 + q * 512:slot * 1024 + q * 512 + 512],
                                op=ALU.mult),
                            after=deps + ([t1] if t1 else []))
                        if kind == "e":
                            t1 = p.do(VEC, lambda e, gm=gm, tlen=tlen, q=q: e.tensor_tensor(
                                out=hbuf[:, gm * tlen + q * 512:gm * tlen + q * 512 + 512],
                                in0=hbuf[:, gm * tlen + q * 512:gm * tlen + q * 512 + 512],
                                in1=cbps[:, q * 512:q * 512 + 512], op=ALU.mult),
                                after=[t1, cb_tok])
                            g["cb_readers"].append(t1)
                    ev13[m] = t1
                    sil_consumers.append(t1)
                g["slot"][slot] = ev13[m]

            h_all = [ev13[ni + i] for i in range(ni)]
            mm2, ev2 = {}, {}
            for m in range(nm2):
                par, slot = m % 2, m % 3
                wt = p.dma(POOL, pfx + "w2p%d" % par,
                           lambda e, m=m, par=par, nk2=nk2: e.dma_start(
                               out=w2b[:, par * SI:par * SI + nk2 * 128],
                               in_=w2_src[m]),
                           after=[g["w2_user"][par]])
                deps = [wt, g["slot"][slot]] + (h_all if m == 0 else [])
                def mmf(e, m=m, par=par, slot=slot, nnb=nnb, nk2=nk2, tlen=tlen):
                    for nb in range(nnb):
                        for kc in range(nk2):
                            last = e.matmul(
                                mps[:, slot * 1024 + nb * 512:slot * 1024 + nb * 512 + 512],
                                w2b[:, par * SI + kc * 128:par * SI + kc * 128 + 128],
                                hbuf[:, kc * tlen + nb * 512:kc * tlen + nb * 512 + 512],
                                start=(kc == 0), stop=(kc == nk2 - 1))
                    return last
                mm2[m] = p.do(TENS, mmf, after=deps)
                g["w2_user"][par] = mm2[m]
                first = (kind == "e" and e_idx == 0)
                evt = None
                for q in range(tlen // 512):
                    def evf(e, m=m, slot=slot, first=first, acc_off=acc_off, q=q):
                        dst_sl = acc[:, m * TBF + acc_off + q * 512:
                                     m * TBF + acc_off + q * 512 + 512]
                        src_sl = mps[:, slot * 1024 + q * 512:slot * 1024 + q * 512 + 512]
                        if first:
                            return e.tensor_copy(out=dst_sl, in_=src_sl)
                        return e.tensor_tensor(out=dst_sl, in0=dst_sl, in1=src_sl,
                                               op=ALU.add)
                    evt = p.do(VEC, evf,
                               after=[mm2[m], g["acc_writer"][m], g["acc_store"][m]] +
                                     ([evt] if evt else []))
                ev2[m] = evt
                g["acc_writer"][m] = ev2[m]
                g["acc_store"][m] = None
                g["slot"][slot] = ev2[m]
            g["sil_guard"] = sil_consumers
            g["h_guard"] = mm2[nm2 - 1]

        for tb in range(NBF):
            for e_idx in range(E):
                expert_pass(tb, "e", e_idx)
            for sb_i in range(TBF // 512):
                expert_pass(tb, "s", sb_i)
            for c in range(NCH):
                par = c % 2
                lt = p.dma(SYNC, pfx + "rx",
                           lambda e, c=c, tb=tb: e.dma_start(
                               out=xrt[:, :],
                               in_=xsrc[c * 128:(c + 1) * 128, tb * TBF:(tb + 1) * TBF]),
                           after=[g["xrt"][0]])
                adt = None
                for q in range(TBF // 512):
                    adt = p.do(VEC, lambda e, c=c, q=q: e.tensor_tensor(
                        out=acc[:, c * TBF + q * 512:c * TBF + q * 512 + 512],
                        in0=acc[:, c * TBF + q * 512:c * TBF + q * 512 + 512],
                        in1=xrt[:, q * 512:q * 512 + 512], op=ALU.add),
                        after=[lt, g["acc_writer"][c]] + ([adt] if adt else []))
                ad = adt
                g["xrt"][0] = ad
                st = p.dma(SYNC, pfx + "sx%d" % par,
                           lambda e, c=c, tb=tb: e.dma_start(
                               out=dst[c * 128:(c + 1) * 128, tb * TBF:(tb + 1) * TBF],
                               in_=acc[:, c * TBF:(c + 1) * TBF]), after=[ad])
                g["acc_writer"][c] = None
                g["acc_store"][c] = st



def prep_ffn_weights(I):
    f32 = np.float32
    w13 = np.asarray(I["L0_ffn_w13"], f32)
    w2 = np.asarray(I["L0_ffn_w2"], f32)
    return {
        "f_w13": np.stack([_relayout(np.ascontiguousarray(w13[e].T)) for e in range(E)]),
        "f_w2": np.stack([_relayout(np.ascontiguousarray(w2[e].T)) for e in range(E)]),
        "f_sw13": _relayout(np.ascontiguousarray(np.asarray(I["L0_ffn_sw13"], f32).T)),
        "f_sw2": _relayout(np.ascontiguousarray(np.asarray(I["L0_ffn_sw2"], f32).T)),
        "f_gate": np.ascontiguousarray(np.asarray(I["L0_ffn_gate_w"], f32).T
                                       .reshape(NCH, 128, 16).transpose(1, 0, 2)
                                       .reshape(128, NCH * 16)),
    }


def declare_ffn_weights(b, wnp):
    return {k: b.dram_in(k, v.shape, F32) for k, v in wnp.items()}


def build_ffn_only():
    b = Build()
    xin = b.dram_in("x_in", [D, T], F32)
    out = b.dram_out("x_out", [D, T], F32)
    wshapes = {
        "f_w13": (E, 22, 128, 2048), "f_w2": (E, 16, 128, FI),
        "f_sw13": (44, 128, 2048), "f_sw2": (16, 128, SI), "f_gate": (128, NCH * 16),
    }
    W = {k: b.dram_in(k, list(s), F32) for k, s in wshapes.items()}
    emit_ffn(b, xin, out, W, "f1")
    with b.nc.Block() as blk:
        b.p.emit(blk)
        @blk.sync
        def _(eng):
            for name, rec in b.p.dsems.items():
                eng.wait_ge(rec[0], rec[1])
    return b


def run_ffn_only(x_fm, wnp, cores=1):
    b = build_ffn_only()
    inmap = {"x_in": x_fm, **{k: wnp[k] for k in
                              ["f_w13", "f_w2", "f_sw13", "f_sw2", "f_gate"]},
             **b.consts}
    nc_obj = b.nc
    res = run_bass_kernel_spmd(nc_obj, [inmap] * cores, core_ids=list(range(cores)))
    b.ctx.close()
    return res.results[0]["x_out"]



WSHAPES = {
    "f_w13": (E, 22, 128, 2048), "f_w2": (E, 16, 128, FI),
    "f_sw13": (44, 128, 2048), "f_sw2": (16, 128, SI), "f_gate": (128, NCH * 16),
    "b_wqa": (4, 128, 2048), "b_wkva": (5, 128, 2048),
    "b_wqb": (16, 128, 768), "b_wkvb": (16, 128, 1024),
    "b_wo": (16, 128, 2048), "b_bqa": (128, 4), "b_bkva": (128, 5),
    "b_bqbn": (128, 16), "b_bqbp": (64, 16), "b_bkbn": (128, 16),
    "b_bkbv": (128, 16), "b_bwo": (128, 16),
    "a_wr": (16, 128, 2048), "a_wk": (16, 128, 2048), "a_wv": (16, 128, 2048),
    "a_wo": (16, 128, 2048), "a_w1": (128, 1024), "a_a1": (128, 1024),
    "a_g1": (128, 2048), "a_w2": (64, 2048), "a_a2": (64, 2048),
    "a_g2": (128, 2048), "a_vecs": (NCH, 128, 13),
}


def build_full():
    b = Build()
    xin = b.dram_in("x_in", [D, T], F32)
    out = b.dram_out("x_out", [D, T], F32)
    x1 = b.dram_tmp("x1", [D, T], F32)
    x2 = b.dram_tmp("x2", [D, T], F32)
    x3 = b.dram_tmp("x3", [D, T], F32)
    W = {k: b.dram_in(k, list(sh), F32) for k, sh in WSHAPES.items()}
    emit_attn_a(b, xin, x1, W, "r1")
    emit_ffn(b, x1, x2, W, "f1")
    emit_attn_b(b, x2, x3, W, "a1")
    emit_ffn(b, x3, out, W, "f2")
    with b.nc.Block() as blk:
        b.p.emit(blk)
        @blk.sync
        def _(eng):
            for name, rec in b.p.dsems.items():
                eng.wait_ge(rec[0], rec[1])
    return b


def prep_all(I):
    w = {}
    w.update(prep_ffn_weights(I))
    w.update(prep_attnb_weights(I))
    w.update(prep_attna_weights(I))
    return w


def kernel(**inputs):
    x = np.asarray(inputs["L0_attn_a_x"], np.float32)
    wnp = prep_all(inputs)
    b = build_full()
    base = {**wnp, **b.consts}
    in_maps = []
    for c in range(NCORES):
        m = dict(base)
        m["x_in"] = np.ascontiguousarray(x[c].T)
        in_maps.append(m)
    res = run_bass_kernel_spmd(b.nc, in_maps, core_ids=list(range(NCORES)))
    out = np.stack([np.ascontiguousarray(res.results[c]["x_out"].T)
                    for c in range(NCORES)])
    return out.astype(np.float32)



def prep_attnb_weights(I):
    f32 = np.float32
    wq_a = np.asarray(I["L0_attn_b_wq_a_w"], f32)
    wq_b = np.asarray(I["L0_attn_b_wq_b_w"], f32)
    wkv_a = _pad(np.asarray(I["L0_attn_b_wkv_a_w"], f32), m=640)
    wkv_b = np.asarray(I["L0_attn_b_wkv_b_w"], f32)
    wo = np.asarray(I["L0_attn_b_wo_w"], f32)
    out = {
        "b_wqa": _relayout(wq_a),
        "b_wkva": _relayout(wkv_a),
        "b_wqb": np.ascontiguousarray(
            wq_b.reshape(4, 128, 16, 192).transpose(2, 1, 0, 3).reshape(16, 128, 768)),
        "b_wkvb": np.ascontiguousarray(
            wkv_b.reshape(4, 128, 16, 256).transpose(2, 1, 0, 3).reshape(16, 128, 1024)),
        "b_wo": _relayout(wo),
        "b_bqa": np.ascontiguousarray(
            np.asarray(I["L0_attn_b_wq_a_b"], f32).reshape(4, 128).T),
        "b_bkva": np.ascontiguousarray(
            _pad(np.asarray(I["L0_attn_b_wkv_a_b"], f32)[None], m=640)[0]
            .reshape(5, 128).T),
        "b_bqbn": np.ascontiguousarray(
            np.asarray(I["L0_attn_b_wq_b_b"], f32).reshape(16, 192)[:, :128].T),
        "b_bqbp": np.ascontiguousarray(
            np.asarray(I["L0_attn_b_wq_b_b"], f32).reshape(16, 192)[:, 128:].T),
        "b_bkbn": np.ascontiguousarray(
            np.asarray(I["L0_attn_b_wkv_b_b"], f32).reshape(16, 256)[:, :128].T),
        "b_bkbv": np.ascontiguousarray(
            np.asarray(I["L0_attn_b_wkv_b_b"], f32).reshape(16, 256)[:, 128:].T),
        "b_bwo": np.ascontiguousarray(
            np.asarray(I["L0_attn_b_wo_b"], f32).reshape(16, 128).T),
    }
    return out


def rope_tables_np(Tn):
    inv = 1.0 / 1e6 ** (np.arange(0, ROPE, 2, dtype=np.float64) / ROPE)
    t = np.arange(Tn, dtype=np.float64)
    fr = np.outer(t, inv)
    cos, sin = np.cos(fr), np.sin(fr)
    Cf = np.empty((ROPE, Tn)); Sf = np.empty((ROPE, Tn))
    Cf[0::2] = cos.T; Cf[1::2] = cos.T
    Sf[0::2] = -sin.T; Sf[1::2] = sin.T
    return Cf.astype(np.float32), Sf.astype(np.float32)


def emit_attn_b(b, xsrc, dst, W, pfx):
    p = b.p
    bf = bf16c(b)
    if "xln" not in b.com:
        b.com["xln"] = b.sbp("g_xln", [128, NCH * T], BF16)
    xln = b.com["xln"]
    ln_done = emit_ln(b, xsrc, xln, pfx + "l")
    ones_b = b.com["ct_ones128_b"]
    ones_f = b.com["ct_ones128_f"]

    with b.phase(pfx + "m"):
        Cf_np, Sf_np = rope_tables_np(T)
        swp = np.zeros((64, 64), bf)
        for i in range(32):
            swp[2 * i + 1, 2 * i] = 1.0
            swp[2 * i, 2 * i + 1] = 1.0
        Cf = b.sb(pfx + "Cf", [ROPE, T], BF16)
        Sf = b.sb(pfx + "Sf", [ROPE, T], BF16)
        swap64 = b.sb(pfx + "swp", [64, 64], BF16)
        c1 = p.dma(SYNC, pfx + "c", lambda e: e.dma_start(
            out=Cf[:], in_=b.const("ropeC", Cf_np.astype(bf))))
        c2 = p.dma(SYNC, pfx + "c", lambda e: e.dma_start(
            out=Sf[:], in_=b.const("ropeS", Sf_np.astype(bf))))
        c3 = p.dma(SYNC, pfx + "c", lambda e: e.dma_start(out=swap64[:], in_=b.const("swap64", swp)))
        bias_t = {}
        for nm, shp in (("b_bqa", [128, 4]), ("b_bkva", [128, 5]), ("b_bqbn", [128, 16]),
                        ("b_bqbp", [64, 16]), ("b_bkbn", [128, 16]), ("b_bkbv", [128, 16])):
            tl = b.sb(pfx + nm, shp, F32)
            bias_t[nm] = tl
            c3 = p.dma(SYNC, pfx + "c", lambda e, tl=tl, nm=nm: e.dma_start(out=tl[:], in_=W[nm]))
        cst = [c1, c2, c3]

        qa = b.sb(pfx + "qa", [128, 4 * T], BF16)
        ckv = b.sb(pfx + "ckv", [128, 4 * T], BF16)
        kpe = b.sb(pfx + "kpe", [65, T], BF16)
        rows3 = b.sb(pfx + "rows3", [33, 2 * T], BF16)
        kpe_sq = b.sb(pfx + "kpesq", [64, T], BF16)
        qh = [b.sb(pfx + "qh%d" % i, [128, T], BF16) for i in range(2)]
        qpe = [b.sb(pfx + "qpe%d" % i, [65, T], BF16) for i in range(2)]
        kn = [b.sb(pfx + "kn%d" % i, [128, T], BF16) for i in range(2)]
        Vtm = [b.sb(pfx + "V%d" % i, [128, 16 * 128], BF16) for i in range(2)]
        def pr_slice64(i):
            return pp[0:64, i * 512:(i + 1) * 512]
        sq = b.sb(pfx + "sq", [128, T], BF16)
        PT = b.sb(pfx + "PT", [128, 8 * 512], BF16)
        rown = rows3[32:33, 0:T]
        rowq = rows3[32:33, T:2 * T]
        b2t = b.sb(pfx + "b2", [33, 2], F32)
        one33 = b.const_tile("one33", np.ones((33, 1), bf16c(b)))
        dsb = b.sb(pfx + "dsb", [1, 512], F32)
        rbsb = b.sb(pfx + "rb", [128, 512], F32)
        wslab = b.sb(pfx + "ws", [128, 2 * 2048], BF16)
        hq = b.sb(pfx + "hq", [128, 2 * 768], BF16)
        hk = b.sb(pfx + "hk", [128, 2 * 1024], BF16)
        pp = b.ps(pfx + "pp", [128, 3072], F32)
        pt = b.ps(pfx + "pt", [128, 2048], BF16)

        def bank(i):
            return pp[:, i * 512:(i + 1) * 512]

        wuser = {0: None, 1: None}
        bank_guard = {i: None for i in range(7)}
        NB = T // 512

        def proj(dst_sb, src_slabs, nm_tiles, bias_tile, name):
            toks = []
            for m in range(nm_tiles):
                par = m % 2
                wt = p.dma(POOL, pfx + name + "%d" % par,
                           lambda e, m=m, par=par: e.dma_start(
                               out=wslab[:, par * 2048:(par + 1) * 2048],
                               in_=src_slabs[m]), after=[wuser[par]])
                def mmf(e, m=m, par=par):
                    for nb in range(NB):
                        for kc in range(NCH):
                            last = e.matmul(bank(nb % 4),
                                            wslab[:, par * 2048 + kc * 128:
                                                  par * 2048 + kc * 128 + 128],
                                            xln[:, kc * T + nb * 512:kc * T + nb * 512 + 512],
                                            start=(kc == 0), stop=(kc == NCH - 1))
                            if kc == NCH - 1 and nb < NB - 1:
                                pass
                    return last
                for nb in range(NB):
                    def mmnb(e, m=m, par=par, nb=nb):
                        for kc in range(NCH):
                            last = e.matmul(bank(nb % 4),
                                            wslab[:, par * 2048 + kc * 128:
                                                  par * 2048 + kc * 128 + 128],
                                            xln[:, kc * T + nb * 512:kc * T + nb * 512 + 512],
                                            start=(kc == 0), stop=(kc == NCH - 1))
                        return last
                    mt = p.do(TENS, mmnb, after=[wt, bank_guard[nb % 4]] + (ln_done if m == 0 and nb == 0 else []))
                    wuser[par] = mt
                    ev = p.do(SCAL, lambda e, m=m, nb=nb, dst_sb=dst_sb, bias_tile=bias_tile:
                              e.activation(out=dst_sb[:, m * T + nb * 512:m * T + nb * 512 + 512],
                                           in_=bank(nb % 4), func=AF.Identity,
                                           bias=bias_tile[:, m:m + 1]),
                              after=[mt] + cst)
                    bank_guard[nb % 4] = ev
                    toks.append(ev)
            return toks

        qa_done = proj(qa, W["b_wqa"], 4, bias_t["b_bqa"], "wqa")
        ckv_done = proj(ckv, W["b_wkva"], 4, bias_t["b_bkva"], "wkva")
        kpe_raw_done = []
        for nb in range(NB):
            par5 = 4 % 2
            wt5 = p.dma(POOL, pfx + "wkva%d" % par5,
                        lambda e: e.dma_start(out=wslab[:, par5 * 2048:(par5 + 1) * 2048],
                                              in_=W["b_wkva"][4]),
                        after=[wuser[par5]]) if nb == 0 else wt5
            bk5 = nb % 4
            mt5 = p.do(TENS, lambda e, nb=nb, bk5=bk5: [
                e.matmul(bank(bk5),
                         wslab[:, par5 * 2048 + kc * 128:par5 * 2048 + kc * 128 + 128],
                         xln[:, kc * T + nb * 512:kc * T + nb * 512 + 512],
                         start=(kc == 0), stop=(kc == NCH - 1))
                for kc in range(NCH)][-1], after=[wt5, bank_guard[bk5]])
            if nb == 0:
                wuser[par5] = mt5
            ev5 = p.do(SCAL, lambda e, nb=nb, bk5=bk5: e.activation(
                out=kpe[0:64, nb * 512:nb * 512 + 512],
                in_=pp[0:64, bk5 * 512:bk5 * 512 + 512], func=AF.Identity,
                bias=bias_t["b_bkva"][0:64, 4:5]), after=[mt5] + cst)
            bank_guard[bk5] = ev5
            kpe_raw_done.append(ev5)

        kp_raw = ckv
        rope_toks = []
        ms = p.do(VEC, lambda e: e.memset(kpe[64:65, :], -1.0), after=cst)
        rope_toks.append(ms)
        for nb in range(NB):
            sw = p.do(TENS, lambda e, nb=nb: e.matmul(
                pp[0:64, 2048:2048 + 512], swap64[:],
                kpe[0:64, nb * 512:nb * 512 + 512], start=True, stop=True),
                after=[kpe_raw_done[nb]] + cst + [rope_toks[-1], bank_guard[4]])
            v1 = p.do(VEC, lambda e, nb=nb: e.tensor_tensor(
                out=sq[0:64, nb * 512:nb * 512 + 512],
                in0=pp[0:64, 2048:2048 + 512],
                in1=Sf[:, nb * 512:nb * 512 + 512], op=ALU.mult), after=[sw] + cst)
            bank_guard[4] = v1
            v2 = p.do(VEC, lambda e, nb=nb: e.tensor_tensor(
                out=kpe[0:64, nb * 512:nb * 512 + 512],
                in0=kpe[0:64, nb * 512:nb * 512 + 512],
                in1=Cf[:, nb * 512:nb * 512 + 512], op=ALU.mult), after=[sw])
            v3 = p.do(VEC, lambda e, nb=nb: e.tensor_tensor(
                out=kpe[0:64, nb * 512:nb * 512 + 512],
                in0=kpe[0:64, nb * 512:nb * 512 + 512],
                in1=sq[0:64, nb * 512:nb * 512 + 512], op=ALU.add), after=[v1, v2])
            rope_toks.append(v3)

        kq_toks = []
        for nb in range(NB):
            kq_toks.append(p.do(SCAL, lambda e, nb=nb: e.activation(
                out=kpe_sq[0:64, nb * 512:nb * 512 + 512],
                in_=kpe[0:64, nb * 512:nb * 512 + 512], func=AF.Square),
                after=rope_toks))

        identb = b.const_tile("identb", np.eye(128, dtype=bf))
        tok_proj_last = Tok(TENS, p.ecnt[TENS])
        ptg = {0: None, 1: None}
        hquser = {0: None, 1: None}
        hkuser = {0: None, 1: None}

        abk = [0]
        def a_bank():
            i = 4 + abk[0] % 2
            abk[0] += 1
            return i
        bank_guard47 = bank_guard

        st_A = {}
        bdone = {0: None, 1: None}
        bdone_s = {0: None, 1: None}
        bdone_v = {0: None, 1: None}
        a_prev = {"sq": None, "rows": None, "vT": None}

        def stageA(h):
            par = h % 2
            qh_, qpe_, kn_, Vtm_ = qh[par], qpe[par], kn[par], Vtm[par]
            wq_t = p.dma(POOL, pfx + "hq%d" % par,
                         lambda e, h=h, par=par: e.dma_start(
                             out=hq[:, par * 768:(par + 1) * 768], in_=W["b_wqb"][h]),
                         after=[hquser[par]])
            wk_t = p.dma(POOL, pfx + "hk%d" % par,
                         lambda e, h=h, par=par: e.dma_start(
                             out=hk[:, par * 1024:(par + 1) * 1024], in_=W["b_wkvb"][h]),
                         after=[hkuser[par]])
            qn_ev, qp_ev, kn_ev, v_ev = [], [], [], []
            for nb in range(NB):
                bk = a_bank()
                mt = p.do(TENS, lambda e, par=par, nb=nb, bk=bk: [
                    e.matmul(bank(bk),
                             hq[:, par * 768 + kc * 192:par * 768 + kc * 192 + 128],
                             qa[:, kc * T + nb * 512:kc * T + nb * 512 + 512],
                             start=(kc == 0), stop=(kc == 3))
                    for kc in range(4)][-1], after=[wq_t, bank_guard47[bk]])
                ev = p.do(SCAL, lambda e, h=h, nb=nb, bk=bk, qh_=qh_: e.activation(
                    out=qh_[:, nb * 512:nb * 512 + 512], in_=bank(bk), func=AF.Identity,
                    bias=bias_t["b_bqbn"][:, h:h + 1]),
                    after=[mt, bdone[par], bdone_s[par]])
                bank_guard47[bk] = ev
                qn_ev.append(ev)
                bk2 = a_bank()
                mt2 = p.do(TENS, lambda e, par=par, nb=nb, bk2=bk2: [
                    e.matmul(pr_slice64(bk2),
                             hq[:, par * 768 + kc * 192 + 128:par * 768 + kc * 192 + 192],
                             qa[:, kc * T + nb * 512:kc * T + nb * 512 + 512],
                             start=(kc == 0), stop=(kc == 3))
                    for kc in range(4)][-1], after=[wq_t, bank_guard47[bk2]])
                ev2 = p.do(SCAL, lambda e, h=h, nb=nb, bk2=bk2, qpe_=qpe_: e.activation(
                    out=qpe_[0:64, nb * 512:nb * 512 + 512],
                    in_=pr_slice64(bk2), func=AF.Identity,
                    bias=bias_t["b_bqbp"][:, h:h + 1]),
                    after=[mt2, bdone[par], bdone_s[par], bdone_v[par]])
                bank_guard47[bk2] = ev2
                qp_ev.append(ev2)
            qn2_ev = []
            for nb in range(NB):
                s1 = p.do(SCAL, lambda e, nb=nb, qh_=qh_: e.activation(
                    out=sq[:, nb * 512:nb * 512 + 512], in_=qh_[:, nb * 512:nb * 512 + 512],
                    func=AF.Square), after=[qn_ev[nb], a_prev["sq"]])
                bk = a_bank()
                mta = p.do(TENS, lambda e, nb=nb, bk=bk: e.matmul(
                    pp[32:33, bk * 512:bk * 512 + 512], ones_b[:],
                    sq[:, nb * 512:nb * 512 + 512], start=True, stop=False),
                    after=[s1, bank_guard47[bk]])
                s2 = p.do(SCAL, lambda e, nb=nb, qpe_=qpe_: e.activation(
                    out=sq[0:64, nb * 512:nb * 512 + 512],
                    in_=qpe_[0:64, nb * 512:nb * 512 + 512],
                    func=AF.Square), after=[qp_ev[nb], mta])
                mt = p.do(TENS, lambda e, nb=nb, bk=bk: e.matmul(
                    pp[32:33, bk * 512:bk * 512 + 512], ones_b[0:64, :],
                    sq[0:64, nb * 512:nb * 512 + 512], start=False, stop=True),
                    after=[s2])
                ev = p.do(VEC, lambda e, nb=nb, bk=bk: e.tensor_copy(
                    out=rowq[0:1, nb * 512:nb * 512 + 512],
                    in_=pp[32:33, bk * 512:bk * 512 + 512]),
                    after=[mt, a_prev["rows"]])
                bank_guard47[bk] = ev
                qn2_ev.append(ev)
            rp_toks = []
            for nb in range(NB):
                bk = a_bank()
                sw = p.do(TENS, lambda e, nb=nb, bk=bk, qpe_=qpe_: e.matmul(
                    pr_slice64(bk), swap64[:],
                    qpe_[0:64, nb * 512:nb * 512 + 512], start=True, stop=True),
                    after=[qp_ev[nb], qn2_ev[nb], bank_guard47[bk]])
                v1 = p.do(VEC, lambda e, nb=nb, bk=bk: e.tensor_tensor(
                    out=sq[0:64, nb * 512:nb * 512 + 512],
                    in0=pr_slice64(bk),
                    in1=Sf[:, nb * 512:nb * 512 + 512], op=ALU.mult),
                    after=[sw, qn2_ev[nb]])
                v2 = p.do(VEC, lambda e, nb=nb, qpe_=qpe_: e.tensor_tensor(
                    out=qpe_[0:64, nb * 512:nb * 512 + 512],
                    in0=qpe_[0:64, nb * 512:nb * 512 + 512],
                    in1=Cf[:, nb * 512:nb * 512 + 512], op=ALU.mult),
                    after=[sw, qn2_ev[nb]])
                v3 = p.do(VEC, lambda e, nb=nb, qpe_=qpe_: e.tensor_tensor(
                    out=qpe_[0:64, nb * 512:nb * 512 + 512],
                    in0=qpe_[0:64, nb * 512:nb * 512 + 512],
                    in1=sq[0:64, nb * 512:nb * 512 + 512], op=ALU.add), after=[v1, v2])
                bank_guard47[bk] = v1
                rp_toks.append(v3)
            for nb in range(NB):
                bk = a_bank()
                mt = p.do(TENS, lambda e, par=par, nb=nb, bk=bk: [
                    e.matmul(bank(bk),
                             hk[:, par * 1024 + kc * 256:par * 1024 + kc * 256 + 128],
                             ckv[:, kc * T + nb * 512:kc * T + nb * 512 + 512],
                             start=(kc == 0), stop=(kc == 3))
                    for kc in range(4)][-1], after=[wk_t, bank_guard47[bk]])
                ev = p.do(SCAL, lambda e, h=h, nb=nb, bk=bk, kn_=kn_: e.activation(
                    out=kn_[:, nb * 512:nb * 512 + 512], in_=bank(bk), func=AF.Identity,
                    bias=bias_t["b_bkbn"][:, h:h + 1]),
                    after=[mt, bdone[par], bdone_s[par]])
                bank_guard47[bk] = ev
                kn_ev.append(ev)
                bk2 = a_bank()
                mt2 = p.do(TENS, lambda e, par=par, nb=nb, bk2=bk2: [
                    e.matmul(bank(bk2),
                             hk[:, par * 1024 + kc * 256 + 128:par * 1024 + kc * 256 + 256],
                             ckv[:, kc * T + nb * 512:kc * T + nb * 512 + 512],
                             start=(kc == 0), stop=(kc == 3))
                    for kc in range(4)][-1], after=[wk_t, bank_guard47[bk2]])
                ev2 = p.do(SCAL, lambda e, h=h, nb=nb, bk2=bk2: e.activation(
                    out=sq[:, nb * 512:nb * 512 + 512],
                    in_=bank(bk2), func=AF.Identity,
                    bias=bias_t["b_bkbv"][:, h:h + 1]),
                    after=[mt2, a_prev["vT"], rp_toks[nb], qn2_ev[nb]])
                bank_guard47[bk2] = ev2
                v_ev.append(ev2)
            vt_toks = []
            for tt in range(NTT):
                tr = p.do(TENS, lambda e, tt=tt: e.transpose(
                    pt[0:128, (tt % 2) * 1024:(tt % 2) * 1024 + 128],
                    sq[:, tt * 128:(tt + 1) * 128], identb[:]),
                    after=[v_ev[(tt * 128) // 512], ptg[tt % 2]])
                cp = p.do(VEC, lambda e, tt=tt, Vtm_=Vtm_: e.tensor_copy(
                    out=Vtm_[:, tt * 128:(tt + 1) * 128],
                    in_=pt[0:128, (tt % 2) * 1024:(tt % 2) * 1024 + 128]),
                    after=[tr, bdone[par] if tt < 2 else None])
                ptg[tt % 2] = cp
                vt_toks.append(cp)
            kn2_ev = []
            for nb in range(NB):
                s1 = p.do(SCAL, lambda e, nb=nb, kn_=kn_: e.activation(
                    out=sq[:, nb * 512:nb * 512 + 512], in_=kn_[:, nb * 512:nb * 512 + 512],
                    func=AF.Square), after=[kn_ev[nb], qn2_ev[nb]] + vt_toks)
                bk = a_bank()
                mt = p.do(TENS, lambda e, nb=nb, bk=bk: [
                    e.matmul(pp[32:33, bk * 512:bk * 512 + 512], ones_b[:],
                             sq[:, nb * 512:nb * 512 + 512], start=True, stop=False),
                    e.matmul(pp[32:33, bk * 512:bk * 512 + 512], ones_b[0:64, :],
                             kpe_sq[0:64, nb * 512:nb * 512 + 512],
                             start=False, stop=True)][-1],
                    after=[s1, bank_guard47[bk]] + (kq_toks if h == 0 and nb == 0 else []))
                ev = p.do(VEC, lambda e, nb=nb, bk=bk: e.tensor_copy(
                    out=rown[0:1, nb * 512:nb * 512 + 512],
                    in_=pp[32:33, bk * 512:bk * 512 + 512]),
                    after=[mt, a_prev["rows"]])
                bank_guard47[bk] = ev
                kn2_ev.append(ev)
            a2 = p.do(VEC, lambda e: e.reduce_max(out=b2t[32:33, 0:1], in_=rown[0:1, :],
                                                  axis=AX.X), after=kn2_ev)
            a3 = p.do(VEC, lambda e: e.tensor_scalar(out=rowq[0:1, :], in0=rowq[0:1, :],
                                                     scalar1=b2t[32:33, 0:1], scalar2=None,
                                                     op0=ALU.mult), after=[a2] + qn2_ev)
            a4l = None
            for nb in range(NB):
                bk = a_bank()
                m4 = p.do(TENS, lambda e, nb=nb, bk=bk: e.matmul(
                    pp[64:65, bk * 512:bk * 512 + 512], one33[32:33, 0:1],
                    rowq[0:1, nb * 512:nb * 512 + 512], start=True, stop=True),
                    after=[a3, bank_guard47[bk]])
                a4l = p.do(SCAL, lambda e, nb=nb, bk=bk, qpe_=qpe_: e.activation(
                    out=qpe_[64:65, nb * 512:nb * 512 + 512],
                    in_=pp[64:65, bk * 512:bk * 512 + 512], func=AF.Sqrt),
                    after=[m4, bdone[par], bdone_v[par]])
                bank_guard47[bk] = a4l
            a_prev["sq"] = kn2_ev[-1]
            a_prev["rows"] = a4l
            a_prev["vT"] = vt_toks[-1]
            hquser[par] = mt2
            hkuser[par] = mt2
            st_A[par] = {"aug": a4l, "rope": rp_toks, "kn_ev": kn_ev,
                         "vt": vt_toks, "qh": qh_, "qpe": qpe_, "kn": kn_,
                         "Vtm": Vtm_}

        def stageB(h):
            par = h % 2
            A = st_A[par]
            qh_, qpe_, kn_, Vtm_ = A["qh"], A["qpe"], A["kn"], A["Vtm"]
            st_guard = {0: bank_guard[0], 1: bank_guard[1]}
            last_pv = None
            for qb in range(NB):
                exp_toks = []
                den_last = None
                pv_last = None
                for kt in range(NTT):
                    stt = p.do(TENS, lambda e, kt=kt, qb=qb, kn_=kn_, qh_=qh_,
                               qpe_=qpe_: [
                        e.matmul(bank(kt % 2), kn_[:, kt * 128:kt * 128 + 128],
                                 qh_[:, qb * 512:qb * 512 + 512], start=True, stop=False),
                        e.matmul(bank(kt % 2), kpe[0:65, kt * 128:kt * 128 + 128],
                                 qpe_[0:65, qb * 512:qb * 512 + 512], start=False,
                                 stop=True)][-1],
                        after=[st_guard[kt % 2], A["aug"], A["rope"][qb]] +
                              (A["kn_ev"] + A["vt"] if kt == 0 and qb == 0 else []))
                    ex = p.do(SCAL, lambda e, kt=kt: e.activation(
                        out=PT[:, (kt % 8) * 512:(kt % 8) * 512 + 512], in_=bank(kt % 2),
                        func=AF.Exp, scale=float(1.0 / np.sqrt(128.0))),
                        after=[stt] + ([exp_toks[kt - 8][1]] if kt >= 8 else []))
                    st_guard[kt % 2] = ex
                    den_last = p.do(TENS, lambda e, kt=kt: e.matmul(
                        pp[0:1, 1024:1024 + 512], ones_b[:],
                        PT[:, (kt % 8) * 512:(kt % 8) * 512 + 512],
                        start=(kt == 0), stop=(kt == NTT - 1)),
                        after=[ex, bank_guard[2] if kt == 0 else None])
                    pv_last = p.do(TENS, lambda e, kt=kt, Vtm_=Vtm_: e.matmul(
                        pp[:, 1536:1536 + 512], Vtm_[:, kt * 128:kt * 128 + 128],
                        PT[:, (kt % 8) * 512:(kt % 8) * 512 + 512],
                        start=(kt == 0), stop=(kt == NTT - 1)),
                        after=[ex, bank_guard[3] if kt == 0 else None])
                    exp_toks.append((ex, pv_last))
                d1 = p.do(VEC, lambda e: e.reciprocal(out=dsb[0:1, :],
                                                      in_=pp[0:1, 1024:1024 + 512]),
                          after=[den_last])
                bank_guard[2] = d1
                r1_ = p.do(TENS, lambda e: e.matmul(pp[:, 0:512], ones_f[:],
                                                    dsb[0:1, :], start=True, stop=True),
                           after=[d1, st_guard[0]])
                r2_ = p.do(VEC, lambda e: e.tensor_copy(out=rbsb[:, :],
                                                        in_=pp[:, 0:512]),
                           after=[r1_])
                st_guard[0] = r2_
                ao = p.do(VEC, lambda e, h=h, qb=qb: e.tensor_tensor(
                    out=xln[:, h * T + qb * 512:h * T + qb * 512 + 512],
                    in0=pp[:, 1536:1536 + 512], in1=rbsb[:, :], op=ALU.mult),
                    after=[pv_last, r2_, tok_proj_last])
                bank_guard[3] = ao
                last_pv = pv_last
                bdone_v[par] = ao
            bank_guard[0], bank_guard[1] = st_guard[0], st_guard[1]
            bdone[par] = last_pv
            bdone_s[par] = Tok(SCAL, p.ecnt[SCAL])

        stageA(0)
        for h in range(HB):
            if h + 1 < HB:
                stageA(h + 1)
            stageB(h)

    with b.phase(pfx + "w"):
        wslab_w = b.sb(pfx + "ws2", [128, 2 * 2048], BF16)
        xo_w = b.sb(pfx + "xo", [128, T], F32)
        xrt_w = b.sb(pfx + "xr", [128, T], F32)
        bwo = b.sb(pfx + "bwo", [128, 16], F32)
        pp_w = b.ps(pfx + "pp2", [128, 3072], F32)
        bankg_w = {i: None for i in range(6)}
        bwo_t = p.dma(SYNC, pfx + "c2", lambda e: e.dma_start(out=bwo[:], in_=W["b_bwo"]))
        wuser2 = {0: None, 1: None}
        xrt_g = None
        xo_g = None
        for m in range(NCH):
            par = m % 2
            wt = p.dma(POOL, pfx + "wo%d" % par,
                       lambda e, m=m, par=par: e.dma_start(
                           out=wslab_w[:, par * 2048:(par + 1) * 2048], in_=W["b_wo"][m]),
                       after=[wuser2[par]])
            lt = p.dma(SYNC, pfx + "rx",
                       lambda e, m=m: e.dma_start(
                           out=xrt_w[:, :], in_=xsrc[m * 128:(m + 1) * 128, :]),
                       after=[xrt_g])
            evs = []
            for nb in range(T // 512):
                bk = nb % 2
                mt = p.do(TENS, lambda e, par=par, nb=nb: [
                    e.matmul(pp_w[:, (nb % 2) * 512:(nb % 2) * 512 + 512],
                             wslab_w[:, par * 2048 + kc * 128:par * 2048 + kc * 128 + 128],
                             xln[:, kc * T + nb * 512:kc * T + nb * 512 + 512],
                             start=(kc == 0), stop=(kc == NCH - 1))
                    for kc in range(NCH)][-1],
                    after=[wt, bankg_w[bk]])
                ev = p.do(SCAL, lambda e, m=m, nb=nb: e.activation(
                    out=xo_w[:, nb * 512:nb * 512 + 512],
                    in_=pp_w[:, (nb % 2) * 512:(nb % 2) * 512 + 512], func=AF.Identity,
                    bias=bwo[:, m:m + 1]), after=[mt, xo_g, bwo_t])
                bankg_w[bk] = ev
                wuser2[par] = mt
                ad = p.do(VEC, lambda e, nb=nb: e.tensor_tensor(
                    out=xo_w[:, nb * 512:nb * 512 + 512],
                    in0=xo_w[:, nb * 512:nb * 512 + 512],
                    in1=xrt_w[:, nb * 512:nb * 512 + 512], op=ALU.add), after=[ev, lt])
                evs.append(ad)
            st = p.dma(SYNC, pfx + "sx",
                       lambda e, m=m: e.dma_start(
                           out=dst[m * 128:(m + 1) * 128, :], in_=xo_w[:]), after=evs)
            xrt_g = st
            xo_g = st


def build_attnb_only():
    b = Build()
    xin = b.dram_in("x_in", [D, T], F32)
    out = b.dram_out("x_out", [D, T], F32)
    wshapes = {"b_wqa": (4, 128, 2048), "b_wkva": (5, 128, 2048),
               "b_wqb": (16, 128, 768), "b_wkvb": (16, 128, 1024),
               "b_wo": (16, 128, 2048), "b_bqa": (128, 4), "b_bkva": (128, 5),
               "b_bqbn": (128, 16), "b_bqbp": (64, 16), "b_bkbn": (128, 16),
               "b_bkbv": (128, 16), "b_bwo": (128, 16)}
    W = {k: b.dram_in(k, list(s), F32) for k, s in wshapes.items()}
    emit_attn_b(b, xin, out, W, "a1")
    with b.nc.Block() as blk:
        b.p.emit(blk)
        @blk.sync
        def _(eng):
            for name, rec in b.p.dsems.items():
                eng.wait_ge(rec[0], rec[1])
    return b



def prep_attna_weights(I):
    f32 = np.float32
    out = {
        "a_wr": _relayout(np.asarray(I["L0_attn_a_W_r"], f32)),
        "a_wk": _relayout(np.asarray(I["L0_attn_a_W_k"], f32)),
        "a_wv": _relayout(np.asarray(I["L0_attn_a_W_v"], f32)),
        "a_wo": _relayout(np.asarray(I["L0_attn_a_W_o"], f32)),
        "a_w1": np.ascontiguousarray(np.asarray(I["L0_attn_a_w1"], f32)
                                     .reshape(16, 128, 64).transpose(1, 0, 2).reshape(128, 1024)),
        "a_a1": np.ascontiguousarray(np.asarray(I["L0_attn_a_a1"], f32)
                                     .reshape(16, 128, 64).transpose(1, 0, 2).reshape(128, 1024)),
        "a_g1": np.ascontiguousarray(np.asarray(I["L0_attn_a_g1"], f32)
                                     .reshape(16, 128, 128).transpose(1, 0, 2).reshape(128, 2048)),
        "a_w2": np.ascontiguousarray(np.asarray(I["L0_attn_a_w2"], f32)),
        "a_a2": np.ascontiguousarray(np.asarray(I["L0_attn_a_a2"], f32)),
        "a_g2": np.ascontiguousarray(np.asarray(I["L0_attn_a_g2"], f32)),
    }
    names = ["x_r", "x_w", "x_k", "x_v", "x_a", "x_g", "w0", "a0", "k_k", "k_a"]
    vs = [np.asarray(I["L0_attn_a_" + n], f32).reshape(-1) for n in names]
    vs.append(np.asarray(I["L0_attn_a_r_k"], f32).reshape(-1))
    vs.append(np.asarray(I["L0_attn_a_gn_scale"], f32).reshape(-1))
    vs.append(np.asarray(I["L0_attn_a_gn_offset"], f32).reshape(-1))
    V = np.stack(vs, axis=1)
    out["a_vecs"] = np.ascontiguousarray(V.reshape(NCH, 128, 13))
    return out


VI = {n: i for i, n in enumerate(
    ["x_r", "x_w", "x_k", "x_v", "x_a", "x_g", "w0", "a0", "k_k", "k_a",
     "r_k", "gn_s", "gn_o"])}


def emit_attn_a(b, xsrc, dst, W, pfx):
    p = b.p
    bf = bf16c(b)
    if "xln" not in b.com:
        b.com["xln"] = b.sbp("g_xln", [128, NCH * T], BF16)
    xln = b.com["xln"]
    ln_done = emit_ln(b, xsrc, xln, pfx + "l")
    ones_b = b.com["ct_ones128_b"]
    ones_f = b.com["ct_ones128_f"]
    identb = b.const_tile("identb", np.eye(128, dtype=bf))

    NB = T // 512
    NCC = T // LCH

    streams = {n: b.dram_tmp(pfx + "s_" + n, [D, T], BF16)
               for n in ["r", "k2", "kk", "bb", "v", "sz", "gg", "bo"]}
    y_hbm = b.dram_tmp(pfx + "s_y", [NCC, 128, 16 * LCH], BF16)

    with b.phase(pfx + "p"):
        vecs = b.sbp(pfx + "vecs", [128, NCH * 13], F32)
        vld = p.dma(SYNC, pfx + "c", lambda e: e.dma_start(
            out=vecs[:].rearrange("p (c v) -> p c v", v=13), in_=W["a_vecs"].rearrange("c p v -> p c v")))
        omc = b.sbp(pfx + "omc", [128, NCH * 7], F32)
        omtok = []
        for c in range(NCH):
            t1 = p.do(VEC, lambda e, c=c: e.tensor_scalar(
                out=omc[:, c * 7:c * 7 + 6], in0=vecs[:, c * 13:c * 13 + 6],
                scalar1=-1.0, scalar2=1.0, op0=ALU.mult, op1=ALU.add), after=[vld])
            t2 = p.do(VEC, lambda e, c=c: e.tensor_scalar(
                out=omc[:, c * 7 + 6:c * 7 + 7], in0=vecs[:, c * 13 + 9:c * 13 + 10],
                scalar1=-1.0, scalar2=1.0, op0=ALU.mult, op1=ALU.add), after=[vld])
            omtok += [t1, t2]

        mix = b.sb(pfx + "mix", [128, NCH * T], BF16)
        wslab = b.sb(pfx + "ws", [128, 2 * 2048], BF16)
        stg = b.sb(pfx + "stg", [128, 2 * T], BF16)
        t64 = b.sb(pfx + "t64", [64, T], BF16)
        lora1 = b.sb(pfx + "lo1", [128, 2048], BF16)
        lora2 = b.sb(pfx + "lo2", [128, 2048], BF16)
        t128 = b.sb(pfx + "t128", [128, T], BF16)
        pp = b.ps(pfx + "pp", [128, 3072], F32)
        def bank(i):
            return pp[:, i * 512:(i + 1) * 512]
        bank_g = {i: None for i in range(6)}
        wuser = {0: None, 1: None}
        stg_g = {0: None, 1: None}

        mix_built = []
        mix_users = []

        def build_mix(vi):
            nonlocal mix_built, mix_users
            toks = []
            for c in range(NCH):
                ms = p.do(VEC, lambda e, c=c: e.memset(mix[:, c * T:c * T + 1], 0.0),
                          after=mix_users + omtok)
                t1 = p.do(VEC, lambda e, c=c, vi=vi: e.tensor_scalar(
                    out=mix[:, c * T + 1:c * T + T],
                    in0=xln[:, c * T:c * T + T - 1],
                    scalar1=vecs[:, c * 13 + vi:c * 13 + vi + 1], scalar2=None,
                    op0=ALU.mult), after=mix_users + omtok + ln_done)
                t2 = p.do(VEC, lambda e, c=c, vi=vi: e.scalar_tensor_tensor(
                    out=mix[:, c * T:c * T + T], in0=xln[:, c * T:c * T + T],
                    scalar=omc[:, c * 7 + vi:c * 7 + vi + 1],
                    in1=mix[:, c * T:c * T + T], op0=ALU.mult, op1=ALU.add),
                    after=[ms, t1])
                toks.append(t2)
            mix_built = toks
            mix_users = []

        def spill(stream, c, par, src_tok):
            st = p.dma(SYNC, pfx + "sp%d" % par,
                       lambda e, stream=stream, c=c, par=par: e.dma_start(
                           out=streams[stream][c * 128:(c + 1) * 128, :],
                           in_=stg[:, par * T:(par + 1) * T]), after=[src_tok])
            stg_g[par] = st
            return st

        def proj_big(wkey, stream, vi):
            build_mix(vi)
            for m in range(NCH):
                par = m % 2
                wt = p.dma(POOL, pfx + "w%d" % par,
                           lambda e, m=m, par=par, wkey=wkey: e.dma_start(
                               out=wslab[:, par * 2048:(par + 1) * 2048],
                               in_=W[wkey][m]), after=[wuser[par]])
                evs = []
                for nb in range(NB):
                    bk = nb % 2
                    mt = p.do(TENS, lambda e, m=m, par=par, nb=nb: [
                        e.matmul(bank(nb % 2),
                                 wslab[:, par * 2048 + kc * 128:par * 2048 + kc * 128 + 128],
                                 mix[:, kc * T + nb * 512:kc * T + nb * 512 + 512],
                                 start=(kc == 0), stop=(kc == NCH - 1))
                        for kc in range(NCH)][-1],
                        after=[wt, bank_g[bk]] + mix_built)
                    mix_users.append(mt)
                    wuser[par] = mt
                    ev = p.do(SCAL, lambda e, m=m, nb=nb: e.activation(
                        out=stg[:, (m % 2) * T + nb * 512:(m % 2) * T + nb * 512 + 512],
                        in_=bank(nb % 2), func=AF.Identity),
                        after=[mt, stg_g[m % 2]])
                    bank_g[bk] = ev
                    evs.append(ev)
                spill(stream, m, m % 2, evs[-1])

        def proj_lora(w1key, w2key, midfunc, outfunc, bias_vi, stream, vi, mid_m=64):
            build_mix(vi)
            l1 = p.dma(POOL, pfx + "l1", lambda e, w1key=w1key: e.dma_start(
                out=lora1[:, 0:NCH * mid_m], in_=W[w1key]), after=[Tok(TENS, p.ecnt[TENS])])
            l2 = p.dma(POOL, pfx + "l2", lambda e, w2key=w2key: e.dma_start(
                out=lora2[0:mid_m, :], in_=W[w2key]), after=[Tok(TENS, p.ecnt[TENS])])
            mid = t64 if mid_m == 64 else t128
            for nb in range(NB):
                bk = nb % 2
                mt = p.do(TENS, lambda e, nb=nb, mid_m=mid_m: [
                    e.matmul(pp[0:mid_m, (nb % 2) * 512:(nb % 2) * 512 + 512],
                             lora1[:, kc * mid_m:(kc + 1) * mid_m],
                             mix[:, kc * T + nb * 512:kc * T + nb * 512 + 512],
                             start=(kc == 0), stop=(kc == NCH - 1))
                    for kc in range(NCH)][-1],
                    after=[l1, bank_g[bk]] + mix_built)
                mix_users.append(mt)
                ev = p.do(SCAL, lambda e, nb=nb, mid=mid, mid_m=mid_m, midfunc=midfunc:
                          e.activation(out=mid[0:mid_m, nb * 512:nb * 512 + 512],
                                       in_=pp[0:mid_m, (nb % 2) * 512:(nb % 2) * 512 + 512],
                                       func=midfunc), after=[mt])
                bank_g[bk] = ev
            mid_done = Tok(SCAL, p.ecnt[SCAL])
            for m in range(NCH):
                evs = []
                for nb in range(NB):
                    bk = 2 + nb % 2
                    mt = p.do(TENS, lambda e, m=m, nb=nb, mid=mid, mid_m=mid_m: e.matmul(
                        bank(2 + nb % 2), lora2[0:mid_m, m * 128:(m + 1) * 128],
                        mid[0:mid_m, nb * 512:nb * 512 + 512], start=True, stop=True),
                        after=[l2, bank_g[bk], mid_done])
                    def evf(e, m=m, nb=nb, outfunc=outfunc, bias_vi=bias_vi):
                        kw = {}
                        if bias_vi is not None:
                            kw["bias"] = vecs[:, m * 13 + bias_vi:m * 13 + bias_vi + 1]
                        return e.activation(
                            out=stg[:, (m % 2) * T + nb * 512:(m % 2) * T + nb * 512 + 512],
                            in_=bank(2 + nb % 2), func=outfunc, **kw)
                    ev = p.do(SCAL, evf, after=[mt, stg_g[m % 2], vld])
                    bank_g[bk] = ev
                    evs.append(ev)
                spill(stream, m, m % 2, evs[-1])

        streams["kraw"] = b.dram_tmp(pfx + "s_kraw", [D, T], BF16)
        streams["as"] = b.dram_tmp(pfx + "s_as", [D, T], BF16)

        proj_big("a_wr", "r", VI["x_r"])
        proj_big("a_wk", "kraw", VI["x_k"])
        proj_big("a_wv", "v", VI["x_v"])
        proj_lora("a_w1", "a_w2", AF.Tanh, AF.Sigmoid, VI["w0"], "sz", VI["x_w"], 64)
        proj_lora("a_a1", "a_a2", AF.Identity, AF.Sigmoid, VI["a0"], "as", VI["x_a"], 64)
        proj_lora("a_g1", "a_g2", AF.Sigmoid, AF.Identity, None, "gg", VI["x_g"], 128)
        vecs_keep = vecs
        omc_keep = omc

    with b.phase(pfx + "q"):
        vecs = vecs_keep
        omc = omc_keep
        hsel2_np = np.zeros((128, 2), bf16c(b))
        hsel2_np[0:64, 0] = 1.0
        hsel2_np[64:128, 1] = 1.0
        bsel2_np = np.zeros((2, 128), np.float32)
        bsel2_np[0, 0:64] = 1.0
        bsel2_np[1, 64:128] = 1.0
        hsel2 = b.const_tile("hsel2", hsel2_np)
        bsel2 = b.const_tile("bsel2", bsel2_np)
        pk = b.sb(pfx + "pk", [128, T], BF16)
        pa = b.sb(pfx + "pa", [128, T], BF16)
        rrt = b.sb(pfx + "rrt", [128, T], BF16)
        pv = b.sb(pfx + "pv", [128, T], BF16)
        kkt = b.sb(pfx + "kkt", [128, T], BF16)
        k2t = b.sb(pfx + "k2t", [128, T], BF16)
        btl = b.sb(pfx + "btl", [128, T], BF16)
        bont = b.sb(pfx + "bont", [128, T], BF16)
        sqs = b.sb(pfx + "sqs", [128, 1024], BF16)
        rows2 = b.sb(pfx + "rws", [2, T], F32)
        pq = b.ps(pfx + "pq", [128, 3072], F32)
        gq = {"ld": None, "done": None}
        for c in range(NCH):
            lk = p.dma(SYNC, pfx + "qk", lambda e, c=c: e.dma_start(
                out=pk[:], in_=streams["kraw"][c * 128:(c + 1) * 128, :]), after=[gq["done"]])
            la = p.dma(SYNC, pfx + "qa", lambda e, c=c: e.dma_start(
                out=pa[:], in_=streams["as"][c * 128:(c + 1) * 128, :]), after=[gq["done"]])
            lr = p.dma(SYNC, pfx + "qr", lambda e, c=c: e.dma_start(
                out=rrt[:], in_=streams["r"][c * 128:(c + 1) * 128, :]), after=[gq["done"]])
            lv = p.dma(SYNC, pfx + "qv", lambda e, c=c: e.dma_start(
                out=pv[:], in_=streams["v"][c * 128:(c + 1) * 128, :]), after=[gq["done"]])
            k1 = p.do(VEC, lambda e, c=c: e.tensor_scalar(
                out=kkt[:], in0=pk[:], scalar1=vecs[:, c * 13 + 8:c * 13 + 9],
                scalar2=None, op0=ALU.mult), after=[lk, gq["done"]])
            nrm_ev = []
            for nb in range(NB):
                s1 = p.do(SCAL, lambda e, nb=nb: e.activation(
                    out=sqs[:, (nb % 2) * 512:(nb % 2) * 512 + 512],
                    in_=kkt[:, nb * 512:nb * 512 + 512], func=AF.Square),
                    after=[k1] + ([nrm_ev[nb - 2]] if nb >= 2 else []))
                m1 = p.do(TENS, lambda e, nb=nb: e.matmul(
                    pq[0:2, 2048:2048 + 512], hsel2[:],
                    sqs[:, (nb % 2) * 512:(nb % 2) * 512 + 512], start=True, stop=True),
                    after=[s1, nrm_ev[-1] if nrm_ev else None])
                e1 = p.do(VEC, lambda e, nb=nb: e.tensor_copy(
                    out=rows2[0:2, nb * 512:nb * 512 + 512], in_=pq[0:2, 2048:2048 + 512]),
                    after=[m1])
                nrm_ev.append(e1)
            s2 = p.do(SCAL, lambda e: e.activation(out=rows2[0:2, :], in_=rows2[0:2, :],
                                                   func=AF.Sqrt), after=nrm_ev)
            v2 = p.do(VEC, lambda e: e.tensor_scalar(out=rows2[0:2, :], in0=rows2[0:2, :],
                                                     scalar1=1e-12, scalar2=None,
                                                     op0=ALU.max), after=[s2])
            v3 = p.do(VEC, lambda e: e.reciprocal(out=rows2[0:2, :], in_=rows2[0:2, :]),
                      after=[v2])
            kk_done = []
            for nb in range(NB):
                m2 = p.do(TENS, lambda e, nb=nb: e.matmul(
                    pq[:, 2560:2560 + 512], bsel2[:],
                    rows2[0:2, nb * 512:nb * 512 + 512], start=True, stop=True),
                    after=[v3, kk_done[-1] if kk_done else None])
                v4 = p.do(VEC, lambda e, nb=nb: e.tensor_tensor(
                    out=kkt[:, nb * 512:nb * 512 + 512], in0=kkt[:, nb * 512:nb * 512 + 512],
                    in1=pq[:, 2560:2560 + 512], op=ALU.mult), after=[m2])
                kk_done.append(v4)
            b1 = p.do(VEC, lambda e, c=c: e.tensor_scalar(
                out=btl[:], in0=pa[:], scalar1=vecs[:, c * 13 + 9:c * 13 + 10],
                scalar2=omc[:, c * 7 + 6:c * 7 + 7], op0=ALU.mult, op1=ALU.add),
                after=[la, gq["done"]])
            b2 = p.do(VEC, lambda e: e.tensor_tensor(out=k2t[:], in0=pk[:], in1=btl[:],
                                                     op=ALU.mult), after=[b1, k1])
            b3 = p.do(VEC, lambda e: e.tensor_tensor(out=btl[:], in0=kkt[:], in1=pa[:],
                                                     op=ALU.mult), after=kk_done + [b1, b2])
            z1 = p.do(VEC, lambda e: e.tensor_tensor(out=bont[:], in0=rrt[:], in1=k2t[:],
                                                     op=ALU.mult), after=[lr, b2])
            z2 = p.do(VEC, lambda e, c=c: e.tensor_scalar(
                out=bont[:], in0=bont[:], scalar1=vecs[:, c * 13 + 10:c * 13 + 11],
                scalar2=None, op0=ALU.mult), after=[z1])
            bon_done = []
            for nb in range(NB):
                m3 = p.do(TENS, lambda e, nb=nb: e.matmul(
                    pq[0:2, 2048:2048 + 512], hsel2[:],
                    bont[:, nb * 512:nb * 512 + 512], start=True, stop=True),
                    after=[z2, bon_done[-1] if bon_done else None] + ([nrm_ev[-1]] if nb == 0 else []))
                e3 = p.do(VEC, lambda e, nb=nb: e.tensor_copy(
                    out=rows2[0:2, nb * 512:nb * 512 + 512], in_=pq[0:2, 2048:2048 + 512]),
                    after=[m3, kk_done[-1] if nb == 0 else None])
                m4 = p.do(TENS, lambda e, nb=nb: e.matmul(
                    pq[:, 2560:2560 + 512], bsel2[:],
                    rows2[0:2, nb * 512:nb * 512 + 512], start=True, stop=True),
                    after=[e3, kk_done[-1] if nb == 0 else None])
                v5 = p.do(VEC, lambda e, nb=nb: e.tensor_tensor(
                    out=bont[:, nb * 512:nb * 512 + 512], in0=pv[:, nb * 512:nb * 512 + 512],
                    in1=pq[:, 2560:2560 + 512], op=ALU.mult), after=[m4, lv])
                bon_done.append(v5)
            sp1 = p.dma(SYNC, pfx + "qs1", lambda e, c=c: e.dma_start(
                out=streams["kk"][c * 128:(c + 1) * 128, :], in_=kkt[:]), after=kk_done + [b3])
            sp2 = p.dma(SYNC, pfx + "qs2", lambda e, c=c: e.dma_start(
                out=streams["k2"][c * 128:(c + 1) * 128, :], in_=k2t[:]), after=[b2, z1])
            sp3 = p.dma(SYNC, pfx + "qs3", lambda e, c=c: e.dma_start(
                out=streams["bb"][c * 128:(c + 1) * 128, :], in_=btl[:]), after=[b3])
            sp4 = p.dma(SYNC, pfx + "qs4", lambda e, c=c: e.dma_start(
                out=streams["bo"][c * 128:(c + 1) * 128, :], in_=bont[:]), after=bon_done)
            gq["done"] = p.do(VEC, lambda e: e.engine_nop(), after=[sp1, sp2, sp3, sp4])

    with b.phase(pfx + "r"):
        bfdt = bf16c(b)
        tri_i = b.const_tile("tri_i", np.triu(np.ones((128, 128), bfdt), 0))
        tri_s = b.const_tile("tri_s", np.triu(np.ones((128, 128), bfdt), 1))
        msk_su = b.const_tile("msk_su", np.triu(np.ones((128, 128), np.float32), 1))
        msk_iu = b.const_tile("msk_iu", np.triu(np.ones((128, 128), np.float32), 0))
        msk_sl = b.const_tile("msk_sl", np.tril(np.ones((128, 128), np.float32), -1))

        SDE = 0.60653065971263342
        nst = 6
        sn = ["r", "k2", "kk", "bb", "v", "sz"]
        stt = {n: b.sb(pfx + "st_" + n, [128, 2 * 2048], BF16) for n in sn}
        lamT = b.sb(pfx + "lamT", [128, 6144], BF16)
        lamL32 = b.sb(pfx + "lamL", [128, NCH], F32)
        ltm = b.sb(pfx + "ltm", [128, 2048], BF16)
        sztm = b.sb(pfx + "sztm", [128, 2048], BF16)
        ktm = b.sb(pfx + "ktm", [128, 2048], BF16)
        btm = b.sb(pfx + "btm", [128, 2048], BF16)
        vtm = b.sb(pfx + "vtm", [128, 2048], BF16)
        AR = b.sb(pfx + "AR", [128, NCH * 256], BF16)
        BK = b.sb(pfx + "BK", [128, NCH * 256], BF16)
        ysb = b.sb(pfx + "ysb", [128, 2048], BF16)
        stateT = b.sb(pfx + "stT", [128, 1024], BF16)
        hw8 = [{n: b.sb(pfx + "h%d_%s" % (q, n), [128, 128], BF16)
                for n in ["gt", "pt", "ht", "qt", "gm", "y1", "g1", "m0", "m1",
                          "cb", "ub"]} for q in range(8)]
        pr = b.ps(pfx + "prw", [128, 3072], F32)
        pt2 = b.ps(pfx + "pt2", [128, 2048], BF16)
        def rbank(i):
            return pr[:, i * 512:(i + 1) * 512]

        st0 = p.do(VEC, lambda e: e.memset(stateT[:], 0.0))
        ld_g = {}
        rb_g = {i: None for i in range(6)}
        ptg2 = {0: None, 1: None}
        prev_cc = {"tens": None, "vec": None, "scal": None, "ysp": None,
                   "state": [st0] * HA}

        for cc in range(NCC):
            par = cc % 2
            lds = {}
            for n in sn:
                lds[n] = p.dma(SYNC, pfx + "r_" + n,
                               lambda e, n=n, cc=cc, par=par: e.dma_start(
                                   out=stt[n][:, par * 2048:(par + 1) * 2048]
                                       .rearrange("p (c t) -> p c t", t=128),
                                   in_=streams[n].rearrange("(c p) t -> p c t", p=128)
                                       [:, :, cc * 128:(cc + 1) * 128]),
                               after=[ld_g.get((n, par))])
            cur = {n: stt[n][:, par * 2048:(par + 1) * 2048] for n in sn}

            tr_toks = []
            for fc in range(NCH):
                tr = p.do(TENS, lambda e, fc=fc, src=cur["sz"]: e.transpose(
                    pt2[:, (fc % 2) * 1024:(fc % 2) * 1024 + 128],
                    src[:, fc * 128:(fc + 1) * 128], identb[:]),
                    after=[lds["sz"], ptg2[fc % 2]])
                cp = p.do(VEC, lambda e, fc=fc: e.tensor_copy(
                    out=sztm[:, fc * 128:(fc + 1) * 128],
                    in_=pt2[:, (fc % 2) * 1024:(fc % 2) * 1024 + 128]),
                    after=[tr, prev_cc["vec"] if fc == 0 else None])
                ptg2[fc % 2] = cp
                tr_toks.append(cp)
            cs_tm = []
            for q in range(4):
                mt = p.do(TENS, lambda e, q=q: e.matmul(
                    rbank(q), tri_i[:], sztm[:, q * 512:q * 512 + 512],
                    start=True, stop=True), after=tr_toks + [rb_g[q]])
                cs_tm.append(mt)
            ex1 = []
            for q in range(4):
                e1 = p.do(SCAL, lambda e, q=q: e.activation(
                    out=ltm[:, q * 512:q * 512 + 512], in_=rbank(q), func=AF.Exp,
                    scale=SDE), after=[cs_tm[q]] +
                    ([prev_cc["scal"], prev_cc["vec"]] if q == 0 else []))
                rb_g[q] = e1
                ex1.append(e1)
            lam_toks = []
            for fc in range(NCH):
                bki = 4 + (fc % 2)
                mi = p.do(TENS, lambda e, fc=fc, bki=bki: [
                    e.matmul(pr[:, bki * 512:bki * 512 + 128],
                             sztm[:, fc * 128:(fc + 1) * 128], tri_i[:],
                             start=True, stop=True),
                    e.matmul(pr[:, bki * 512 + 128:bki * 512 + 256],
                             sztm[:, fc * 128:(fc + 1) * 128], tri_s[:],
                             start=True, stop=True)][-1],
                    after=tr_toks + [rb_g[bki]])
                e2 = p.do(SCAL, lambda e, fc=fc, bki=bki: [
                    e.activation(out=lamT[:, fc * 128:fc * 128 + 128],
                                 in_=pr[:, bki * 512:bki * 512 + 128],
                                 func=AF.Exp, scale=-SDE),
                    e.activation(out=lamT[:, 2048 + fc * 128:2048 + fc * 128 + 128],
                                 in_=pr[:, bki * 512 + 128:bki * 512 + 256],
                                 func=AF.Exp, scale=-SDE),
                    e.activation(out=lamT[:, 4096 + fc * 128:4096 + fc * 128 + 128],
                                 in_=pr[:, bki * 512:bki * 512 + 128],
                                 func=AF.Exp, scale=SDE),
                    e.activation(out=lamL32[:, fc:fc + 1],
                                 in_=pr[:, bki * 512 + 127:bki * 512 + 128],
                                 func=AF.Exp, scale=-SDE)][-1],
                    after=[mi] + ([prev_cc["vec"], prev_cc["scal"]] if fc == 0 else []))
                rb_g[bki] = e2
                e3 = p.do(VEC, lambda e, fc=fc: e.tensor_scalar(
                    out=lamT[:, 2048 + fc * 128:2048 + fc * 128 + 128],
                    in0=lamT[:, 2048 + fc * 128:2048 + fc * 128 + 128],
                    scalar1=-1.0, scalar2=None, op0=ALU.mult), after=[e2])
                lam_toks.append(e3)

            fm_toks = []
            for fc in range(NCH):
                a1_ = p.do(VEC, lambda e, fc=fc, cur=cur: e.tensor_tensor(
                    out=AR[:, fc * 256:fc * 256 + 128],
                    in0=cur["kk"][:, fc * 128:(fc + 1) * 128],
                    in1=lamT[:, 2048 + fc * 128:2048 + fc * 128 + 128], op=ALU.mult),
                    after=[lam_toks[fc], lds["kk"], prev_cc["tens"]])
                a2_ = p.do(VEC, lambda e, fc=fc, cur=cur: e.tensor_tensor(
                    out=AR[:, fc * 256 + 128:fc * 256 + 256],
                    in0=cur["r"][:, fc * 128:(fc + 1) * 128],
                    in1=lamT[:, fc * 128:fc * 128 + 128], op=ALU.mult),
                    after=[lam_toks[fc], lds["r"], prev_cc["tens"]])
                a3_ = p.do(VEC, lambda e, fc=fc, cur=cur: e.tensor_tensor(
                    out=BK[:, fc * 256:fc * 256 + 128],
                    in0=cur["bb"][:, fc * 128:(fc + 1) * 128],
                    in1=lamT[:, 4096 + fc * 128:4096 + fc * 128 + 128], op=ALU.mult),
                    after=[lam_toks[fc], lds["bb"], prev_cc["tens"]])
                a4_ = p.do(VEC, lambda e, fc=fc, cur=cur: e.tensor_tensor(
                    out=BK[:, fc * 256 + 128:fc * 256 + 256],
                    in0=cur["k2"][:, fc * 128:(fc + 1) * 128],
                    in1=lamT[:, 4096 + fc * 128:4096 + fc * 128 + 128], op=ALU.mult),
                    after=[lam_toks[fc], lds["k2"], prev_cc["tens"]])
                fm_toks += [a1_, a2_, a3_, a4_]

            tm_done = {}
            for n, dsttile in (("k2", ktm), ("bb", btm), ("v", vtm)):
                tts = []
                for fc in range(NCH):
                    tr = p.do(TENS, lambda e, fc=fc, src=cur[n]: e.transpose(
                        pt2[:, (fc % 2) * 1024:(fc % 2) * 1024 + 128],
                        src[:, fc * 128:(fc + 1) * 128], identb[:]),
                        after=[lds[n], ptg2[fc % 2]])
                    cp = p.do(VEC, lambda e, n=n, fc=fc, dsttile=dsttile: e.tensor_copy(
                        out=dsttile[:, fc * 128:(fc + 1) * 128],
                        in_=pt2[:, (fc % 2) * 1024:(fc % 2) * 1024 + 128]),
                        after=[tr, prev_cc["tens"] if fc == 0 else None])
                    ptg2[fc % 2] = cp
                    tts.append(cp)
                tm_done[n] = tts
            ksc = []
            for q in range(4):
                k_ = p.do(VEC, lambda e, q=q: e.tensor_tensor(
                    out=ktm[:, q * 512:q * 512 + 512], in0=ktm[:, q * 512:q * 512 + 512],
                    in1=ltm[:, q * 512:q * 512 + 512], op=ALU.mult),
                    after=tm_done["k2"] + [ex1[q]])
                b_ = p.do(VEC, lambda e, q=q: e.tensor_tensor(
                    out=btm[:, q * 512:q * 512 + 512], in0=btm[:, q * 512:q * 512 + 512],
                    in1=ltm[:, q * 512:q * 512 + 512], op=ALU.mult),
                    after=tm_done["bb"] + [ex1[q]])
                ksc += [k_, b_]
            ld_g[("sz", par)] = lam_toks[-1]
            ld_g[("kk", par)] = fm_toks[-1]
            ld_g[("r", par)] = fm_toks[-1]
            ld_g[("k2", par)] = tm_done["k2"][-1]
            ld_g[("bb", par)] = tm_done["bb"][-1]
            ld_g[("v", par)] = tm_done["v"][-1]

            head_toks = [None] * HA
            ysb_toks = []
            rbc = [0]
            def nxt_bank():
                i = rbc[0] % 6
                rbc[0] += 1
                return i
            for hg in range(0, HA, 8):
                H8 = list(range(hg, hg + 8))
                X = {}
                for h in H8:
                    q = h % 8
                    fc, po = h // 2, (h % 2) * 64
                    X[h] = {
                        "a": AR[po:po + 64, fc * 256:fc * 256 + 128],
                        "r": AR[po:po + 64, fc * 256 + 128:fc * 256 + 256],
                        "b": BK[po:po + 64, fc * 256:fc * 256 + 128],
                        "k": BK[po:po + 64, fc * 256 + 128:fc * 256 + 256],
                        "ar": AR[po:po + 64, fc * 256:fc * 256 + 256],
                        "Ktm": ktm[:, fc * 128 + po:fc * 128 + po + 64],
                        "Btm": btm[:, fc * 128 + po:fc * 128 + po + 64],
                        "Vtm": vtm[:, fc * 128 + po:fc * 128 + po + 64],
                        "stT": stateT[po:po + 64, fc * 64:fc * 64 + 64],
                        "lamL": lamL32[po:po + 64, fc:fc + 1],
                        "po": po, "fc": fc,
                        "eng": VEC if (h % 2 == 0) else SCAL,
                        "w": hw8[q],
                    }
                dep0 = fm_toks + ksc + tm_done["v"] + [t for t in prev_cc["state"] if t]

                def ev_step(eng, dst, src_ap, after):
                    if eng == SCAL:
                        return p.do(SCAL, lambda e, dst=dst, src_ap=src_ap: e.activation(
                            out=dst, in_=src_ap, func=AF.Identity), after=after)
                    return p.do(VEC, lambda e, dst=dst, src_ap=src_ap: e.tensor_copy(
                        out=dst, in_=src_ap), after=after)

                for h in H8:
                    x = X[h]
                    w = x["w"]
                    b0 = nxt_bank(); b1 = nxt_bank(); b2 = nxt_bank()
                    pa_ = p.do(TENS, lambda e, x=x, b0=b0: e.matmul(
                        pr[:, b0 * 512:b0 * 512 + 256], x["b"], x["ar"],
                        start=True, stop=True), after=dep0 + [rb_g[b0]])
                    pb_ = p.do(TENS, lambda e, x=x, b1=b1: e.matmul(
                        pr[:, b1 * 512:b1 * 512 + 256], x["k"], x["ar"],
                        start=True, stop=True), after=dep0 + [rb_g[b1]])
                    pc_ = p.do(TENS, lambda e, x=x, b2=b2: e.matmul(
                        pr[:, b2 * 512:b2 * 512 + 128], x["a"], x["b"],
                        start=True, stop=True), after=dep0 + [rb_g[b2]])
                    ma = p.do(VEC, lambda e, w=w, b0=b0: e.tensor_tensor(
                        out=w["gt"][:], in0=pr[:, b0 * 512:b0 * 512 + 128],
                        in1=msk_su[:], op=ALU.mult), after=[pa_])
                    ma2 = p.do(VEC, lambda e, w=w, b0=b0: e.tensor_tensor(
                        out=w["pt"][:], in0=pr[:, b0 * 512 + 128:b0 * 512 + 256],
                        in1=msk_iu[:], op=ALU.mult), after=[pa_])
                    mb = p.do(SCAL if h % 2 else VEC, None, after=[pb_]) if False else None
                    mb = p.do(VEC, lambda e, w=w, b1=b1: e.tensor_tensor(
                        out=w["ht"][:], in0=pr[:, b1 * 512:b1 * 512 + 128],
                        in1=msk_su[:], op=ALU.mult), after=[pb_])
                    mb2 = p.do(VEC, lambda e, w=w, b1=b1: e.tensor_tensor(
                        out=w["qt"][:], in0=pr[:, b1 * 512:b1 * 512 + 256]
                        [:, 128:256] if False else pr[:, b1 * 512 + 128:b1 * 512 + 256],
                        in1=msk_iu[:], op=ALU.mult), after=[pb_])
                    mc2 = p.do(VEC, lambda e, w=w, b2=b2: e.tensor_tensor(
                        out=w["gm"][:], in0=pr[:, b2 * 512:b2 * 512 + 128],
                        in1=msk_sl[:], op=ALU.mult), after=[pc_])
                    m0t = p.do(VEC, lambda e, w=w: e.tensor_tensor(
                        out=w["m0"][:], in0=w["gt"][:], in1=identb[:], op=ALU.add),
                        after=[ma])
                    rb_g[b0], rb_g[b1], rb_g[b2] = ma2, mb2, mc2
                    x["masks"] = (ma, ma2, mb, mb2, mc2)
                    x["y_tile"], x["y_alt"] = w["gt"], w["y1"]
                    x["g_tile"], x["g_alt"] = w["gm"], w["g1"]
                    x["m_tile"], x["m_alt"] = w["m0"], w["m1"]
                    x["y_tok"], x["g_tok"], x["m_tok"] = ma, mc2, m0t

                for lev in range(1, 7):
                    for h in H8:
                        x = X[h]
                        bk_ = nxt_bank()
                        gt2 = p.do(TENS, lambda e, yt=x["y_tile"], gtl=x["g_tile"],
                                   bk_=bk_: e.matmul(
                            pr[:, bk_ * 512:bk_ * 512 + 128], yt[:], gtl[:],
                            start=True, stop=True),
                            after=[x["y_tok"], x["g_tok"], rb_g[bk_]])
                        ge = ev_step(x["eng"], x["g_alt"][:],
                                     pr[:, bk_ * 512:bk_ * 512 + 128], [gt2])
                        rb_g[bk_] = ge
                        x["g_new_tok"] = ge
                    if lev < 6:
                        for h in H8:
                            x = X[h]
                            bk_ = nxt_bank()
                            yt2 = p.do(TENS, lambda e, yt=x["y_tile"], gtl=x["g_tile"],
                                       bk_=bk_: e.matmul(
                                pr[:, bk_ * 512:bk_ * 512 + 128], gtl[:], yt[:],
                                start=True, stop=True),
                                after=[x["y_tok"], x["g_tok"], rb_g[bk_]])
                            ye = ev_step(x["eng"], x["y_alt"][:],
                                         pr[:, bk_ * 512:bk_ * 512 + 128], [yt2])
                            rb_g[bk_] = ye
                            x["y_new_tok"] = ye
                    for h in H8:
                        x = X[h]
                        bk_ = nxt_bank()
                        mm1 = p.do(TENS, lambda e, mt_=x["m_tile"], ga=x["g_alt"],
                                   bk_=bk_: [
                            e.matmul(pr[:, bk_ * 512:bk_ * 512 + 128], identb[:],
                                     mt_[:], start=True, stop=False),
                            e.matmul(pr[:, bk_ * 512:bk_ * 512 + 128], ga[:],
                                     mt_[:], start=False, stop=True)][-1],
                            after=[x["m_tok"], x["g_new_tok"], rb_g[bk_]])
                        me = ev_step(x["eng"], x["m_alt"][:],
                                     pr[:, bk_ * 512:bk_ * 512 + 128], [mm1])
                        rb_g[bk_] = me
                        x["m_tok"] = me
                        x["m_tile"], x["m_alt"] = x["m_alt"], x["m_tile"]
                        if lev < 6:
                            x["y_tile"], x["y_alt"] = x["y_alt"], x["y_tile"]
                            x["y_tok"] = x["y_new_tok"]
                        x["g_tile"], x["g_alt"] = x["g_alt"], x["g_tile"]
                        x["g_tok"] = x["g_new_tok"]

                for h in H8:
                    x = X[h]
                    bk_ = nxt_bank()
                    cmm = p.do(TENS, lambda e, x=x, bk_=bk_: [
                        e.matmul(pr[:, bk_ * 512:bk_ * 512 + 64], x["a"], x["stT"],
                                 start=True, stop=False),
                        e.matmul(pr[:, bk_ * 512:bk_ * 512 + 64], x["w"]["ht"][:],
                                 x["Vtm"], start=False, stop=True)][-1],
                        after=[rb_g[bk_], x["masks"][2],
                               prev_cc["state"][h] if prev_cc["state"][h] else prev_cc["state"][0] or st0])
                    cev = ev_step(x["eng"], x["w"]["cb"][:, 0:64],
                                  pr[:, bk_ * 512:bk_ * 512 + 64], [cmm])
                    rb_g[bk_] = cev
                    x["cmm"], x["cev"] = cmm, cev
                for h in H8:
                    x = X[h]
                    bk_ = nxt_bank()
                    umm = p.do(TENS, lambda e, mt_=x["m_tile"], cb_=x["w"]["cb"],
                               bk_=bk_: e.matmul(
                        pr[:, bk_ * 512:bk_ * 512 + 64], mt_[:],
                        cb_[:, 0:64], start=True, stop=True),
                        after=[x["m_tok"], x["cev"], rb_g[bk_]])
                    uev = ev_step(x["eng"], x["w"]["ub"][:, 0:64],
                                  pr[:, bk_ * 512:bk_ * 512 + 64], [umm])
                    rb_g[bk_] = uev
                    x["umm"], x["uev"] = umm, uev
                for h in H8:
                    x = X[h]
                    bk_ = nxt_bank()
                    ymm = p.do(TENS, lambda e, x=x, bk_=bk_: [
                        e.matmul(pr[0:64, bk_ * 512:bk_ * 512 + 128], x["stT"], x["r"],
                                 start=True, stop=False),
                        e.matmul(pr[0:64, bk_ * 512:bk_ * 512 + 128], x["w"]["ub"][:, 0:64],
                                 x["w"]["pt"][:], start=False, stop=False),
                        e.matmul(pr[0:64, bk_ * 512:bk_ * 512 + 128], x["Vtm"],
                                 x["w"]["qt"][:], start=False, stop=True)][-1],
                        after=[x["uev"], x["masks"][1], x["masks"][3], rb_g[bk_]])
                    yev = p.do(VEC, lambda e, x=x, bk_=bk_: e.tensor_copy(
                        out=ysb[x["po"]:x["po"] + 64, x["fc"] * 128:x["fc"] * 128 + 128],
                        in_=pr[0:64, bk_ * 512:bk_ * 512 + 128]),
                        after=[ymm, prev_cc["ysp"]])
                    rb_g[bk_] = yev
                    x["ymm"] = ymm
                    ysb_toks.append(yev)
                for h in H8:
                    x = X[h]
                    bk_ = nxt_bank()
                    smm = p.do(TENS, lambda e, x=x, bk_=bk_: [
                        e.matmul(pr[0:64, bk_ * 512:bk_ * 512 + 64],
                                 identb[x["po"]:x["po"] + 64, x["po"]:x["po"] + 64],
                                 x["stT"], start=True, stop=False),
                        e.matmul(pr[0:64, bk_ * 512:bk_ * 512 + 64], x["Btm"],
                                 x["w"]["ub"][:, 0:64], start=False, stop=False),
                        e.matmul(pr[0:64, bk_ * 512:bk_ * 512 + 64], x["Ktm"],
                                 x["Vtm"], start=False, stop=True)][-1],
                        after=[x["uev"], x["ymm"], x["cmm"], rb_g[bk_]])
                    sev = p.do(VEC, lambda e, x=x, bk_=bk_: e.tensor_scalar(
                        out=x["stT"], in0=pr[0:64, bk_ * 512:bk_ * 512 + 64],
                        scalar1=x["lamL"], scalar2=None, op0=ALU.mult), after=[smm])
                    rb_g[bk_] = sev
                    head_toks[h] = sev
            prev_cc["state"] = head_toks
            prev_cc["state"] = head_toks

            ysp = p.dma(SYNC, pfx + "ry", lambda e, cc=cc: e.dma_start(
                out=y_hbm[cc], in_=ysb[:]), after=ysb_toks)
            prev_cc["ysp"] = ysp
            prev_cc["tens"] = Tok(TENS, p.ecnt[TENS])
            prev_cc["vec"] = Tok(VEC, p.ecnt[VEC])
            prev_cc["scal"] = Tok(SCAL, p.ecnt[SCAL])

    with b.phase(pfx + "e"):
        vecs = vecs_keep
        obuf = b.sb(pfx + "ob", [128, NCH * T], BF16)
        yt = b.sb(pfx + "yt", [128, T], BF16)
        gt_ = b.sb(pfx + "gt", [128, T], BF16)
        bot = b.sb(pfx + "bot", [128, T], BF16)
        wslab2 = b.sb(pfx + "ws2", [128, 2 * 2048], BF16)
        xo = b.sb(pfx + "xo", [128, T], F32)
        xrt = b.sb(pfx + "xr", [128, T], F32)
        pe_ = b.ps(pfx + "pe", [128, 3072], F32)
        eg = {"done": None}
        ob_toks = []
        for c in range(NCH):
            ly = p.dma(SYNC, pfx + "ey", lambda e, c=c: e.dma_start(
                out=yt[:].rearrange("p (n t) -> p n t", t=128),
                in_=y_hbm.rearrange("n p f -> p n f")[:, :, c * 128:(c + 1) * 128]),
                after=[eg["done"]])
            lg_ = p.dma(SYNC, pfx + "eg", lambda e, c=c: e.dma_start(
                out=gt_[:], in_=streams["gg"][c * 128:(c + 1) * 128, :]),
                after=[eg["done"]])
            lb = p.dma(SYNC, pfx + "eb", lambda e, c=c: e.dma_start(
                out=bot[:], in_=streams["bo"][c * 128:(c + 1) * 128, :]),
                after=[eg["done"]])
            t1 = p.do(VEC, lambda e, c=c: e.tensor_scalar(
                out=yt[:], in0=yt[:], scalar1=vecs[:, c * 13 + 11:c * 13 + 12],
                scalar2=vecs[:, c * 13 + 12:c * 13 + 13], op0=ALU.mult, op1=ALU.add),
                after=[ly, eg["done"]])
            t2 = p.do(VEC, lambda e: e.tensor_tensor(out=yt[:], in0=yt[:], in1=bot[:],
                                                     op=ALU.add), after=[t1, lb])
            t3 = p.do(VEC, lambda e, c=c: e.tensor_tensor(
                out=obuf[:, c * T:(c + 1) * T], in0=yt[:], in1=gt_[:], op=ALU.mult),
                after=[t2, lg_])
            eg["done"] = t3
            ob_toks.append(t3)
        wuser3 = {0: None, 1: None}
        bank_g3 = {i: None for i in range(6)}
        xrt_g = None
        xo_g = None
        for m in range(NCH):
            par = m % 2
            wt = p.dma(POOL, pfx + "ew%d" % par,
                       lambda e, m=m, par=par: e.dma_start(
                           out=wslab2[:, par * 2048:(par + 1) * 2048],
                           in_=W["a_wo"][m]), after=[wuser3[par]])
            lt = p.dma(SYNC, pfx + "erx",
                       lambda e, m=m: e.dma_start(
                           out=xrt[:], in_=xsrc[m * 128:(m + 1) * 128, :]), after=[xrt_g])
            evs = []
            for nb in range(T // 512):
                bk = nb % 2
                mt = p.do(TENS, lambda e, m=m, par=par, nb=nb: [
                    e.matmul(pe_[:, (nb % 2) * 512:(nb % 2) * 512 + 512],
                             wslab2[:, par * 2048 + kc * 128:par * 2048 + kc * 128 + 128],
                             obuf[:, kc * T + nb * 512:kc * T + nb * 512 + 512],
                             start=(kc == 0), stop=(kc == NCH - 1))
                    for kc in range(NCH)][-1],
                    after=[wt, bank_g3[bk]] + (ob_toks if m == 0 and nb == 0 else []))
                wuser3[par] = mt
                ev = p.do(SCAL, lambda e, nb=nb: e.activation(
                    out=xo[:, nb * 512:nb * 512 + 512],
                    in_=pe_[:, (nb % 2) * 512:(nb % 2) * 512 + 512], func=AF.Identity),
                    after=[mt, xo_g])
                bank_g3[bk] = ev
                ad = p.do(VEC, lambda e, nb=nb: e.tensor_tensor(
                    out=xo[:, nb * 512:nb * 512 + 512], in0=xo[:, nb * 512:nb * 512 + 512],
                    in1=xrt[:, nb * 512:nb * 512 + 512], op=ALU.add), after=[ev, lt])
                evs.append(ad)
            st = p.dma(SYNC, pfx + "esx",
                       lambda e, m=m: e.dma_start(
                           out=dst[m * 128:(m + 1) * 128, :], in_=xo[:]), after=evs)
            xrt_g = st
            xo_g = st


def build_attna_only():
    b = Build()
    xin = b.dram_in("x_in", [D, T], F32)
    out = b.dram_out("x_out", [D, T], F32)
    wshapes = {"a_wr": (16, 128, 2048), "a_wk": (16, 128, 2048), "a_wv": (16, 128, 2048),
               "a_wo": (16, 128, 2048), "a_w1": (128, 1024), "a_a1": (128, 1024),
               "a_g1": (128, 2048), "a_w2": (64, 2048), "a_a2": (64, 2048),
               "a_g2": (128, 2048), "a_vecs": (NCH, 128, 13)}
    W = {k: b.dram_in(k, list(s), F32) for k, s in wshapes.items()}
    emit_attn_a(b, xin, out, W, "r1")
    with b.nc.Block() as blk:
        b.p.emit(blk)
        @blk.sync
        def _(eng):
            for name, rec in b.p.dsems.items():
                eng.wait_ge(rec[0], rec[1])
    return b


# version 04
# speedup vs baseline: 1.0197x; 1.0197x over previous
import contextlib
import numpy as np
import concourse.bass as bass
import concourse.mybir as mybir
from concourse.bass_utils import run_bass_kernel_spmd

F32 = mybir.dt.float32
BF16 = mybir.dt.bfloat16
AF = mybir.ActivationFunctionType
ALU = mybir.AluOpType
AX = mybir.AxisListType

D = 2048
T = 2048
NCH = 16
NTT = 16
NCORES = 8

E = 16
FI = 1408
SI = 2816
TOPK = 6
TBF = 1024
NBF = T // TBF

HA, NA, LCH = 32, 64, 128
HB, NOPE, ROPE, VHD, QKD = 16, 128, 64, 128, 192

DEBUG_TAPS = False
TENS, VEC, SCAL, POOL, SYNC = "tensor", "vector", "scalar", "gpsimd", "sync"
ENGINES = [TENS, VEC, SCAL, POOL, SYNC]



def _relayout(W):
    K, M = W.shape
    assert K % 128 == 0 and M % 128 == 0, (K, M)
    nK, nM = K // 128, M // 128
    return np.ascontiguousarray(
        W.reshape(nK, 128, nM, 128).transpose(2, 1, 0, 3).reshape(nM, 128, nK * 128))


def _pad(W, k=None, m=None):
    K, M = W.shape
    k, m = k or K, m or M
    if (k, m) == (K, M):
        return np.ascontiguousarray(W)
    out = np.zeros((k, m), W.dtype)
    out[:K, :M] = W
    return out



class Tok:
    __slots__ = ("eng", "val", "sem")
    def __init__(self, eng, val, sem=None):
        self.eng, self.val, self.sem = eng, val, sem


class Prog:
    def __init__(self, nc, ctx):
        self.nc = nc
        self.ctx = ctx
        self.steps = []
        self.esem = {e: ctx.enter_context(nc.semaphore("es_" + e)) for e in ENGINES}
        self.ecnt = {e: 0 for e in ENGINES}
        self.dsems = {}
        self.guard = []

    def _dsem(self, name):
        if name not in self.dsems:
            self.dsems[name] = [self.ctx.enter_context(self.nc.semaphore("ds_" + name)), 0]
        return self.dsems[name]

    def do(self, eng, fn, after=()):
        self.ecnt[eng] += 1
        tok = Tok(eng, self.ecnt[eng])
        deps = tuple(t for t in after if t is not None) + tuple(self.guard)
        self.steps.append((eng, fn, deps, None))
        return tok

    def dma(self, eng, sem_name, fn, after=()):
        rec = self._dsem(sem_name)
        prev = rec[1]
        rec[1] += 16
        tok = Tok(None, rec[1], sem_name)
        deps = tuple(t for t in after if t is not None) + tuple(self.guard)
        if prev > 0:
            deps = deps + (Tok(None, prev, sem_name),)
        self.steps.append((eng, fn, deps, sem_name))
        return tok

    def set_guard(self):
        self.guard = [Tok(e, self.ecnt[e]) for e in ENGINES if self.ecnt[e] > 0] + \
                     [Tok(None, rec[1], n) for n, rec in self.dsems.items()]

    def emit(self, block):
        seen = {e: {} for e in ENGINES}

        def emit_for(eng):
            def body(E_):
                for s_eng, fn, deps, dsem in self.steps:
                    if s_eng != eng:
                        continue
                    for d in deps:
                        if d.sem is None:
                            if d.eng == eng and d.val >= self.ecnt[eng]:
                                continue
                            tgt = self.esem[d.eng]
                        else:
                            tgt = self.dsems[d.sem][0]
                        key = id(tgt)
                        if seen[eng].get(key, -1) < d.val:
                            E_.wait_ge(tgt, d.val)
                            seen[eng][key] = d.val
                    instr = fn(E_)
                    if dsem is not None:
                        instr.then_inc(self.dsems[dsem][0], 16)
                    else:
                        instr.then_inc(self.esem[eng], 1)
            return body

        used = {s[0] for s in self.steps}
        for eng in ENGINES:
            if eng in used:
                getattr(block, eng)(emit_for(eng))

    def n_steps(self):
        return len(self.steps)



class Build:
    def __init__(self):
        self.ctx = contextlib.ExitStack()
        self.nc = bass.Bass("TRN2", target_bir_lowering=False)
        self.p = Prog(self.nc, self.ctx)
        self.consts = {}
        self.const_ap = {}
        self.phase_ctx = None
        self.com = {}

    def dram_in(self, name, shape, dtype=F32):
        return self.nc.dram_tensor(name, list(shape), dtype, kind="ExternalInput").ap()

    def dram_out(self, name, shape, dtype=F32):
        return self.nc.dram_tensor(name, list(shape), dtype, kind="ExternalOutput").ap()

    def dram_tmp(self, name, shape, dtype=F32):
        return self.nc.dram_tensor(name, list(shape), dtype, kind="Internal").ap()

    def sb(self, name, shape, dtype):
        stk = self.phase_ctx if self.phase_ctx is not None else self.ctx
        return stk.enter_context(self.nc.sbuf_tensor(name, list(shape), dtype))

    def sbp(self, name, shape, dtype):
        return self.ctx.enter_context(
            self.nc.sbuf_tensor(name, list(shape), dtype, side="right"))

    def ps(self, name, shape, dtype=F32):
        stk = self.phase_ctx if self.phase_ctx is not None else self.ctx
        return stk.enter_context(self.nc.psum_tensor(name, list(shape), dtype))

    def const(self, name, arr):
        if name not in self.const_ap:
            self.consts["cst_" + name] = np.ascontiguousarray(arr)
            dt = {"float32": F32, "bfloat16": BF16}[str(arr.dtype)]
            self.const_ap[name] = self.dram_in("cst_" + name, arr.shape, dt)
        return self.const_ap[name]

    def const_tile(self, name, arr):
        key = "ct_" + name
        if key not in self.com:
            ap = self.const(name, arr)
            dt = {"float32": F32, "bfloat16": BF16}[str(arr.dtype)]
            sbt = self.ctx.enter_context(
                self.nc.sbuf_tensor(key, list(arr.shape), dt, side="right"))
            tok = self.p.dma(SYNC, "cst",
                             lambda e, ap=ap, sbt=sbt: e.dma_start(out=sbt[:], in_=ap))
            self.p.guard = list(self.p.guard) + [tok]
            self.com[key] = sbt
        return self.com[key]

    @contextlib.contextmanager
    def phase(self, name):
        assert self.phase_ctx is None
        self.phase_ctx = contextlib.ExitStack()
        try:
            yield
        finally:
            self.phase_ctx.close()
            self.phase_ctx = None
            self.p.set_guard()


def bf16c(b):
    import ml_dtypes
    return ml_dtypes.bfloat16



def emit_ln(b, xsrc, xln, pfx):
    p = b.p
    bf = bf16c(b)
    ones_b = b.const_tile("ones128_b", np.ones((128, 1), bf))
    ones_f = b.const_tile("ones128_f", np.ones((1, 128), np.float32))
    assert b.phase_ctx is None, "emit_ln manages its own phases"
    outer = contextlib.ExitStack()
    r1 = outer.enter_context(b.nc.sbuf_tensor(pfx + "r1", [1, 2048], F32))
    r2 = outer.enter_context(b.nc.sbuf_tensor(pfx + "r2", [1, 2048], F32))
    b.phase_ctx = contextlib.ExitStack()
    stats = b.ps(pfx + "st", [128, 4096], F32)
    xsq = b.sb(pfx + "xsq", [128, 1024], BF16)

    ld = [p.dma(POOL, pfx + "x%d" % (c % 2),
                lambda e, c=c: e.dma_start(out=xln[:, c * T:(c + 1) * T],
                                           in_=xsrc[c * 128:(c + 1) * 128, :]))
          for c in range(NCH)]

    NNB = T // 512
    sq, mmt = {}, []
    for c in range(NCH):
        for nb in range(NNB):
            i = c * NNB + nb
            deps = [ld[c]] + ([mmt[i - 2]] if i >= 2 else [])
            sq[i] = p.do(SCAL, lambda e, c=c, nb=nb, i=i: e.activation(
                out=xsq[:, (i % 2) * 512:(i % 2) * 512 + 512],
                in_=xln[:, c * T + nb * 512:c * T + nb * 512 + 512],
                func=AF.Square), after=deps)
            def mm(e, c=c, nb=nb, i=i):
                e.matmul(stats[0:1, nb * 512:nb * 512 + 512], ones_b[:],
                         xln[:, c * T + nb * 512:c * T + nb * 512 + 512],
                         start=(c == 0), stop=(c == NCH - 1))
                return e.matmul(stats[0:1, 2048 + nb * 512:2048 + nb * 512 + 512], ones_b[:],
                                xsq[:, (i % 2) * 512:(i % 2) * 512 + 512],
                                start=(c == 0), stop=(c == NCH - 1))
            mmt.append(p.do(TENS, mm, after=[ld[c], sq[i]]))

    t0 = None
    for base, dstr in ((0, r1), (2048, r2)):
        for q in range(T // 512):
            t0 = p.do(VEC, lambda e, q=q, base=base, dstr=dstr: e.tensor_copy(
                out=dstr[0:1, q * 512:(q + 1) * 512],
                in_=stats[0:1, base + q * 512:base + (q + 1) * 512]),
                after=[mmt[-1]])
    t1 = p.do(VEC, lambda e: e.tensor_scalar(out=r1[0:1, 0:T], in0=r1[0:1, 0:T],
                                             scalar1=1.0 / D, scalar2=None, op0=ALU.mult),
              after=[t0])
    t2 = p.do(VEC, lambda e: e.tensor_scalar(out=r2[0:1, 0:T], in0=r2[0:1, 0:T],
                                             scalar1=1.0 / D, scalar2=None, op0=ALU.mult),
              after=[t0])
    t3 = p.do(VEC, lambda e: e.tensor_tensor(out=stats[0:1, 0:T], in0=r1[0:1, 0:T],
                                             in1=r1[0:1, 0:T], op=ALU.mult), after=[t1])
    t4 = p.do(VEC, lambda e: e.tensor_tensor(out=r2[0:1, 0:T], in0=r2[0:1, 0:T],
                                             in1=stats[0:1, 0:T], op=ALU.subtract),
              after=[t2, t3])
    t4b = p.do(VEC, lambda e: e.tensor_scalar(out=r2[0:1, 0:T], in0=r2[0:1, 0:T],
                                              scalar1=1e-5, scalar2=None, op0=ALU.add),
               after=[t4])
    t5 = p.do(SCAL, lambda e: e.activation(out=stats[0:1, 0:T], in_=r2[0:1, 0:T],
                                           func=AF.Sqrt), after=[t4b])
    t6 = p.do(VEC, lambda e: e.reciprocal(out=r2[0:1, 0:T], in_=stats[0:1, 0:T]),
              after=[t5])
    t7 = p.do(VEC, lambda e: e.tensor_tensor(out=r1[0:1, 0:T], in0=r1[0:1, 0:T],
                                             in1=r2[0:1, 0:T], op=ALU.mult), after=[t6])
    t8 = p.do(VEC, lambda e: e.tensor_scalar(out=r1[0:1, 0:T], in0=r1[0:1, 0:T],
                                             scalar1=-1.0, scalar2=None, op0=ALU.mult),
              after=[t7])
    t9 = t8
    b.phase_ctx.close(); b.phase_ctx = None; p.set_guard()
    b.phase_ctx = contextlib.ExitStack()
    bcAB = b.ps(pfx + "bc", [128, 4096], F32)

    bc = []
    for nb in range(T // 512):
        def mm(e, nb=nb):
            e.matmul(bcAB[:, nb * 512:nb * 512 + 512], ones_f[:],
                     r2[0:1, nb * 512:nb * 512 + 512], start=True, stop=True)
            return e.matmul(bcAB[:, 2048 + nb * 512:2048 + nb * 512 + 512], ones_f[:],
                            r1[0:1, nb * 512:nb * 512 + 512],
                            start=True, stop=True)
        bc.append(p.do(TENS, mm, after=[t8, t9]))

    out_toks = []
    for c in range(NCH):
        for nb in range(T // 512):
            sl = slice(c * T + nb * 512, c * T + nb * 512 + 512)
            a = p.do(VEC, lambda e, sl=sl, nb=nb: e.tensor_tensor(
                out=xln[:, sl], in0=xln[:, sl],
                in1=bcAB[:, nb * 512:nb * 512 + 512], op=ALU.mult), after=[bc[nb], mmt[-1]])
            out_toks.append(p.do(VEC, lambda e, sl=sl, nb=nb: e.tensor_tensor(
                out=xln[:, sl], in0=xln[:, sl],
                in1=bcAB[:, 2048 + nb * 512:2048 + nb * 512 + 512], op=ALU.add), after=[a]))
    b.phase_ctx.close(); b.phase_ctx = None
    outer.close()
    p.set_guard()
    return out_toks



def emit_ffn(b, xsrc, dst, W, pfx):
    p = b.p
    bf = bf16c(b)
    ffn_outer = contextlib.ExitStack()
    cT = ffn_outer.enter_context(b.nc.sbuf_tensor(pfx + "cT", [16, 2048], F32))
    if "xln" not in b.com:
        b.com["xln"] = b.sbp("g_xln", [128, NCH * T], BF16)
    xln = b.com["xln"]
    ln_done = emit_ln(b, xsrc, xln, pfx + "l")

    esel_np = np.zeros((16, 16 * 128), np.float32)
    for e_ in range(16):
        esel_np[e_, e_ * 128:(e_ + 1) * 128] = 1.0
    with b.phase(pfx + "gate"):
        gwT = b.sb(pfx + "gw", [128, NCH * 16], BF16)
        ldg = p.dma(POOL, pfx + "gw", lambda e: e.dma_start(out=gwT[:], in_=W["f_gate"]))
        logit = b.ps(pfx + "lg", [128, 2048], F32)
        cps = b.ps(pfx + "cps", [128, 1024], F32)
        comb = b.sb(pfx + "cmb", [128, NTT * 16], F32)
        srow = b.sb(pfx + "srw", [128, 48], F32)
        mrow = b.sb(pfx + "mrw", [128, 2], F32)
        identf = b.const_tile("identf", np.eye(128, dtype=np.float32))

        slot_free = {}
        comb_done = {}
        for tt in range(NTT):
            deps = [ldg] + ([slot_free[tt - 4]] if tt >= 4 else [])
            def mm(e, tt=tt):
                for kc in range(NCH):
                    last = e.matmul(logit[:, (tt % 4) * 512:(tt % 4) * 512 + 16],
                                    xln[:, kc * T + tt * 128:kc * T + tt * 128 + 128],
                                    gwT[:, kc * 16:(kc + 1) * 16],
                                    start=(kc == 0), stop=(kc == NCH - 1))
                return last
            g0 = p.do(TENS, mm, after=deps)
            sl = logit[:, (tt % 4) * 512:(tt % 4) * 512 + 16]
            a0 = p.do(VEC, lambda e, sl=sl: e.reduce_max(out=mrow[:, 0:1], in_=sl, axis=AX.X),
                      after=[g0])
            a1 = p.do(VEC, lambda e: e.tensor_scalar(out=mrow[:, 1:2], in0=mrow[:, 0:1],
                                                     scalar1=-1.0, scalar2=None, op0=ALU.mult),
                      after=[a0])
            a2 = p.do(SCAL, lambda e, sl=sl: e.activation(out=srow[:, 0:16], in_=sl,
                                                          func=AF.Exp, bias=mrow[:, 1:2]),
                      after=[a1])
            slot_free[tt] = a2
            a3 = p.do(VEC, lambda e: e.reduce_sum(out=mrow[:, 0:1], in_=srow[:, 0:16],
                                                  axis=AX.X), after=[a2])
            a4 = p.do(VEC, lambda e: e.reciprocal(out=mrow[:, 1:2], in_=mrow[:, 0:1]),
                      after=[a3])
            cur = p.do(VEC, lambda e: e.tensor_scalar(out=srow[:, 0:16], in0=srow[:, 0:16],
                                                      scalar1=mrow[:, 1:2], scalar2=None,
                                                      op0=ALU.mult), after=[a4])
            cur = p.do(VEC, lambda e: e.tensor_copy(out=srow[:, 16:32], in_=srow[:, 0:16]),
                       after=[cur])
            for it in range(TOPK - 1):
                m0 = p.do(VEC, lambda e: e.reduce_max(out=mrow[:, 0:1], in_=srow[:, 16:32],
                                                      axis=AX.X), after=[cur])
                m1 = p.do(VEC, lambda e: e.tensor_scalar(out=srow[:, 32:48],
                                                         in0=srow[:, 16:32],
                                                         scalar1=mrow[:, 0:1], scalar2=None,
                                                         op0=ALU.is_lt), after=[m0])
                cur = p.do(VEC, lambda e: e.tensor_tensor(out=srow[:, 16:32],
                                                          in0=srow[:, 16:32],
                                                          in1=srow[:, 32:48], op=ALU.mult),
                           after=[m1])
            m0 = p.do(VEC, lambda e: e.reduce_max(out=mrow[:, 0:1], in_=srow[:, 16:32],
                                                  axis=AX.X), after=[cur])
            m1 = p.do(VEC, lambda e: e.tensor_scalar(out=srow[:, 32:48], in0=srow[:, 0:16],
                                                     scalar1=mrow[:, 0:1], scalar2=None,
                                                     op0=ALU.is_ge), after=[m0])
            m2 = p.do(VEC, lambda e: e.tensor_tensor(out=srow[:, 16:32], in0=srow[:, 0:16],
                                                     in1=srow[:, 32:48], op=ALU.mult),
                      after=[m1])
            m3 = p.do(VEC, lambda e: e.reduce_sum(out=mrow[:, 0:1], in_=srow[:, 16:32],
                                                  axis=AX.X), after=[m2])
            m4 = p.do(VEC, lambda e: e.tensor_scalar(out=mrow[:, 0:1], in0=mrow[:, 0:1],
                                                     scalar1=1e-20, scalar2=None, op0=ALU.add),
                      after=[m3])
            m5 = p.do(VEC, lambda e: e.reciprocal(out=mrow[:, 1:2], in_=mrow[:, 0:1]),
                      after=[m4])
            comb_done[tt] = p.do(VEC, lambda e, tt=tt: e.tensor_scalar(
                out=comb[:, tt * 16:(tt + 1) * 16], in0=srow[:, 16:32],
                scalar1=mrow[:, 1:2], scalar2=None, op0=ALU.mult), after=[m5])

        prev_par = {}
        for tt in range(NTT):
            def tr(e, tt=tt):
                return e.transpose(cps[0:16, (tt % 2) * 512:(tt % 2) * 512 + 128],
                                   comb[:, tt * 16:(tt + 1) * 16], identf[:])
            trt = p.do(TENS, tr, after=[comb_done[tt], prev_par.get(tt % 2)])
            prev_par[tt % 2] = p.do(VEC, lambda e, tt=tt: e.tensor_copy(
                out=cT[:, tt * 128:(tt + 1) * 128],
                in_=cps[0:16, (tt % 2) * 512:(tt % 2) * 512 + 128]), after=[trt])

    with b.phase(pfx + "ex"):
        acc = b.sb(pfx + "acc", [128, NCH * TBF], F32)
        nsil = max(11 * TBF, 22 * 512)
        sil = b.sb(pfx + "sil", [128, 2 * TBF], BF16)
        hbuf = b.sb(pfx + "h", [128, nsil], BF16)
        w13b = b.sb(pfx + "w13", [128, 2 * 2048], BF16)
        w2b = b.sb(pfx + "w2", [128, 2 * SI], BF16)
        xrt = b.sb(pfx + "xr", [128, TBF], F32)
        mps = b.ps(pfx + "mps", [128, 3072], F32)
        cbps = b.ps(pfx + "cb", [128, 1024], F32)
        eselt = b.const_tile("esel", esel_np)

        g = {
            "w13_user": {0: None, 1: None},
            "w2_user": {0: None, 1: None},
            "slot": {0: None, 1: None, 2: None},
            "cb_readers": [],
            "sil_guard": [],
            "h_guard": None,
            "acc_writer": {m: None for m in range(NCH)},
            "acc_store": {m: None for m in range(NCH)},
            "xrt": {0: None, 1: None},
        }

        def expert_pass(tb, kind, e_idx):
            if kind == "e":
                w13_src, w2_src = W["f_w13"][e_idx], W["f_w2"][e_idx]
                nm13, nk13, nm2, nk2, ni = 22, NCH, 16, 11, 11
                toff, tlen = tb * TBF, TBF
            else:
                w13_src, w2_src = W["f_sw13"], W["f_sw2"]
                nm13, nk13, nm2, nk2, ni = 44, NCH, 16, 22, 22
                toff, tlen = tb * TBF + e_idx * 512, 512
            nnb = tlen // 512
            acc_off = toff - tb * TBF

            cb_tok = None
            if kind == "e":
                def cbmm(e):
                    for nb in range(nnb):
                        last = e.matmul(cbps[:, nb * 512:nb * 512 + 512],
                                        eselt[:, e_idx * 128:(e_idx + 1) * 128],
                                        cT[:, toff + nb * 512:toff + nb * 512 + 512],
                                        start=True, stop=True)
                    return last
                cb_tok = p.do(TENS, cbmm, after=g["cb_readers"])
                g["cb_readers"] = []

            mm13, ev13 = {}, {}
            sil_consumers = []
            m_order = [x for i in range(ni) for x in (i, ni + i)]
            for mi, m in enumerate(m_order):
                par, slot = mi % 2, mi % 3
                wt = p.dma(POOL, pfx + "w13p%d" % par,
                           lambda e, m=m, par=par: e.dma_start(
                               out=w13b[:, par * 2048:par * 2048 + nk13 * 128],
                               in_=w13_src[m]),
                           after=[g["w13_user"][par]])
                def mmf(e, m=m, par=par, slot=slot, nnb=nnb, nk13=nk13, toff=toff):
                    for nb in range(nnb):
                        for kc in range(nk13):
                            last = e.matmul(
                                mps[:, slot * 1024 + nb * 512:slot * 1024 + nb * 512 + 512],
                                w13b[:, par * 2048 + kc * 128:par * 2048 + kc * 128 + 128],
                                xln[:, kc * T + toff + nb * 512:
                                     kc * T + toff + nb * 512 + 512],
                                start=(kc == 0), stop=(kc == nk13 - 1))
                    return last
                mm13[m] = p.do(TENS, mmf, after=[wt, g["slot"][slot]])
                g["w13_user"][par] = mm13[m]
                if m < ni:
                    deps = [mm13[m]] + ([ev13[ni + m - 2]] if m >= 2 else g["sil_guard"])
                    for q in range(tlen // 512):
                        ev13[m] = p.do(SCAL, lambda e, m=m, slot=slot, tlen=tlen, q=q:
                            e.activation(
                                out=sil[:, (m % 2) * tlen + q * 512:
                                         (m % 2) * tlen + q * 512 + 512],
                                in_=mps[:, slot * 1024 + q * 512:slot * 1024 + q * 512 + 512],
                                func=AF.Silu),
                            after=deps)
                else:
                    gm = m - ni
                    deps = [mm13[m], ev13[gm], g["h_guard"]]
                    t1 = None
                    for q in range(tlen // 512):
                        t1 = p.do(VEC, lambda e, gm=gm, slot=slot, tlen=tlen, q=q:
                            e.tensor_tensor(
                                out=hbuf[:, gm * tlen + q * 512:gm * tlen + q * 512 + 512],
                                in0=sil[:, (gm % 2) * tlen + q * 512:
                                        (gm % 2) * tlen + q * 512 + 512],
                                in1=mps[:, slot * 1024 + q * 512:slot * 1024 + q * 512 + 512],
                                op=ALU.mult),
                            after=deps + ([t1] if t1 else []))
                        if kind == "e":
                            t1 = p.do(VEC, lambda e, gm=gm, tlen=tlen, q=q: e.tensor_tensor(
                                out=hbuf[:, gm * tlen + q * 512:gm * tlen + q * 512 + 512],
                                in0=hbuf[:, gm * tlen + q * 512:gm * tlen + q * 512 + 512],
                                in1=cbps[:, q * 512:q * 512 + 512], op=ALU.mult),
                                after=[t1, cb_tok])
                            g["cb_readers"].append(t1)
                    ev13[m] = t1
                    sil_consumers.append(t1)
                g["slot"][slot] = ev13[m]

            h_all = [ev13[ni + i] for i in range(ni)]
            mm2, ev2 = {}, {}
            for m in range(nm2):
                par, slot = m % 2, m % 3
                wt = p.dma(POOL, pfx + "w2p%d" % par,
                           lambda e, m=m, par=par, nk2=nk2: e.dma_start(
                               out=w2b[:, par * SI:par * SI + nk2 * 128],
                               in_=w2_src[m]),
                           after=[g["w2_user"][par]])
                deps = [wt, g["slot"][slot]] + (h_all if m == 0 else [])
                def mmf(e, m=m, par=par, slot=slot, nnb=nnb, nk2=nk2, tlen=tlen):
                    for nb in range(nnb):
                        for kc in range(nk2):
                            last = e.matmul(
                                mps[:, slot * 1024 + nb * 512:slot * 1024 + nb * 512 + 512],
                                w2b[:, par * SI + kc * 128:par * SI + kc * 128 + 128],
                                hbuf[:, kc * tlen + nb * 512:kc * tlen + nb * 512 + 512],
                                start=(kc == 0), stop=(kc == nk2 - 1))
                    return last
                mm2[m] = p.do(TENS, mmf, after=deps)
                g["w2_user"][par] = mm2[m]
                first = (kind == "e" and e_idx == 0)
                evt = None
                for q in range(tlen // 512):
                    def evf(e, m=m, slot=slot, first=first, acc_off=acc_off, q=q):
                        dst_sl = acc[:, m * TBF + acc_off + q * 512:
                                     m * TBF + acc_off + q * 512 + 512]
                        src_sl = mps[:, slot * 1024 + q * 512:slot * 1024 + q * 512 + 512]
                        if first:
                            return e.tensor_copy(out=dst_sl, in_=src_sl)
                        return e.tensor_tensor(out=dst_sl, in0=dst_sl, in1=src_sl,
                                               op=ALU.add)
                    evt = p.do(VEC, evf,
                               after=[mm2[m], g["acc_writer"][m], g["acc_store"][m]] +
                                     ([evt] if evt else []))
                ev2[m] = evt
                g["acc_writer"][m] = ev2[m]
                g["acc_store"][m] = None
                g["slot"][slot] = ev2[m]
            g["sil_guard"] = sil_consumers
            g["h_guard"] = mm2[nm2 - 1]

        for tb in range(NBF):
            for e_idx in range(E):
                expert_pass(tb, "e", e_idx)
            for sb_i in range(TBF // 512):
                expert_pass(tb, "s", sb_i)
            for c in range(NCH):
                par = c % 2
                lt = p.dma(SYNC, pfx + "rx",
                           lambda e, c=c, tb=tb: e.dma_start(
                               out=xrt[:, :],
                               in_=xsrc[c * 128:(c + 1) * 128, tb * TBF:(tb + 1) * TBF]),
                           after=[g["xrt"][0]])
                adt = None
                for q in range(TBF // 512):
                    adt = p.do(VEC, lambda e, c=c, q=q: e.tensor_tensor(
                        out=acc[:, c * TBF + q * 512:c * TBF + q * 512 + 512],
                        in0=acc[:, c * TBF + q * 512:c * TBF + q * 512 + 512],
                        in1=xrt[:, q * 512:q * 512 + 512], op=ALU.add),
                        after=[lt, g["acc_writer"][c]] + ([adt] if adt else []))
                ad = adt
                g["xrt"][0] = ad
                st = p.dma(SYNC, pfx + "sx%d" % par,
                           lambda e, c=c, tb=tb: e.dma_start(
                               out=dst[c * 128:(c + 1) * 128, tb * TBF:(tb + 1) * TBF],
                               in_=acc[:, c * TBF:(c + 1) * TBF]), after=[ad])
                g["acc_writer"][c] = None
                g["acc_store"][c] = st



def prep_ffn_weights(I):
    f32 = np.float32
    w13 = np.asarray(I["L0_ffn_w13"], f32)
    w2 = np.asarray(I["L0_ffn_w2"], f32)
    return {
        "f_w13": np.stack([_relayout(np.ascontiguousarray(w13[e].T)) for e in range(E)]),
        "f_w2": np.stack([_relayout(np.ascontiguousarray(w2[e].T)) for e in range(E)]),
        "f_sw13": _relayout(np.ascontiguousarray(np.asarray(I["L0_ffn_sw13"], f32).T)),
        "f_sw2": _relayout(np.ascontiguousarray(np.asarray(I["L0_ffn_sw2"], f32).T)),
        "f_gate": np.ascontiguousarray(np.asarray(I["L0_ffn_gate_w"], f32).T
                                       .reshape(NCH, 128, 16).transpose(1, 0, 2)
                                       .reshape(128, NCH * 16)),
    }


def declare_ffn_weights(b, wnp):
    return {k: b.dram_in(k, v.shape, F32) for k, v in wnp.items()}


def build_ffn_only():
    b = Build()
    xin = b.dram_in("x_in", [D, T], F32)
    out = b.dram_out("x_out", [D, T], F32)
    wshapes = {
        "f_w13": (E, 22, 128, 2048), "f_w2": (E, 16, 128, FI),
        "f_sw13": (44, 128, 2048), "f_sw2": (16, 128, SI), "f_gate": (128, NCH * 16),
    }
    W = {k: b.dram_in(k, list(s), F32) for k, s in wshapes.items()}
    emit_ffn(b, xin, out, W, "f1")
    with b.nc.Block() as blk:
        b.p.emit(blk)
        @blk.sync
        def _(eng):
            for name, rec in b.p.dsems.items():
                eng.wait_ge(rec[0], rec[1])
    return b


def run_ffn_only(x_fm, wnp, cores=1):
    b = build_ffn_only()
    inmap = {"x_in": x_fm, **{k: wnp[k] for k in
                              ["f_w13", "f_w2", "f_sw13", "f_sw2", "f_gate"]},
             **b.consts}
    nc_obj = b.nc
    res = run_bass_kernel_spmd(nc_obj, [inmap] * cores, core_ids=list(range(cores)))
    b.ctx.close()
    return res.results[0]["x_out"]



WSHAPES = {
    "f_w13": (E, 22, 128, 2048), "f_w2": (E, 16, 128, FI),
    "f_sw13": (44, 128, 2048), "f_sw2": (16, 128, SI), "f_gate": (128, NCH * 16),
    "b_wqa": (4, 128, 2048), "b_wkva": (5, 128, 2048),
    "b_wqb": (16, 128, 768), "b_wkvb": (16, 128, 1024),
    "b_wo": (16, 128, 2048), "b_bqa": (128, 4), "b_bkva": (128, 5),
    "b_bqbn": (128, 16), "b_bqbp": (64, 16), "b_bkbn": (128, 16),
    "b_bkbv": (128, 16), "b_bwo": (128, 16),
    "a_wr": (16, 128, 2048), "a_wk": (16, 128, 2048), "a_wv": (16, 128, 2048),
    "a_wo": (16, 128, 2048), "a_w1": (128, 1024), "a_a1": (128, 1024),
    "a_g1": (128, 2048), "a_w2": (64, 2048), "a_a2": (64, 2048),
    "a_g2": (128, 2048), "a_vecs": (NCH, 128, 13),
}


def build_full():
    b = Build()
    xin = b.dram_in("x_in", [D, T], F32)
    out = b.dram_out("x_out", [D, T], F32)
    x1 = b.dram_tmp("x1", [D, T], F32)
    x2 = b.dram_tmp("x2", [D, T], F32)
    x3 = b.dram_tmp("x3", [D, T], F32)
    W = {k: b.dram_in(k, list(sh), F32) for k, sh in WSHAPES.items()}
    emit_attn_a(b, xin, x1, W, "r1")
    emit_ffn(b, x1, x2, W, "f1")
    emit_attn_b(b, x2, x3, W, "a1")
    emit_ffn(b, x3, out, W, "f2")
    with b.nc.Block() as blk:
        b.p.emit(blk)
        @blk.sync
        def _(eng):
            for name, rec in b.p.dsems.items():
                eng.wait_ge(rec[0], rec[1])
    return b


def prep_all(I):
    w = {}
    w.update(prep_ffn_weights(I))
    w.update(prep_attnb_weights(I))
    w.update(prep_attna_weights(I))
    return w


def kernel(**inputs):
    x = np.asarray(inputs["L0_attn_a_x"], np.float32)
    wnp = prep_all(inputs)
    b = build_full()
    base = {**wnp, **b.consts}
    in_maps = []
    for c in range(NCORES):
        m = dict(base)
        m["x_in"] = np.ascontiguousarray(x[c].T)
        in_maps.append(m)
    res = run_bass_kernel_spmd(b.nc, in_maps, core_ids=list(range(NCORES)))
    out = np.stack([np.ascontiguousarray(res.results[c]["x_out"].T)
                    for c in range(NCORES)])
    return out.astype(np.float32)



def prep_attnb_weights(I):
    f32 = np.float32
    wq_a = np.asarray(I["L0_attn_b_wq_a_w"], f32)
    wq_b = np.asarray(I["L0_attn_b_wq_b_w"], f32)
    wkv_a = _pad(np.asarray(I["L0_attn_b_wkv_a_w"], f32), m=640)
    wkv_b = np.asarray(I["L0_attn_b_wkv_b_w"], f32)
    wo = np.asarray(I["L0_attn_b_wo_w"], f32)
    out = {
        "b_wqa": _relayout(wq_a),
        "b_wkva": _relayout(wkv_a),
        "b_wqb": np.ascontiguousarray(
            wq_b.reshape(4, 128, 16, 192).transpose(2, 1, 0, 3).reshape(16, 128, 768)),
        "b_wkvb": np.ascontiguousarray(
            wkv_b.reshape(4, 128, 16, 256).transpose(2, 1, 0, 3).reshape(16, 128, 1024)),
        "b_wo": _relayout(wo),
        "b_bqa": np.ascontiguousarray(
            np.asarray(I["L0_attn_b_wq_a_b"], f32).reshape(4, 128).T),
        "b_bkva": np.ascontiguousarray(
            _pad(np.asarray(I["L0_attn_b_wkv_a_b"], f32)[None], m=640)[0]
            .reshape(5, 128).T),
        "b_bqbn": np.ascontiguousarray(
            np.asarray(I["L0_attn_b_wq_b_b"], f32).reshape(16, 192)[:, :128].T),
        "b_bqbp": np.ascontiguousarray(
            np.asarray(I["L0_attn_b_wq_b_b"], f32).reshape(16, 192)[:, 128:].T),
        "b_bkbn": np.ascontiguousarray(
            np.asarray(I["L0_attn_b_wkv_b_b"], f32).reshape(16, 256)[:, :128].T),
        "b_bkbv": np.ascontiguousarray(
            np.asarray(I["L0_attn_b_wkv_b_b"], f32).reshape(16, 256)[:, 128:].T),
        "b_bwo": np.ascontiguousarray(
            np.asarray(I["L0_attn_b_wo_b"], f32).reshape(16, 128).T),
    }
    return out


def rope_tables_np(Tn):
    inv = 1.0 / 1e6 ** (np.arange(0, ROPE, 2, dtype=np.float64) / ROPE)
    t = np.arange(Tn, dtype=np.float64)
    fr = np.outer(t, inv)
    cos, sin = np.cos(fr), np.sin(fr)
    Cf = np.empty((ROPE, Tn)); Sf = np.empty((ROPE, Tn))
    Cf[0::2] = cos.T; Cf[1::2] = cos.T
    Sf[0::2] = -sin.T; Sf[1::2] = sin.T
    return Cf.astype(np.float32), Sf.astype(np.float32)


def emit_attn_b(b, xsrc, dst, W, pfx):
    p = b.p
    bf = bf16c(b)
    if "xln" not in b.com:
        b.com["xln"] = b.sbp("g_xln", [128, NCH * T], BF16)
    xln = b.com["xln"]
    ln_done = emit_ln(b, xsrc, xln, pfx + "l")
    ones_b = b.com["ct_ones128_b"]
    ones_f = b.com["ct_ones128_f"]

    with b.phase(pfx + "m"):
        Cf_np, Sf_np = rope_tables_np(T)
        swp = np.zeros((64, 64), bf)
        for i in range(32):
            swp[2 * i + 1, 2 * i] = 1.0
            swp[2 * i, 2 * i + 1] = 1.0
        Cf = b.sb(pfx + "Cf", [ROPE, T], BF16)
        Sf = b.sb(pfx + "Sf", [ROPE, T], BF16)
        swap64 = b.sb(pfx + "swp", [64, 64], BF16)
        c1 = p.dma(SYNC, pfx + "c", lambda e: e.dma_start(
            out=Cf[:], in_=b.const("ropeC", Cf_np.astype(bf))))
        c2 = p.dma(SYNC, pfx + "c", lambda e: e.dma_start(
            out=Sf[:], in_=b.const("ropeS", Sf_np.astype(bf))))
        c3 = p.dma(SYNC, pfx + "c", lambda e: e.dma_start(out=swap64[:], in_=b.const("swap64", swp)))
        bias_t = {}
        for nm, shp in (("b_bqa", [128, 4]), ("b_bkva", [128, 5]), ("b_bqbn", [128, 16]),
                        ("b_bqbp", [64, 16]), ("b_bkbn", [128, 16]), ("b_bkbv", [128, 16])):
            tl = b.sb(pfx + nm, shp, F32)
            bias_t[nm] = tl
            c3 = p.dma(SYNC, pfx + "c", lambda e, tl=tl, nm=nm: e.dma_start(out=tl[:], in_=W[nm]))
        cst = [c1, c2, c3]

        qa = b.sb(pfx + "qa", [128, 4 * T], BF16)
        ckv = b.sb(pfx + "ckv", [128, 4 * T], BF16)
        kpe = b.sb(pfx + "kpe", [65, T], BF16)
        rows3 = b.sb(pfx + "rows3", [33, 2 * T], BF16)
        kpe_sq = b.sb(pfx + "kpesq", [64, T], BF16)
        qh = [b.sb(pfx + "qh%d" % i, [128, T], BF16) for i in range(2)]
        qpe = [b.sb(pfx + "qpe%d" % i, [65, T], BF16) for i in range(2)]
        kn = [b.sb(pfx + "kn%d" % i, [128, T], BF16) for i in range(2)]
        Vtm = [b.sb(pfx + "V%d" % i, [128, 16 * 128], BF16) for i in range(2)]
        def pr_slice64(i):
            return pp[0:64, i * 512:(i + 1) * 512]
        sq = b.sb(pfx + "sq", [128, T], BF16)
        PT = b.sb(pfx + "PT", [128, 8 * 512], BF16)
        rown = rows3[32:33, 0:T]
        rowq = rows3[32:33, T:2 * T]
        b2t = b.sb(pfx + "b2", [33, 2], F32)
        one33 = b.const_tile("one33", np.ones((33, 1), bf16c(b)))
        dsb = b.sb(pfx + "dsb", [1, 512], F32)
        rbsb = b.sb(pfx + "rb", [128, 512], F32)
        wslab = b.sb(pfx + "ws", [128, 2 * 2048], BF16)
        hq = b.sb(pfx + "hq", [128, 2 * 768], BF16)
        hk = b.sb(pfx + "hk", [128, 2 * 1024], BF16)
        pp = b.ps(pfx + "pp", [128, 3072], F32)
        pt = b.ps(pfx + "pt", [128, 2048], BF16)

        def bank(i):
            return pp[:, i * 512:(i + 1) * 512]

        wuser = {0: None, 1: None}
        bank_guard = {i: None for i in range(7)}
        NB = T // 512

        def proj(dst_sb, src_slabs, nm_tiles, bias_tile, name):
            toks = []
            for m in range(nm_tiles):
                par = m % 2
                wt = p.dma(POOL, pfx + name + "%d" % par,
                           lambda e, m=m, par=par: e.dma_start(
                               out=wslab[:, par * 2048:(par + 1) * 2048],
                               in_=src_slabs[m]), after=[wuser[par]])
                def mmf(e, m=m, par=par):
                    for nb in range(NB):
                        for kc in range(NCH):
                            last = e.matmul(bank(nb % 4),
                                            wslab[:, par * 2048 + kc * 128:
                                                  par * 2048 + kc * 128 + 128],
                                            xln[:, kc * T + nb * 512:kc * T + nb * 512 + 512],
                                            start=(kc == 0), stop=(kc == NCH - 1))
                            if kc == NCH - 1 and nb < NB - 1:
                                pass
                    return last
                for nb in range(NB):
                    def mmnb(e, m=m, par=par, nb=nb):
                        for kc in range(NCH):
                            last = e.matmul(bank(nb % 4),
                                            wslab[:, par * 2048 + kc * 128:
                                                  par * 2048 + kc * 128 + 128],
                                            xln[:, kc * T + nb * 512:kc * T + nb * 512 + 512],
                                            start=(kc == 0), stop=(kc == NCH - 1))
                        return last
                    mt = p.do(TENS, mmnb, after=[wt, bank_guard[nb % 4]] + (ln_done if m == 0 and nb == 0 else []))
                    wuser[par] = mt
                    ev = p.do(SCAL, lambda e, m=m, nb=nb, dst_sb=dst_sb, bias_tile=bias_tile:
                              e.activation(out=dst_sb[:, m * T + nb * 512:m * T + nb * 512 + 512],
                                           in_=bank(nb % 4), func=AF.Identity,
                                           bias=bias_tile[:, m:m + 1]),
                              after=[mt] + cst)
                    bank_guard[nb % 4] = ev
                    toks.append(ev)
            return toks

        qa_done = proj(qa, W["b_wqa"], 4, bias_t["b_bqa"], "wqa")
        ckv_done = proj(ckv, W["b_wkva"], 4, bias_t["b_bkva"], "wkva")
        kpe_raw_done = []
        for nb in range(NB):
            par5 = 4 % 2
            wt5 = p.dma(POOL, pfx + "wkva%d" % par5,
                        lambda e: e.dma_start(out=wslab[:, par5 * 2048:(par5 + 1) * 2048],
                                              in_=W["b_wkva"][4]),
                        after=[wuser[par5]]) if nb == 0 else wt5
            bk5 = nb % 4
            mt5 = p.do(TENS, lambda e, nb=nb, bk5=bk5: [
                e.matmul(bank(bk5),
                         wslab[:, par5 * 2048 + kc * 128:par5 * 2048 + kc * 128 + 128],
                         xln[:, kc * T + nb * 512:kc * T + nb * 512 + 512],
                         start=(kc == 0), stop=(kc == NCH - 1))
                for kc in range(NCH)][-1], after=[wt5, bank_guard[bk5]])
            if nb == 0:
                wuser[par5] = mt5
            ev5 = p.do(SCAL, lambda e, nb=nb, bk5=bk5: e.activation(
                out=kpe[0:64, nb * 512:nb * 512 + 512],
                in_=pp[0:64, bk5 * 512:bk5 * 512 + 512], func=AF.Identity,
                bias=bias_t["b_bkva"][0:64, 4:5]), after=[mt5] + cst)
            bank_guard[bk5] = ev5
            kpe_raw_done.append(ev5)

        kp_raw = ckv
        rope_toks = []
        ms = p.do(VEC, lambda e: e.memset(kpe[64:65, :], -1.0), after=cst)
        rope_toks.append(ms)
        for nb in range(NB):
            sw = p.do(TENS, lambda e, nb=nb: e.matmul(
                pp[0:64, 2048:2048 + 512], swap64[:],
                kpe[0:64, nb * 512:nb * 512 + 512], start=True, stop=True),
                after=[kpe_raw_done[nb]] + cst + [rope_toks[-1], bank_guard[4]])
            v1 = p.do(VEC, lambda e, nb=nb: e.tensor_tensor(
                out=sq[0:64, nb * 512:nb * 512 + 512],
                in0=pp[0:64, 2048:2048 + 512],
                in1=Sf[:, nb * 512:nb * 512 + 512], op=ALU.mult), after=[sw] + cst)
            bank_guard[4] = v1
            v2 = p.do(VEC, lambda e, nb=nb: e.tensor_tensor(
                out=kpe[0:64, nb * 512:nb * 512 + 512],
                in0=kpe[0:64, nb * 512:nb * 512 + 512],
                in1=Cf[:, nb * 512:nb * 512 + 512], op=ALU.mult), after=[sw])
            v3 = p.do(VEC, lambda e, nb=nb: e.tensor_tensor(
                out=kpe[0:64, nb * 512:nb * 512 + 512],
                in0=kpe[0:64, nb * 512:nb * 512 + 512],
                in1=sq[0:64, nb * 512:nb * 512 + 512], op=ALU.add), after=[v1, v2])
            rope_toks.append(v3)

        kq_toks = []
        for nb in range(NB):
            kq_toks.append(p.do(SCAL, lambda e, nb=nb: e.activation(
                out=kpe_sq[0:64, nb * 512:nb * 512 + 512],
                in_=kpe[0:64, nb * 512:nb * 512 + 512], func=AF.Square),
                after=rope_toks))

        identb = b.const_tile("identb", np.eye(128, dtype=bf))
        tok_proj_last = Tok(TENS, p.ecnt[TENS])
        ptg = {0: None, 1: None}
        hquser = {0: None, 1: None}
        hkuser = {0: None, 1: None}

        abk = [0]
        def a_bank():
            i = 4 + abk[0] % 2
            abk[0] += 1
            return i
        bank_guard47 = bank_guard

        st_A = {}
        bdone = {0: None, 1: None}
        bdone_s = {0: None, 1: None}
        bdone_v = {0: None, 1: None}
        a_prev = {"sq": None, "rows": None, "vT": None}

        def stageA(h):
            par = h % 2
            qh_, qpe_, kn_, Vtm_ = qh[par], qpe[par], kn[par], Vtm[par]
            wq_t = p.dma(POOL, pfx + "hq%d" % par,
                         lambda e, h=h, par=par: e.dma_start(
                             out=hq[:, par * 768:(par + 1) * 768], in_=W["b_wqb"][h]),
                         after=[hquser[par]])
            wk_t = p.dma(POOL, pfx + "hk%d" % par,
                         lambda e, h=h, par=par: e.dma_start(
                             out=hk[:, par * 1024:(par + 1) * 1024], in_=W["b_wkvb"][h]),
                         after=[hkuser[par]])
            qn_ev, qp_ev, kn_ev, v_ev = [], [], [], []
            for nb in range(NB):
                bk = a_bank()
                mt = p.do(TENS, lambda e, par=par, nb=nb, bk=bk: [
                    e.matmul(bank(bk),
                             hq[:, par * 768 + kc * 192:par * 768 + kc * 192 + 128],
                             qa[:, kc * T + nb * 512:kc * T + nb * 512 + 512],
                             start=(kc == 0), stop=(kc == 3))
                    for kc in range(4)][-1], after=[wq_t, bank_guard47[bk]])
                ev = p.do(SCAL, lambda e, h=h, nb=nb, bk=bk, qh_=qh_: e.activation(
                    out=qh_[:, nb * 512:nb * 512 + 512], in_=bank(bk), func=AF.Identity,
                    bias=bias_t["b_bqbn"][:, h:h + 1]),
                    after=[mt, bdone[par], bdone_s[par]])
                bank_guard47[bk] = ev
                qn_ev.append(ev)
                bk2 = a_bank()
                mt2 = p.do(TENS, lambda e, par=par, nb=nb, bk2=bk2: [
                    e.matmul(pr_slice64(bk2),
                             hq[:, par * 768 + kc * 192 + 128:par * 768 + kc * 192 + 192],
                             qa[:, kc * T + nb * 512:kc * T + nb * 512 + 512],
                             start=(kc == 0), stop=(kc == 3))
                    for kc in range(4)][-1], after=[wq_t, bank_guard47[bk2]])
                ev2 = p.do(SCAL, lambda e, h=h, nb=nb, bk2=bk2, qpe_=qpe_: e.activation(
                    out=qpe_[0:64, nb * 512:nb * 512 + 512],
                    in_=pr_slice64(bk2), func=AF.Identity,
                    bias=bias_t["b_bqbp"][:, h:h + 1]),
                    after=[mt2, bdone[par], bdone_s[par], bdone_v[par]])
                bank_guard47[bk2] = ev2
                qp_ev.append(ev2)
            qn2_ev = []
            for nb in range(NB):
                s1 = p.do(SCAL, lambda e, nb=nb, qh_=qh_: e.activation(
                    out=sq[:, nb * 512:nb * 512 + 512], in_=qh_[:, nb * 512:nb * 512 + 512],
                    func=AF.Square), after=[qn_ev[nb], a_prev["sq"]])
                bk = a_bank()
                mta = p.do(TENS, lambda e, nb=nb, bk=bk: e.matmul(
                    pp[32:33, bk * 512:bk * 512 + 512], ones_b[:],
                    sq[:, nb * 512:nb * 512 + 512], start=True, stop=False),
                    after=[s1, bank_guard47[bk]])
                s2 = p.do(SCAL, lambda e, nb=nb, qpe_=qpe_: e.activation(
                    out=sq[0:64, nb * 512:nb * 512 + 512],
                    in_=qpe_[0:64, nb * 512:nb * 512 + 512],
                    func=AF.Square), after=[qp_ev[nb], mta])
                mt = p.do(TENS, lambda e, nb=nb, bk=bk: e.matmul(
                    pp[32:33, bk * 512:bk * 512 + 512], ones_b[0:64, :],
                    sq[0:64, nb * 512:nb * 512 + 512], start=False, stop=True),
                    after=[s2])
                ev = p.do(VEC, lambda e, nb=nb, bk=bk: e.tensor_copy(
                    out=rowq[0:1, nb * 512:nb * 512 + 512],
                    in_=pp[32:33, bk * 512:bk * 512 + 512]),
                    after=[mt, a_prev["rows"]])
                bank_guard47[bk] = ev
                qn2_ev.append(ev)
            rp_toks = []
            for nb in range(NB):
                bk = a_bank()
                sw = p.do(TENS, lambda e, nb=nb, bk=bk, qpe_=qpe_: e.matmul(
                    pr_slice64(bk), swap64[:],
                    qpe_[0:64, nb * 512:nb * 512 + 512], start=True, stop=True),
                    after=[qp_ev[nb], qn2_ev[nb], bank_guard47[bk]])
                v1 = p.do(VEC, lambda e, nb=nb, bk=bk: e.tensor_tensor(
                    out=sq[0:64, nb * 512:nb * 512 + 512],
                    in0=pr_slice64(bk),
                    in1=Sf[:, nb * 512:nb * 512 + 512], op=ALU.mult),
                    after=[sw, qn2_ev[nb]])
                v2 = p.do(VEC, lambda e, nb=nb, qpe_=qpe_: e.tensor_tensor(
                    out=qpe_[0:64, nb * 512:nb * 512 + 512],
                    in0=qpe_[0:64, nb * 512:nb * 512 + 512],
                    in1=Cf[:, nb * 512:nb * 512 + 512], op=ALU.mult),
                    after=[sw, qn2_ev[nb]])
                v3 = p.do(VEC, lambda e, nb=nb, qpe_=qpe_: e.tensor_tensor(
                    out=qpe_[0:64, nb * 512:nb * 512 + 512],
                    in0=qpe_[0:64, nb * 512:nb * 512 + 512],
                    in1=sq[0:64, nb * 512:nb * 512 + 512], op=ALU.add), after=[v1, v2])
                bank_guard47[bk] = v1
                rp_toks.append(v3)
            for nb in range(NB):
                bk = a_bank()
                mt = p.do(TENS, lambda e, par=par, nb=nb, bk=bk: [
                    e.matmul(bank(bk),
                             hk[:, par * 1024 + kc * 256:par * 1024 + kc * 256 + 128],
                             ckv[:, kc * T + nb * 512:kc * T + nb * 512 + 512],
                             start=(kc == 0), stop=(kc == 3))
                    for kc in range(4)][-1], after=[wk_t, bank_guard47[bk]])
                ev = p.do(SCAL, lambda e, h=h, nb=nb, bk=bk, kn_=kn_: e.activation(
                    out=kn_[:, nb * 512:nb * 512 + 512], in_=bank(bk), func=AF.Identity,
                    bias=bias_t["b_bkbn"][:, h:h + 1]),
                    after=[mt, bdone[par], bdone_s[par]])
                bank_guard47[bk] = ev
                kn_ev.append(ev)
                bk2 = a_bank()
                mt2 = p.do(TENS, lambda e, par=par, nb=nb, bk2=bk2: [
                    e.matmul(bank(bk2),
                             hk[:, par * 1024 + kc * 256 + 128:par * 1024 + kc * 256 + 256],
                             ckv[:, kc * T + nb * 512:kc * T + nb * 512 + 512],
                             start=(kc == 0), stop=(kc == 3))
                    for kc in range(4)][-1], after=[wk_t, bank_guard47[bk2]])
                ev2 = p.do(SCAL, lambda e, h=h, nb=nb, bk2=bk2: e.activation(
                    out=sq[:, nb * 512:nb * 512 + 512],
                    in_=bank(bk2), func=AF.Identity,
                    bias=bias_t["b_bkbv"][:, h:h + 1]),
                    after=[mt2, a_prev["vT"], rp_toks[nb], qn2_ev[nb]])
                bank_guard47[bk2] = ev2
                v_ev.append(ev2)
            vt_toks = []
            for tt in range(NTT):
                tr = p.do(TENS, lambda e, tt=tt: e.transpose(
                    pt[0:128, (tt % 2) * 1024:(tt % 2) * 1024 + 128],
                    sq[:, tt * 128:(tt + 1) * 128], identb[:]),
                    after=[v_ev[(tt * 128) // 512], ptg[tt % 2]])
                cp = p.do(VEC, lambda e, tt=tt, Vtm_=Vtm_: e.tensor_copy(
                    out=Vtm_[:, tt * 128:(tt + 1) * 128],
                    in_=pt[0:128, (tt % 2) * 1024:(tt % 2) * 1024 + 128]),
                    after=[tr, bdone[par] if tt < 2 else None])
                ptg[tt % 2] = cp
                vt_toks.append(cp)
            def emit_aug():
                kn2_ev = []
                for nb in range(NB):
                    s1 = p.do(SCAL, lambda e, nb=nb, kn_=kn_: e.activation(
                        out=sq[:, nb * 512:nb * 512 + 512], in_=kn_[:, nb * 512:nb * 512 + 512],
                        func=AF.Square), after=[kn_ev[nb], qn2_ev[nb]] + vt_toks)
                    bk = a_bank()
                    mt = p.do(TENS, lambda e, nb=nb, bk=bk: [
                        e.matmul(pp[32:33, bk * 512:bk * 512 + 512], ones_b[:],
                                 sq[:, nb * 512:nb * 512 + 512], start=True, stop=False),
                        e.matmul(pp[32:33, bk * 512:bk * 512 + 512], ones_b[0:64, :],
                                 kpe_sq[0:64, nb * 512:nb * 512 + 512],
                                 start=False, stop=True)][-1],
                        after=[s1, bank_guard47[bk]] + (kq_toks if h == 0 and nb == 0 else []))
                    ev = p.do(VEC, lambda e, nb=nb, bk=bk: e.tensor_copy(
                        out=rown[0:1, nb * 512:nb * 512 + 512],
                        in_=pp[32:33, bk * 512:bk * 512 + 512]),
                        after=[mt, a_prev["rows"]])
                    bank_guard47[bk] = ev
                    kn2_ev.append(ev)
                a2 = p.do(VEC, lambda e: e.reduce_max(out=b2t[32:33, 0:1], in_=rown[0:1, :],
                                                      axis=AX.X), after=kn2_ev)
                a3 = p.do(VEC, lambda e: e.tensor_scalar(out=rowq[0:1, :], in0=rowq[0:1, :],
                                                         scalar1=b2t[32:33, 0:1], scalar2=None,
                                                         op0=ALU.mult), after=[a2] + qn2_ev)
                a4l = None
                for nb in range(NB):
                    bk = a_bank()
                    m4 = p.do(TENS, lambda e, nb=nb, bk=bk: e.matmul(
                        pp[64:65, bk * 512:bk * 512 + 512], one33[32:33, 0:1],
                        rowq[0:1, nb * 512:nb * 512 + 512], start=True, stop=True),
                        after=[a3, bank_guard47[bk]])
                    a4l = p.do(SCAL, lambda e, nb=nb, bk=bk, qpe_=qpe_: e.activation(
                        out=qpe_[64:65, nb * 512:nb * 512 + 512],
                        in_=pp[64:65, bk * 512:bk * 512 + 512], func=AF.Sqrt),
                        after=[m4, bdone[par], bdone_v[par]])
                    bank_guard47[bk] = a4l
                a_prev["sq"] = kn2_ev[-1]
                a_prev["rows"] = a4l
                st_A[par]["aug"] = a4l
            a_prev["vT"] = vt_toks[-1]
            hquser[par] = mt2
            hkuser[par] = mt2
            st_A[par] = {"aug": None, "rope": rp_toks, "kn_ev": kn_ev,
                         "vt": vt_toks, "qh": qh_, "qpe": qpe_, "kn": kn_,
                         "Vtm": Vtm_}
            return emit_aug

        def stageB(h, mid=None):
            par = h % 2
            A = st_A[par]
            qh_, qpe_, kn_, Vtm_ = A["qh"], A["qpe"], A["kn"], A["Vtm"]
            st_guard = {0: bank_guard[0], 1: bank_guard[1]}
            last_pv = None
            for qb in range(NB):
                exp_toks = []
                den_last = None
                pv_last = None
                for kt in range(NTT):
                    stt = p.do(TENS, lambda e, kt=kt, qb=qb, kn_=kn_, qh_=qh_,
                               qpe_=qpe_: [
                        e.matmul(bank(kt % 2), kn_[:, kt * 128:kt * 128 + 128],
                                 qh_[:, qb * 512:qb * 512 + 512], start=True, stop=False),
                        e.matmul(bank(kt % 2), kpe[0:65, kt * 128:kt * 128 + 128],
                                 qpe_[0:65, qb * 512:qb * 512 + 512], start=False,
                                 stop=True)][-1],
                        after=[st_guard[kt % 2], A["aug"], A["rope"][qb]] +
                              (A["kn_ev"] + A["vt"] if kt == 0 and qb == 0 else []))
                    ex = p.do(SCAL, lambda e, kt=kt: e.activation(
                        out=PT[:, (kt % 8) * 512:(kt % 8) * 512 + 512], in_=bank(kt % 2),
                        func=AF.Exp, scale=float(1.0 / np.sqrt(128.0))),
                        after=[stt] + ([exp_toks[kt - 8][1]] if kt >= 8 else []))
                    st_guard[kt % 2] = ex
                    den_last = p.do(TENS, lambda e, kt=kt: e.matmul(
                        pp[0:1, 1024:1024 + 512], ones_b[:],
                        PT[:, (kt % 8) * 512:(kt % 8) * 512 + 512],
                        start=(kt == 0), stop=(kt == NTT - 1)),
                        after=[ex, bank_guard[2] if kt == 0 else None])
                    pv_last = p.do(TENS, lambda e, kt=kt, Vtm_=Vtm_: e.matmul(
                        pp[:, 1536:1536 + 512], Vtm_[:, kt * 128:kt * 128 + 128],
                        PT[:, (kt % 8) * 512:(kt % 8) * 512 + 512],
                        start=(kt == 0), stop=(kt == NTT - 1)),
                        after=[ex, bank_guard[3] if kt == 0 else None])
                    exp_toks.append((ex, pv_last))
                d1 = p.do(VEC, lambda e: e.reciprocal(out=dsb[0:1, :],
                                                      in_=pp[0:1, 1024:1024 + 512]),
                          after=[den_last])
                bank_guard[2] = d1
                r1_ = p.do(TENS, lambda e: e.matmul(pp[:, 0:512], ones_f[:],
                                                    dsb[0:1, :], start=True, stop=True),
                           after=[d1, st_guard[0]])
                r2_ = p.do(VEC, lambda e: e.tensor_copy(out=rbsb[:, :],
                                                        in_=pp[:, 0:512]),
                           after=[r1_])
                st_guard[0] = r2_
                ao = p.do(VEC, lambda e, h=h, qb=qb: e.tensor_tensor(
                    out=xln[:, h * T + qb * 512:h * T + qb * 512 + 512],
                    in0=pp[:, 1536:1536 + 512], in1=rbsb[:, :], op=ALU.mult),
                    after=[pv_last, r2_, tok_proj_last])
                bank_guard[3] = ao
                last_pv = pv_last
                bdone_v[par] = ao
                if qb == 0 and mid is not None:
                    mid()
            bank_guard[0], bank_guard[1] = st_guard[0], st_guard[1]
            bdone[par] = last_pv
            bdone_s[par] = Tok(SCAL, p.ecnt[SCAL])

        pend = stageA(0)
        pend()
        for h in range(HB):
            pend = stageA(h + 1) if h + 1 < HB else None
            stageB(h, mid=pend)

    with b.phase(pfx + "w"):
        wslab_w = b.sb(pfx + "ws2", [128, 2 * 2048], BF16)
        xo_w = b.sb(pfx + "xo", [128, T], F32)
        xrt_w = b.sb(pfx + "xr", [128, T], F32)
        bwo = b.sb(pfx + "bwo", [128, 16], F32)
        pp_w = b.ps(pfx + "pp2", [128, 3072], F32)
        bankg_w = {i: None for i in range(6)}
        bwo_t = p.dma(SYNC, pfx + "c2", lambda e: e.dma_start(out=bwo[:], in_=W["b_bwo"]))
        wuser2 = {0: None, 1: None}
        xrt_g = None
        xo_g = None
        for m in range(NCH):
            par = m % 2
            wt = p.dma(POOL, pfx + "wo%d" % par,
                       lambda e, m=m, par=par: e.dma_start(
                           out=wslab_w[:, par * 2048:(par + 1) * 2048], in_=W["b_wo"][m]),
                       after=[wuser2[par]])
            lt = p.dma(SYNC, pfx + "rx",
                       lambda e, m=m: e.dma_start(
                           out=xrt_w[:, :], in_=xsrc[m * 128:(m + 1) * 128, :]),
                       after=[xrt_g])
            evs = []
            for nb in range(T // 512):
                bk = nb % 2
                mt = p.do(TENS, lambda e, par=par, nb=nb: [
                    e.matmul(pp_w[:, (nb % 2) * 512:(nb % 2) * 512 + 512],
                             wslab_w[:, par * 2048 + kc * 128:par * 2048 + kc * 128 + 128],
                             xln[:, kc * T + nb * 512:kc * T + nb * 512 + 512],
                             start=(kc == 0), stop=(kc == NCH - 1))
                    for kc in range(NCH)][-1],
                    after=[wt, bankg_w[bk]])
                ev = p.do(SCAL, lambda e, m=m, nb=nb: e.activation(
                    out=xo_w[:, nb * 512:nb * 512 + 512],
                    in_=pp_w[:, (nb % 2) * 512:(nb % 2) * 512 + 512], func=AF.Identity,
                    bias=bwo[:, m:m + 1]), after=[mt, xo_g, bwo_t])
                bankg_w[bk] = ev
                wuser2[par] = mt
                ad = p.do(VEC, lambda e, nb=nb: e.tensor_tensor(
                    out=xo_w[:, nb * 512:nb * 512 + 512],
                    in0=xo_w[:, nb * 512:nb * 512 + 512],
                    in1=xrt_w[:, nb * 512:nb * 512 + 512], op=ALU.add), after=[ev, lt])
                evs.append(ad)
            st = p.dma(SYNC, pfx + "sx",
                       lambda e, m=m: e.dma_start(
                           out=dst[m * 128:(m + 1) * 128, :], in_=xo_w[:]), after=evs)
            xrt_g = st
            xo_g = st


def build_attnb_only():
    b = Build()
    xin = b.dram_in("x_in", [D, T], F32)
    out = b.dram_out("x_out", [D, T], F32)
    wshapes = {"b_wqa": (4, 128, 2048), "b_wkva": (5, 128, 2048),
               "b_wqb": (16, 128, 768), "b_wkvb": (16, 128, 1024),
               "b_wo": (16, 128, 2048), "b_bqa": (128, 4), "b_bkva": (128, 5),
               "b_bqbn": (128, 16), "b_bqbp": (64, 16), "b_bkbn": (128, 16),
               "b_bkbv": (128, 16), "b_bwo": (128, 16)}
    W = {k: b.dram_in(k, list(s), F32) for k, s in wshapes.items()}
    emit_attn_b(b, xin, out, W, "a1")
    with b.nc.Block() as blk:
        b.p.emit(blk)
        @blk.sync
        def _(eng):
            for name, rec in b.p.dsems.items():
                eng.wait_ge(rec[0], rec[1])
    return b



def prep_attna_weights(I):
    f32 = np.float32
    out = {
        "a_wr": _relayout(np.asarray(I["L0_attn_a_W_r"], f32)),
        "a_wk": _relayout(np.asarray(I["L0_attn_a_W_k"], f32)),
        "a_wv": _relayout(np.asarray(I["L0_attn_a_W_v"], f32)),
        "a_wo": _relayout(np.asarray(I["L0_attn_a_W_o"], f32)),
        "a_w1": np.ascontiguousarray(np.asarray(I["L0_attn_a_w1"], f32)
                                     .reshape(16, 128, 64).transpose(1, 0, 2).reshape(128, 1024)),
        "a_a1": np.ascontiguousarray(np.asarray(I["L0_attn_a_a1"], f32)
                                     .reshape(16, 128, 64).transpose(1, 0, 2).reshape(128, 1024)),
        "a_g1": np.ascontiguousarray(np.asarray(I["L0_attn_a_g1"], f32)
                                     .reshape(16, 128, 128).transpose(1, 0, 2).reshape(128, 2048)),
        "a_w2": np.ascontiguousarray(np.asarray(I["L0_attn_a_w2"], f32)),
        "a_a2": np.ascontiguousarray(np.asarray(I["L0_attn_a_a2"], f32)),
        "a_g2": np.ascontiguousarray(np.asarray(I["L0_attn_a_g2"], f32)),
    }
    names = ["x_r", "x_w", "x_k", "x_v", "x_a", "x_g", "w0", "a0", "k_k", "k_a"]
    vs = [np.asarray(I["L0_attn_a_" + n], f32).reshape(-1) for n in names]
    vs.append(np.asarray(I["L0_attn_a_r_k"], f32).reshape(-1))
    vs.append(np.asarray(I["L0_attn_a_gn_scale"], f32).reshape(-1))
    vs.append(np.asarray(I["L0_attn_a_gn_offset"], f32).reshape(-1))
    V = np.stack(vs, axis=1)
    out["a_vecs"] = np.ascontiguousarray(V.reshape(NCH, 128, 13))
    return out


VI = {n: i for i, n in enumerate(
    ["x_r", "x_w", "x_k", "x_v", "x_a", "x_g", "w0", "a0", "k_k", "k_a",
     "r_k", "gn_s", "gn_o"])}


def emit_attn_a(b, xsrc, dst, W, pfx):
    p = b.p
    bf = bf16c(b)
    if "xln" not in b.com:
        b.com["xln"] = b.sbp("g_xln", [128, NCH * T], BF16)
    xln = b.com["xln"]
    ln_done = emit_ln(b, xsrc, xln, pfx + "l")
    ones_b = b.com["ct_ones128_b"]
    ones_f = b.com["ct_ones128_f"]
    identb = b.const_tile("identb", np.eye(128, dtype=bf))

    NB = T // 512
    NCC = T // LCH

    streams = {n: b.dram_tmp(pfx + "s_" + n, [D, T], BF16)
               for n in ["r", "k2", "kk", "bb", "v", "sz", "gg", "bo"]}
    y_hbm = b.dram_tmp(pfx + "s_y", [NCC, 128, 16 * LCH], BF16)

    with b.phase(pfx + "p"):
        vecs = b.sbp(pfx + "vecs", [128, NCH * 13], F32)
        vld = p.dma(SYNC, pfx + "c", lambda e: e.dma_start(
            out=vecs[:].rearrange("p (c v) -> p c v", v=13), in_=W["a_vecs"].rearrange("c p v -> p c v")))
        omc = b.sbp(pfx + "omc", [128, NCH * 7], F32)
        omtok = []
        for c in range(NCH):
            t1 = p.do(VEC, lambda e, c=c: e.tensor_scalar(
                out=omc[:, c * 7:c * 7 + 6], in0=vecs[:, c * 13:c * 13 + 6],
                scalar1=-1.0, scalar2=1.0, op0=ALU.mult, op1=ALU.add), after=[vld])
            t2 = p.do(VEC, lambda e, c=c: e.tensor_scalar(
                out=omc[:, c * 7 + 6:c * 7 + 7], in0=vecs[:, c * 13 + 9:c * 13 + 10],
                scalar1=-1.0, scalar2=1.0, op0=ALU.mult, op1=ALU.add), after=[vld])
            omtok += [t1, t2]

        mix = b.sb(pfx + "mix", [128, NCH * T], BF16)
        wslab = b.sb(pfx + "ws", [128, 2 * 2048], BF16)
        stg = b.sb(pfx + "stg", [128, 2 * T], BF16)
        t64 = b.sb(pfx + "t64", [64, T], BF16)
        lora1 = b.sb(pfx + "lo1", [128, 2048], BF16)
        lora2 = b.sb(pfx + "lo2", [128, 2048], BF16)
        t128 = b.sb(pfx + "t128", [128, T], BF16)
        pp = b.ps(pfx + "pp", [128, 3072], F32)
        def bank(i):
            return pp[:, i * 512:(i + 1) * 512]
        bank_g = {i: None for i in range(6)}
        wuser = {0: None, 1: None}
        stg_g = {0: None, 1: None}

        mix_built = []
        mix_users = []

        def build_mix(vi):
            nonlocal mix_built, mix_users
            toks = []
            for c in range(NCH):
                ms = p.do(VEC, lambda e, c=c: e.memset(mix[:, c * T:c * T + 1], 0.0),
                          after=mix_users + omtok)
                t1 = p.do(VEC, lambda e, c=c, vi=vi: e.tensor_scalar(
                    out=mix[:, c * T + 1:c * T + T],
                    in0=xln[:, c * T:c * T + T - 1],
                    scalar1=vecs[:, c * 13 + vi:c * 13 + vi + 1], scalar2=None,
                    op0=ALU.mult), after=mix_users + omtok + ln_done)
                t2 = p.do(VEC, lambda e, c=c, vi=vi: e.scalar_tensor_tensor(
                    out=mix[:, c * T:c * T + T], in0=xln[:, c * T:c * T + T],
                    scalar=omc[:, c * 7 + vi:c * 7 + vi + 1],
                    in1=mix[:, c * T:c * T + T], op0=ALU.mult, op1=ALU.add),
                    after=[ms, t1])
                toks.append(t2)
            mix_built = toks
            mix_users = []

        def spill(stream, c, par, src_tok):
            st = p.dma(SYNC, pfx + "sp%d" % par,
                       lambda e, stream=stream, c=c, par=par: e.dma_start(
                           out=streams[stream][c * 128:(c + 1) * 128, :],
                           in_=stg[:, par * T:(par + 1) * T]), after=[src_tok])
            stg_g[par] = st
            return st

        def proj_big(wkey, stream, vi):
            build_mix(vi)
            for m in range(NCH):
                par = m % 2
                wt = p.dma(POOL, pfx + "w%d" % par,
                           lambda e, m=m, par=par, wkey=wkey: e.dma_start(
                               out=wslab[:, par * 2048:(par + 1) * 2048],
                               in_=W[wkey][m]), after=[wuser[par]])
                evs = []
                for nb in range(NB):
                    bk = nb % 2
                    mt = p.do(TENS, lambda e, m=m, par=par, nb=nb: [
                        e.matmul(bank(nb % 2),
                                 wslab[:, par * 2048 + kc * 128:par * 2048 + kc * 128 + 128],
                                 mix[:, kc * T + nb * 512:kc * T + nb * 512 + 512],
                                 start=(kc == 0), stop=(kc == NCH - 1))
                        for kc in range(NCH)][-1],
                        after=[wt, bank_g[bk]] + mix_built)
                    mix_users.append(mt)
                    wuser[par] = mt
                    ev = p.do(SCAL, lambda e, m=m, nb=nb: e.activation(
                        out=stg[:, (m % 2) * T + nb * 512:(m % 2) * T + nb * 512 + 512],
                        in_=bank(nb % 2), func=AF.Identity),
                        after=[mt, stg_g[m % 2]])
                    bank_g[bk] = ev
                    evs.append(ev)
                spill(stream, m, m % 2, evs[-1])

        def proj_lora(w1key, w2key, midfunc, outfunc, bias_vi, stream, vi, mid_m=64):
            build_mix(vi)
            l1 = p.dma(POOL, pfx + "l1", lambda e, w1key=w1key: e.dma_start(
                out=lora1[:, 0:NCH * mid_m], in_=W[w1key]), after=[Tok(TENS, p.ecnt[TENS])])
            l2 = p.dma(POOL, pfx + "l2", lambda e, w2key=w2key: e.dma_start(
                out=lora2[0:mid_m, :], in_=W[w2key]), after=[Tok(TENS, p.ecnt[TENS])])
            mid = t64 if mid_m == 64 else t128
            for nb in range(NB):
                bk = nb % 2
                mt = p.do(TENS, lambda e, nb=nb, mid_m=mid_m: [
                    e.matmul(pp[0:mid_m, (nb % 2) * 512:(nb % 2) * 512 + 512],
                             lora1[:, kc * mid_m:(kc + 1) * mid_m],
                             mix[:, kc * T + nb * 512:kc * T + nb * 512 + 512],
                             start=(kc == 0), stop=(kc == NCH - 1))
                    for kc in range(NCH)][-1],
                    after=[l1, bank_g[bk]] + mix_built)
                mix_users.append(mt)
                ev = p.do(SCAL, lambda e, nb=nb, mid=mid, mid_m=mid_m, midfunc=midfunc:
                          e.activation(out=mid[0:mid_m, nb * 512:nb * 512 + 512],
                                       in_=pp[0:mid_m, (nb % 2) * 512:(nb % 2) * 512 + 512],
                                       func=midfunc), after=[mt])
                bank_g[bk] = ev
            mid_done = Tok(SCAL, p.ecnt[SCAL])
            for m in range(NCH):
                evs = []
                for nb in range(NB):
                    bk = 2 + nb % 2
                    mt = p.do(TENS, lambda e, m=m, nb=nb, mid=mid, mid_m=mid_m: e.matmul(
                        bank(2 + nb % 2), lora2[0:mid_m, m * 128:(m + 1) * 128],
                        mid[0:mid_m, nb * 512:nb * 512 + 512], start=True, stop=True),
                        after=[l2, bank_g[bk], mid_done])
                    def evf(e, m=m, nb=nb, outfunc=outfunc, bias_vi=bias_vi):
                        kw = {}
                        if bias_vi is not None:
                            kw["bias"] = vecs[:, m * 13 + bias_vi:m * 13 + bias_vi + 1]
                        return e.activation(
                            out=stg[:, (m % 2) * T + nb * 512:(m % 2) * T + nb * 512 + 512],
                            in_=bank(2 + nb % 2), func=outfunc, **kw)
                    ev = p.do(SCAL, evf, after=[mt, stg_g[m % 2], vld])
                    bank_g[bk] = ev
                    evs.append(ev)
                spill(stream, m, m % 2, evs[-1])

        streams["kraw"] = b.dram_tmp(pfx + "s_kraw", [D, T], BF16)
        streams["as"] = b.dram_tmp(pfx + "s_as", [D, T], BF16)

        proj_big("a_wr", "r", VI["x_r"])
        proj_big("a_wk", "kraw", VI["x_k"])
        proj_big("a_wv", "v", VI["x_v"])
        proj_lora("a_w1", "a_w2", AF.Tanh, AF.Sigmoid, VI["w0"], "sz", VI["x_w"], 64)
        proj_lora("a_a1", "a_a2", AF.Identity, AF.Sigmoid, VI["a0"], "as", VI["x_a"], 64)
        proj_lora("a_g1", "a_g2", AF.Sigmoid, AF.Identity, None, "gg", VI["x_g"], 128)
        vecs_keep = vecs
        omc_keep = omc

    with b.phase(pfx + "q"):
        vecs = vecs_keep
        omc = omc_keep
        hsel2_np = np.zeros((128, 2), bf16c(b))
        hsel2_np[0:64, 0] = 1.0
        hsel2_np[64:128, 1] = 1.0
        bsel2_np = np.zeros((2, 128), np.float32)
        bsel2_np[0, 0:64] = 1.0
        bsel2_np[1, 64:128] = 1.0
        hsel2 = b.const_tile("hsel2", hsel2_np)
        bsel2 = b.const_tile("bsel2", bsel2_np)
        pk = b.sb(pfx + "pk", [128, T], BF16)
        pa = b.sb(pfx + "pa", [128, T], BF16)
        rrt = b.sb(pfx + "rrt", [128, T], BF16)
        pv = b.sb(pfx + "pv", [128, T], BF16)
        kkt = b.sb(pfx + "kkt", [128, T], BF16)
        k2t = b.sb(pfx + "k2t", [128, T], BF16)
        btl = b.sb(pfx + "btl", [128, T], BF16)
        bont = b.sb(pfx + "bont", [128, T], BF16)
        sqs = b.sb(pfx + "sqs", [128, 1024], BF16)
        rows2 = b.sb(pfx + "rws", [2, T], F32)
        pq = b.ps(pfx + "pq", [128, 3072], F32)
        gq = {"ld": None, "done": None}
        for c in range(NCH):
            lk = p.dma(SYNC, pfx + "qk", lambda e, c=c: e.dma_start(
                out=pk[:], in_=streams["kraw"][c * 128:(c + 1) * 128, :]), after=[gq["done"]])
            la = p.dma(SYNC, pfx + "qa", lambda e, c=c: e.dma_start(
                out=pa[:], in_=streams["as"][c * 128:(c + 1) * 128, :]), after=[gq["done"]])
            lr = p.dma(SYNC, pfx + "qr", lambda e, c=c: e.dma_start(
                out=rrt[:], in_=streams["r"][c * 128:(c + 1) * 128, :]), after=[gq["done"]])
            lv = p.dma(SYNC, pfx + "qv", lambda e, c=c: e.dma_start(
                out=pv[:], in_=streams["v"][c * 128:(c + 1) * 128, :]), after=[gq["done"]])
            k1 = p.do(VEC, lambda e, c=c: e.tensor_scalar(
                out=kkt[:], in0=pk[:], scalar1=vecs[:, c * 13 + 8:c * 13 + 9],
                scalar2=None, op0=ALU.mult), after=[lk, gq["done"]])
            nrm_ev = []
            for nb in range(NB):
                s1 = p.do(SCAL, lambda e, nb=nb: e.activation(
                    out=sqs[:, (nb % 2) * 512:(nb % 2) * 512 + 512],
                    in_=kkt[:, nb * 512:nb * 512 + 512], func=AF.Square),
                    after=[k1] + ([nrm_ev[nb - 2]] if nb >= 2 else []))
                m1 = p.do(TENS, lambda e, nb=nb: e.matmul(
                    pq[0:2, 2048:2048 + 512], hsel2[:],
                    sqs[:, (nb % 2) * 512:(nb % 2) * 512 + 512], start=True, stop=True),
                    after=[s1, nrm_ev[-1] if nrm_ev else None])
                e1 = p.do(VEC, lambda e, nb=nb: e.tensor_copy(
                    out=rows2[0:2, nb * 512:nb * 512 + 512], in_=pq[0:2, 2048:2048 + 512]),
                    after=[m1])
                nrm_ev.append(e1)
            s2 = p.do(SCAL, lambda e: e.activation(out=rows2[0:2, :], in_=rows2[0:2, :],
                                                   func=AF.Sqrt), after=nrm_ev)
            v2 = p.do(VEC, lambda e: e.tensor_scalar(out=rows2[0:2, :], in0=rows2[0:2, :],
                                                     scalar1=1e-12, scalar2=None,
                                                     op0=ALU.max), after=[s2])
            v3 = p.do(VEC, lambda e: e.reciprocal(out=rows2[0:2, :], in_=rows2[0:2, :]),
                      after=[v2])
            kk_done = []
            for nb in range(NB):
                m2 = p.do(TENS, lambda e, nb=nb: e.matmul(
                    pq[:, 2560:2560 + 512], bsel2[:],
                    rows2[0:2, nb * 512:nb * 512 + 512], start=True, stop=True),
                    after=[v3, kk_done[-1] if kk_done else None])
                v4 = p.do(VEC, lambda e, nb=nb: e.tensor_tensor(
                    out=kkt[:, nb * 512:nb * 512 + 512], in0=kkt[:, nb * 512:nb * 512 + 512],
                    in1=pq[:, 2560:2560 + 512], op=ALU.mult), after=[m2])
                kk_done.append(v4)
            b1 = p.do(VEC, lambda e, c=c: e.tensor_scalar(
                out=btl[:], in0=pa[:], scalar1=vecs[:, c * 13 + 9:c * 13 + 10],
                scalar2=omc[:, c * 7 + 6:c * 7 + 7], op0=ALU.mult, op1=ALU.add),
                after=[la, gq["done"]])
            b2 = p.do(VEC, lambda e: e.tensor_tensor(out=k2t[:], in0=pk[:], in1=btl[:],
                                                     op=ALU.mult), after=[b1, k1])
            b3 = p.do(VEC, lambda e: e.tensor_tensor(out=btl[:], in0=kkt[:], in1=pa[:],
                                                     op=ALU.mult), after=kk_done + [b1, b2])
            z1 = p.do(VEC, lambda e: e.tensor_tensor(out=bont[:], in0=rrt[:], in1=k2t[:],
                                                     op=ALU.mult), after=[lr, b2])
            z2 = p.do(VEC, lambda e, c=c: e.tensor_scalar(
                out=bont[:], in0=bont[:], scalar1=vecs[:, c * 13 + 10:c * 13 + 11],
                scalar2=None, op0=ALU.mult), after=[z1])
            bon_done = []
            for nb in range(NB):
                m3 = p.do(TENS, lambda e, nb=nb: e.matmul(
                    pq[0:2, 2048:2048 + 512], hsel2[:],
                    bont[:, nb * 512:nb * 512 + 512], start=True, stop=True),
                    after=[z2, bon_done[-1] if bon_done else None] + ([nrm_ev[-1]] if nb == 0 else []))
                e3 = p.do(VEC, lambda e, nb=nb: e.tensor_copy(
                    out=rows2[0:2, nb * 512:nb * 512 + 512], in_=pq[0:2, 2048:2048 + 512]),
                    after=[m3, kk_done[-1] if nb == 0 else None])
                m4 = p.do(TENS, lambda e, nb=nb: e.matmul(
                    pq[:, 2560:2560 + 512], bsel2[:],
                    rows2[0:2, nb * 512:nb * 512 + 512], start=True, stop=True),
                    after=[e3, kk_done[-1] if nb == 0 else None])
                v5 = p.do(VEC, lambda e, nb=nb: e.tensor_tensor(
                    out=bont[:, nb * 512:nb * 512 + 512], in0=pv[:, nb * 512:nb * 512 + 512],
                    in1=pq[:, 2560:2560 + 512], op=ALU.mult), after=[m4, lv])
                bon_done.append(v5)
            sp1 = p.dma(SYNC, pfx + "qs1", lambda e, c=c: e.dma_start(
                out=streams["kk"][c * 128:(c + 1) * 128, :], in_=kkt[:]), after=kk_done + [b3])
            sp2 = p.dma(SYNC, pfx + "qs2", lambda e, c=c: e.dma_start(
                out=streams["k2"][c * 128:(c + 1) * 128, :], in_=k2t[:]), after=[b2, z1])
            sp3 = p.dma(SYNC, pfx + "qs3", lambda e, c=c: e.dma_start(
                out=streams["bb"][c * 128:(c + 1) * 128, :], in_=btl[:]), after=[b3])
            sp4 = p.dma(SYNC, pfx + "qs4", lambda e, c=c: e.dma_start(
                out=streams["bo"][c * 128:(c + 1) * 128, :], in_=bont[:]), after=bon_done)
            gq["done"] = p.do(VEC, lambda e: e.engine_nop(), after=[sp1, sp2, sp3, sp4])

    with b.phase(pfx + "r"):
        bfdt = bf16c(b)
        tri_i = b.const_tile("tri_i", np.triu(np.ones((128, 128), bfdt), 0))
        tri_s = b.const_tile("tri_s", np.triu(np.ones((128, 128), bfdt), 1))
        msk_su = b.const_tile("msk_su", np.triu(np.ones((128, 128), np.float32), 1))
        msk_iu = b.const_tile("msk_iu", np.triu(np.ones((128, 128), np.float32), 0))
        msk_sl = b.const_tile("msk_sl", np.tril(np.ones((128, 128), np.float32), -1))

        SDE = 0.60653065971263342
        nst = 6
        sn = ["r", "k2", "kk", "bb", "v", "sz"]
        stt = {n: b.sb(pfx + "st_" + n, [128, 2 * 2048], BF16) for n in sn}
        lamT = b.sb(pfx + "lamT", [128, 6144], BF16)
        lamL32 = b.sb(pfx + "lamL", [128, NCH], F32)
        ltm = b.sb(pfx + "ltm", [128, 2048], BF16)
        sztm = b.sb(pfx + "sztm", [128, 2048], BF16)
        ktm = b.sb(pfx + "ktm", [128, 2048], BF16)
        btm = b.sb(pfx + "btm", [128, 2048], BF16)
        vtm = b.sb(pfx + "vtm", [128, 2048], BF16)
        AR = b.sb(pfx + "AR", [128, NCH * 256], BF16)
        BK = b.sb(pfx + "BK", [128, NCH * 256], BF16)
        ysb = b.sb(pfx + "ysb", [128, 2048], BF16)
        stateT = b.sb(pfx + "stT", [128, 1024], BF16)
        hw8 = [{n: b.sb(pfx + "h%d_%s" % (q, n), [128, 128], BF16)
                for n in ["gt", "pt", "ht", "qt", "gm", "y1", "g1", "m0", "m1",
                          "cb", "ub"]} for q in range(8)]
        pr = b.ps(pfx + "prw", [128, 3072], F32)
        pt2 = b.ps(pfx + "pt2", [128, 2048], BF16)
        def rbank(i):
            return pr[:, i * 512:(i + 1) * 512]

        st0 = p.do(VEC, lambda e: e.memset(stateT[:], 0.0))
        ld_g = {}
        rb_g = {i: None for i in range(6)}
        ptg2 = {0: None, 1: None}
        prev_cc = {"tens": None, "vec": None, "scal": None, "ysp": None,
                   "state": [st0] * HA}

        for cc in range(NCC):
            par = cc % 2
            lds = {}
            for n in sn:
                lds[n] = p.dma(SYNC, pfx + "r_" + n,
                               lambda e, n=n, cc=cc, par=par: e.dma_start(
                                   out=stt[n][:, par * 2048:(par + 1) * 2048]
                                       .rearrange("p (c t) -> p c t", t=128),
                                   in_=streams[n].rearrange("(c p) t -> p c t", p=128)
                                       [:, :, cc * 128:(cc + 1) * 128]),
                               after=[ld_g.get((n, par))])
            cur = {n: stt[n][:, par * 2048:(par + 1) * 2048] for n in sn}

            tr_toks = []
            for fc in range(NCH):
                tr = p.do(TENS, lambda e, fc=fc, src=cur["sz"]: e.transpose(
                    pt2[:, (fc % 2) * 1024:(fc % 2) * 1024 + 128],
                    src[:, fc * 128:(fc + 1) * 128], identb[:]),
                    after=[lds["sz"], ptg2[fc % 2]])
                cp = p.do(VEC, lambda e, fc=fc: e.tensor_copy(
                    out=sztm[:, fc * 128:(fc + 1) * 128],
                    in_=pt2[:, (fc % 2) * 1024:(fc % 2) * 1024 + 128]),
                    after=[tr, prev_cc["vec"] if fc == 0 else None])
                ptg2[fc % 2] = cp
                tr_toks.append(cp)
            cs_tm = []
            for q in range(4):
                mt = p.do(TENS, lambda e, q=q: e.matmul(
                    rbank(q), tri_i[:], sztm[:, q * 512:q * 512 + 512],
                    start=True, stop=True), after=tr_toks + [rb_g[q]])
                cs_tm.append(mt)
            ex1 = []
            for q in range(4):
                e1 = p.do(SCAL, lambda e, q=q: e.activation(
                    out=ltm[:, q * 512:q * 512 + 512], in_=rbank(q), func=AF.Exp,
                    scale=SDE), after=[cs_tm[q]] +
                    ([prev_cc["scal"], prev_cc["vec"]] if q == 0 else []))
                rb_g[q] = e1
                ex1.append(e1)
            lam_toks = []
            for fc in range(NCH):
                bki = 4 + (fc % 2)
                mi = p.do(TENS, lambda e, fc=fc, bki=bki: [
                    e.matmul(pr[:, bki * 512:bki * 512 + 128],
                             sztm[:, fc * 128:(fc + 1) * 128], tri_i[:],
                             start=True, stop=True),
                    e.matmul(pr[:, bki * 512 + 128:bki * 512 + 256],
                             sztm[:, fc * 128:(fc + 1) * 128], tri_s[:],
                             start=True, stop=True)][-1],
                    after=tr_toks + [rb_g[bki]])
                e2 = p.do(SCAL, lambda e, fc=fc, bki=bki: [
                    e.activation(out=lamT[:, fc * 128:fc * 128 + 128],
                                 in_=pr[:, bki * 512:bki * 512 + 128],
                                 func=AF.Exp, scale=-SDE),
                    e.activation(out=lamT[:, 2048 + fc * 128:2048 + fc * 128 + 128],
                                 in_=pr[:, bki * 512 + 128:bki * 512 + 256],
                                 func=AF.Exp, scale=-SDE),
                    e.activation(out=lamT[:, 4096 + fc * 128:4096 + fc * 128 + 128],
                                 in_=pr[:, bki * 512:bki * 512 + 128],
                                 func=AF.Exp, scale=SDE),
                    e.activation(out=lamL32[:, fc:fc + 1],
                                 in_=pr[:, bki * 512 + 127:bki * 512 + 128],
                                 func=AF.Exp, scale=-SDE)][-1],
                    after=[mi] + ([prev_cc["vec"], prev_cc["scal"]] if fc == 0 else []))
                rb_g[bki] = e2
                e3 = p.do(VEC, lambda e, fc=fc: e.tensor_scalar(
                    out=lamT[:, 2048 + fc * 128:2048 + fc * 128 + 128],
                    in0=lamT[:, 2048 + fc * 128:2048 + fc * 128 + 128],
                    scalar1=-1.0, scalar2=None, op0=ALU.mult), after=[e2])
                lam_toks.append(e3)

            fm_toks = []
            for fc in range(NCH):
                a1_ = p.do(VEC, lambda e, fc=fc, cur=cur: e.tensor_tensor(
                    out=AR[:, fc * 256:fc * 256 + 128],
                    in0=cur["kk"][:, fc * 128:(fc + 1) * 128],
                    in1=lamT[:, 2048 + fc * 128:2048 + fc * 128 + 128], op=ALU.mult),
                    after=[lam_toks[fc], lds["kk"], prev_cc["tens"]])
                a2_ = p.do(VEC, lambda e, fc=fc, cur=cur: e.tensor_tensor(
                    out=AR[:, fc * 256 + 128:fc * 256 + 256],
                    in0=cur["r"][:, fc * 128:(fc + 1) * 128],
                    in1=lamT[:, fc * 128:fc * 128 + 128], op=ALU.mult),
                    after=[lam_toks[fc], lds["r"], prev_cc["tens"]])
                a3_ = p.do(VEC, lambda e, fc=fc, cur=cur: e.tensor_tensor(
                    out=BK[:, fc * 256:fc * 256 + 128],
                    in0=cur["bb"][:, fc * 128:(fc + 1) * 128],
                    in1=lamT[:, 4096 + fc * 128:4096 + fc * 128 + 128], op=ALU.mult),
                    after=[lam_toks[fc], lds["bb"], prev_cc["tens"]])
                a4_ = p.do(VEC, lambda e, fc=fc, cur=cur: e.tensor_tensor(
                    out=BK[:, fc * 256 + 128:fc * 256 + 256],
                    in0=cur["k2"][:, fc * 128:(fc + 1) * 128],
                    in1=lamT[:, 4096 + fc * 128:4096 + fc * 128 + 128], op=ALU.mult),
                    after=[lam_toks[fc], lds["k2"], prev_cc["tens"]])
                fm_toks += [a1_, a2_, a3_, a4_]

            tm_done = {}
            for n, dsttile in (("k2", ktm), ("bb", btm), ("v", vtm)):
                tts = []
                for fc in range(NCH):
                    tr = p.do(TENS, lambda e, fc=fc, src=cur[n]: e.transpose(
                        pt2[:, (fc % 2) * 1024:(fc % 2) * 1024 + 128],
                        src[:, fc * 128:(fc + 1) * 128], identb[:]),
                        after=[lds[n], ptg2[fc % 2]])
                    cp = p.do(VEC, lambda e, n=n, fc=fc, dsttile=dsttile: e.tensor_copy(
                        out=dsttile[:, fc * 128:(fc + 1) * 128],
                        in_=pt2[:, (fc % 2) * 1024:(fc % 2) * 1024 + 128]),
                        after=[tr, prev_cc["tens"] if fc == 0 else None])
                    ptg2[fc % 2] = cp
                    tts.append(cp)
                tm_done[n] = tts
            ksc = []
            for q in range(4):
                k_ = p.do(VEC, lambda e, q=q: e.tensor_tensor(
                    out=ktm[:, q * 512:q * 512 + 512], in0=ktm[:, q * 512:q * 512 + 512],
                    in1=ltm[:, q * 512:q * 512 + 512], op=ALU.mult),
                    after=tm_done["k2"] + [ex1[q]])
                b_ = p.do(VEC, lambda e, q=q: e.tensor_tensor(
                    out=btm[:, q * 512:q * 512 + 512], in0=btm[:, q * 512:q * 512 + 512],
                    in1=ltm[:, q * 512:q * 512 + 512], op=ALU.mult),
                    after=tm_done["bb"] + [ex1[q]])
                ksc += [k_, b_]
            ld_g[("sz", par)] = lam_toks[-1]
            ld_g[("kk", par)] = fm_toks[-1]
            ld_g[("r", par)] = fm_toks[-1]
            ld_g[("k2", par)] = tm_done["k2"][-1]
            ld_g[("bb", par)] = tm_done["bb"][-1]
            ld_g[("v", par)] = tm_done["v"][-1]

            head_toks = [None] * HA
            ysb_toks = []
            rbc = [0]
            def nxt_bank():
                i = rbc[0] % 6
                rbc[0] += 1
                return i
            for hg in range(0, HA, 8):
                H8 = list(range(hg, hg + 8))
                X = {}
                for h in H8:
                    q = h % 8
                    fc, po = h // 2, (h % 2) * 64
                    X[h] = {
                        "a": AR[po:po + 64, fc * 256:fc * 256 + 128],
                        "r": AR[po:po + 64, fc * 256 + 128:fc * 256 + 256],
                        "b": BK[po:po + 64, fc * 256:fc * 256 + 128],
                        "k": BK[po:po + 64, fc * 256 + 128:fc * 256 + 256],
                        "ar": AR[po:po + 64, fc * 256:fc * 256 + 256],
                        "Ktm": ktm[:, fc * 128 + po:fc * 128 + po + 64],
                        "Btm": btm[:, fc * 128 + po:fc * 128 + po + 64],
                        "Vtm": vtm[:, fc * 128 + po:fc * 128 + po + 64],
                        "stT": stateT[po:po + 64, fc * 64:fc * 64 + 64],
                        "lamL": lamL32[po:po + 64, fc:fc + 1],
                        "po": po, "fc": fc,
                        "eng": VEC if (h % 2 == 0) else SCAL,
                        "w": hw8[q],
                    }
                dep0 = fm_toks + ksc + tm_done["v"] + [t for t in prev_cc["state"] if t]

                def ev_step(eng, dst, src_ap, after):
                    if eng == SCAL:
                        return p.do(SCAL, lambda e, dst=dst, src_ap=src_ap: e.activation(
                            out=dst, in_=src_ap, func=AF.Identity), after=after)
                    return p.do(VEC, lambda e, dst=dst, src_ap=src_ap: e.tensor_copy(
                        out=dst, in_=src_ap), after=after)

                for h in H8:
                    x = X[h]
                    w = x["w"]
                    b0 = nxt_bank(); b1 = nxt_bank(); b2 = nxt_bank()
                    pa_ = p.do(TENS, lambda e, x=x, b0=b0: e.matmul(
                        pr[:, b0 * 512:b0 * 512 + 256], x["b"], x["ar"],
                        start=True, stop=True), after=dep0 + [rb_g[b0]])
                    pb_ = p.do(TENS, lambda e, x=x, b1=b1: e.matmul(
                        pr[:, b1 * 512:b1 * 512 + 256], x["k"], x["ar"],
                        start=True, stop=True), after=dep0 + [rb_g[b1]])
                    pc_ = p.do(TENS, lambda e, x=x, b2=b2: e.matmul(
                        pr[:, b2 * 512:b2 * 512 + 128], x["a"], x["b"],
                        start=True, stop=True), after=dep0 + [rb_g[b2]])
                    ma = p.do(VEC, lambda e, w=w, b0=b0: e.tensor_tensor(
                        out=w["gt"][:], in0=pr[:, b0 * 512:b0 * 512 + 128],
                        in1=msk_su[:], op=ALU.mult), after=[pa_])
                    ma2 = p.do(VEC, lambda e, w=w, b0=b0: e.tensor_tensor(
                        out=w["pt"][:], in0=pr[:, b0 * 512 + 128:b0 * 512 + 256],
                        in1=msk_iu[:], op=ALU.mult), after=[pa_])
                    mb = p.do(SCAL if h % 2 else VEC, None, after=[pb_]) if False else None
                    mb = p.do(VEC, lambda e, w=w, b1=b1: e.tensor_tensor(
                        out=w["ht"][:], in0=pr[:, b1 * 512:b1 * 512 + 128],
                        in1=msk_su[:], op=ALU.mult), after=[pb_])
                    mb2 = p.do(VEC, lambda e, w=w, b1=b1: e.tensor_tensor(
                        out=w["qt"][:], in0=pr[:, b1 * 512:b1 * 512 + 256]
                        [:, 128:256] if False else pr[:, b1 * 512 + 128:b1 * 512 + 256],
                        in1=msk_iu[:], op=ALU.mult), after=[pb_])
                    mc2 = p.do(VEC, lambda e, w=w, b2=b2: e.tensor_tensor(
                        out=w["gm"][:], in0=pr[:, b2 * 512:b2 * 512 + 128],
                        in1=msk_sl[:], op=ALU.mult), after=[pc_])
                    m0t = p.do(VEC, lambda e, w=w: e.tensor_tensor(
                        out=w["m0"][:], in0=w["gt"][:], in1=identb[:], op=ALU.add),
                        after=[ma])
                    rb_g[b0], rb_g[b1], rb_g[b2] = ma2, mb2, mc2
                    x["masks"] = (ma, ma2, mb, mb2, mc2)
                    x["y_tile"], x["y_alt"] = w["gt"], w["y1"]
                    x["g_tile"], x["g_alt"] = w["gm"], w["g1"]
                    x["m_tile"], x["m_alt"] = w["m0"], w["m1"]
                    x["y_tok"], x["g_tok"], x["m_tok"] = ma, mc2, m0t

                for lev in range(1, 7):
                    for h in H8:
                        x = X[h]
                        bk_ = nxt_bank()
                        gt2 = p.do(TENS, lambda e, yt=x["y_tile"], gtl=x["g_tile"],
                                   bk_=bk_: e.matmul(
                            pr[:, bk_ * 512:bk_ * 512 + 128], yt[:], gtl[:],
                            start=True, stop=True),
                            after=[x["y_tok"], x["g_tok"], rb_g[bk_]])
                        ge = ev_step(x["eng"], x["g_alt"][:],
                                     pr[:, bk_ * 512:bk_ * 512 + 128], [gt2])
                        rb_g[bk_] = ge
                        x["g_new_tok"] = ge
                    if lev < 6:
                        for h in H8:
                            x = X[h]
                            bk_ = nxt_bank()
                            yt2 = p.do(TENS, lambda e, yt=x["y_tile"], gtl=x["g_tile"],
                                       bk_=bk_: e.matmul(
                                pr[:, bk_ * 512:bk_ * 512 + 128], gtl[:], yt[:],
                                start=True, stop=True),
                                after=[x["y_tok"], x["g_tok"], rb_g[bk_]])
                            ye = ev_step(x["eng"], x["y_alt"][:],
                                         pr[:, bk_ * 512:bk_ * 512 + 128], [yt2])
                            rb_g[bk_] = ye
                            x["y_new_tok"] = ye
                    for h in H8:
                        x = X[h]
                        bk_ = nxt_bank()
                        mm1 = p.do(TENS, lambda e, mt_=x["m_tile"], ga=x["g_alt"],
                                   bk_=bk_: [
                            e.matmul(pr[:, bk_ * 512:bk_ * 512 + 128], identb[:],
                                     mt_[:], start=True, stop=False),
                            e.matmul(pr[:, bk_ * 512:bk_ * 512 + 128], ga[:],
                                     mt_[:], start=False, stop=True)][-1],
                            after=[x["m_tok"], x["g_new_tok"], rb_g[bk_]])
                        me = ev_step(x["eng"], x["m_alt"][:],
                                     pr[:, bk_ * 512:bk_ * 512 + 128], [mm1])
                        rb_g[bk_] = me
                        x["m_tok"] = me
                        x["m_tile"], x["m_alt"] = x["m_alt"], x["m_tile"]
                        if lev < 6:
                            x["y_tile"], x["y_alt"] = x["y_alt"], x["y_tile"]
                            x["y_tok"] = x["y_new_tok"]
                        x["g_tile"], x["g_alt"] = x["g_alt"], x["g_tile"]
                        x["g_tok"] = x["g_new_tok"]

                for h in H8:
                    x = X[h]
                    bk_ = nxt_bank()
                    cmm = p.do(TENS, lambda e, x=x, bk_=bk_: [
                        e.matmul(pr[:, bk_ * 512:bk_ * 512 + 64], x["a"], x["stT"],
                                 start=True, stop=False),
                        e.matmul(pr[:, bk_ * 512:bk_ * 512 + 64], x["w"]["ht"][:],
                                 x["Vtm"], start=False, stop=True)][-1],
                        after=[rb_g[bk_], x["masks"][2],
                               prev_cc["state"][h] if prev_cc["state"][h] else prev_cc["state"][0] or st0])
                    cev = ev_step(x["eng"], x["w"]["cb"][:, 0:64],
                                  pr[:, bk_ * 512:bk_ * 512 + 64], [cmm])
                    rb_g[bk_] = cev
                    x["cmm"], x["cev"] = cmm, cev
                for h in H8:
                    x = X[h]
                    bk_ = nxt_bank()
                    umm = p.do(TENS, lambda e, mt_=x["m_tile"], cb_=x["w"]["cb"],
                               bk_=bk_: e.matmul(
                        pr[:, bk_ * 512:bk_ * 512 + 64], mt_[:],
                        cb_[:, 0:64], start=True, stop=True),
                        after=[x["m_tok"], x["cev"], rb_g[bk_]])
                    uev = ev_step(x["eng"], x["w"]["ub"][:, 0:64],
                                  pr[:, bk_ * 512:bk_ * 512 + 64], [umm])
                    rb_g[bk_] = uev
                    x["umm"], x["uev"] = umm, uev
                for h in H8:
                    x = X[h]
                    bk_ = nxt_bank()
                    ymm = p.do(TENS, lambda e, x=x, bk_=bk_: [
                        e.matmul(pr[0:64, bk_ * 512:bk_ * 512 + 128], x["stT"], x["r"],
                                 start=True, stop=False),
                        e.matmul(pr[0:64, bk_ * 512:bk_ * 512 + 128], x["w"]["ub"][:, 0:64],
                                 x["w"]["pt"][:], start=False, stop=False),
                        e.matmul(pr[0:64, bk_ * 512:bk_ * 512 + 128], x["Vtm"],
                                 x["w"]["qt"][:], start=False, stop=True)][-1],
                        after=[x["uev"], x["masks"][1], x["masks"][3], rb_g[bk_]])
                    yev = p.do(VEC, lambda e, x=x, bk_=bk_: e.tensor_copy(
                        out=ysb[x["po"]:x["po"] + 64, x["fc"] * 128:x["fc"] * 128 + 128],
                        in_=pr[0:64, bk_ * 512:bk_ * 512 + 128]),
                        after=[ymm, prev_cc["ysp"]])
                    rb_g[bk_] = yev
                    x["ymm"] = ymm
                    ysb_toks.append(yev)
                for h in H8:
                    x = X[h]
                    bk_ = nxt_bank()
                    smm = p.do(TENS, lambda e, x=x, bk_=bk_: [
                        e.matmul(pr[0:64, bk_ * 512:bk_ * 512 + 64],
                                 identb[x["po"]:x["po"] + 64, x["po"]:x["po"] + 64],
                                 x["stT"], start=True, stop=False),
                        e.matmul(pr[0:64, bk_ * 512:bk_ * 512 + 64], x["Btm"],
                                 x["w"]["ub"][:, 0:64], start=False, stop=False),
                        e.matmul(pr[0:64, bk_ * 512:bk_ * 512 + 64], x["Ktm"],
                                 x["Vtm"], start=False, stop=True)][-1],
                        after=[x["uev"], x["ymm"], x["cmm"], rb_g[bk_]])
                    sev = p.do(VEC, lambda e, x=x, bk_=bk_: e.tensor_scalar(
                        out=x["stT"], in0=pr[0:64, bk_ * 512:bk_ * 512 + 64],
                        scalar1=x["lamL"], scalar2=None, op0=ALU.mult), after=[smm])
                    rb_g[bk_] = sev
                    head_toks[h] = sev
            prev_cc["state"] = head_toks
            prev_cc["state"] = head_toks

            ysp = p.dma(SYNC, pfx + "ry", lambda e, cc=cc: e.dma_start(
                out=y_hbm[cc], in_=ysb[:]), after=ysb_toks)
            prev_cc["ysp"] = ysp
            prev_cc["tens"] = Tok(TENS, p.ecnt[TENS])
            prev_cc["vec"] = Tok(VEC, p.ecnt[VEC])
            prev_cc["scal"] = Tok(SCAL, p.ecnt[SCAL])

    with b.phase(pfx + "e"):
        vecs = vecs_keep
        obuf = b.sb(pfx + "ob", [128, NCH * T], BF16)
        yt = b.sb(pfx + "yt", [128, T], BF16)
        gt_ = b.sb(pfx + "gt", [128, T], BF16)
        bot = b.sb(pfx + "bot", [128, T], BF16)
        wslab2 = b.sb(pfx + "ws2", [128, 2 * 2048], BF16)
        xo = b.sb(pfx + "xo", [128, T], F32)
        xrt = b.sb(pfx + "xr", [128, T], F32)
        pe_ = b.ps(pfx + "pe", [128, 3072], F32)
        eg = {"done": None}
        ob_toks = []
        for c in range(NCH):
            ly = p.dma(SYNC, pfx + "ey", lambda e, c=c: e.dma_start(
                out=yt[:].rearrange("p (n t) -> p n t", t=128),
                in_=y_hbm.rearrange("n p f -> p n f")[:, :, c * 128:(c + 1) * 128]),
                after=[eg["done"]])
            lg_ = p.dma(SYNC, pfx + "eg", lambda e, c=c: e.dma_start(
                out=gt_[:], in_=streams["gg"][c * 128:(c + 1) * 128, :]),
                after=[eg["done"]])
            lb = p.dma(SYNC, pfx + "eb", lambda e, c=c: e.dma_start(
                out=bot[:], in_=streams["bo"][c * 128:(c + 1) * 128, :]),
                after=[eg["done"]])
            t1 = p.do(VEC, lambda e, c=c: e.tensor_scalar(
                out=yt[:], in0=yt[:], scalar1=vecs[:, c * 13 + 11:c * 13 + 12],
                scalar2=vecs[:, c * 13 + 12:c * 13 + 13], op0=ALU.mult, op1=ALU.add),
                after=[ly, eg["done"]])
            t2 = p.do(VEC, lambda e: e.tensor_tensor(out=yt[:], in0=yt[:], in1=bot[:],
                                                     op=ALU.add), after=[t1, lb])
            t3 = p.do(VEC, lambda e, c=c: e.tensor_tensor(
                out=obuf[:, c * T:(c + 1) * T], in0=yt[:], in1=gt_[:], op=ALU.mult),
                after=[t2, lg_])
            eg["done"] = t3
            ob_toks.append(t3)
        wuser3 = {0: None, 1: None}
        bank_g3 = {i: None for i in range(6)}
        xrt_g = None
        xo_g = None
        for m in range(NCH):
            par = m % 2
            wt = p.dma(POOL, pfx + "ew%d" % par,
                       lambda e, m=m, par=par: e.dma_start(
                           out=wslab2[:, par * 2048:(par + 1) * 2048],
                           in_=W["a_wo"][m]), after=[wuser3[par]])
            lt = p.dma(SYNC, pfx + "erx",
                       lambda e, m=m: e.dma_start(
                           out=xrt[:], in_=xsrc[m * 128:(m + 1) * 128, :]), after=[xrt_g])
            evs = []
            for nb in range(T // 512):
                bk = nb % 2
                mt = p.do(TENS, lambda e, m=m, par=par, nb=nb: [
                    e.matmul(pe_[:, (nb % 2) * 512:(nb % 2) * 512 + 512],
                             wslab2[:, par * 2048 + kc * 128:par * 2048 + kc * 128 + 128],
                             obuf[:, kc * T + nb * 512:kc * T + nb * 512 + 512],
                             start=(kc == 0), stop=(kc == NCH - 1))
                    for kc in range(NCH)][-1],
                    after=[wt, bank_g3[bk]] + (ob_toks if m == 0 and nb == 0 else []))
                wuser3[par] = mt
                ev = p.do(SCAL, lambda e, nb=nb: e.activation(
                    out=xo[:, nb * 512:nb * 512 + 512],
                    in_=pe_[:, (nb % 2) * 512:(nb % 2) * 512 + 512], func=AF.Identity),
                    after=[mt, xo_g])
                bank_g3[bk] = ev
                ad = p.do(VEC, lambda e, nb=nb: e.tensor_tensor(
                    out=xo[:, nb * 512:nb * 512 + 512], in0=xo[:, nb * 512:nb * 512 + 512],
                    in1=xrt[:, nb * 512:nb * 512 + 512], op=ALU.add), after=[ev, lt])
                evs.append(ad)
            st = p.dma(SYNC, pfx + "esx",
                       lambda e, m=m: e.dma_start(
                           out=dst[m * 128:(m + 1) * 128, :], in_=xo[:]), after=evs)
            xrt_g = st
            xo_g = st


def build_attna_only():
    b = Build()
    xin = b.dram_in("x_in", [D, T], F32)
    out = b.dram_out("x_out", [D, T], F32)
    wshapes = {"a_wr": (16, 128, 2048), "a_wk": (16, 128, 2048), "a_wv": (16, 128, 2048),
               "a_wo": (16, 128, 2048), "a_w1": (128, 1024), "a_a1": (128, 1024),
               "a_g1": (128, 2048), "a_w2": (64, 2048), "a_a2": (64, 2048),
               "a_g2": (128, 2048), "a_vecs": (NCH, 128, 13)}
    W = {k: b.dram_in(k, list(s), F32) for k, s in wshapes.items()}
    emit_attn_a(b, xin, out, W, "r1")
    with b.nc.Block() as blk:
        b.p.emit(blk)
        @blk.sync
        def _(eng):
            for name, rec in b.p.dsems.items():
                eng.wait_ge(rec[0], rec[1])
    return b


# version 05
# speedup vs baseline: 1.0380x; 1.0179x over previous
import contextlib
import numpy as np
import concourse.bass as bass
import concourse.mybir as mybir
from concourse.bass_utils import run_bass_kernel_spmd

F32 = mybir.dt.float32
BF16 = mybir.dt.bfloat16
AF = mybir.ActivationFunctionType
ALU = mybir.AluOpType
AX = mybir.AxisListType

D = 2048
T = 2048
NCH = 16
NTT = 16
NCORES = 8

E = 16
FI = 1408
SI = 2816
TOPK = 6
TBF = 1024
NBF = T // TBF

HA, NA, LCH = 32, 64, 128
HB, NOPE, ROPE, VHD, QKD = 16, 128, 64, 128, 192

DEBUG_TAPS = False
TENS, VEC, SCAL, POOL, SYNC = "tensor", "vector", "scalar", "gpsimd", "sync"
ENGINES = [TENS, VEC, SCAL, POOL, SYNC]



def _relayout(W):
    K, M = W.shape
    assert K % 128 == 0 and M % 128 == 0, (K, M)
    nK, nM = K // 128, M // 128
    return np.ascontiguousarray(
        W.reshape(nK, 128, nM, 128).transpose(2, 1, 0, 3).reshape(nM, 128, nK * 128))


def _pad(W, k=None, m=None):
    K, M = W.shape
    k, m = k or K, m or M
    if (k, m) == (K, M):
        return np.ascontiguousarray(W)
    out = np.zeros((k, m), W.dtype)
    out[:K, :M] = W
    return out



class Tok:
    __slots__ = ("eng", "val", "sem")
    def __init__(self, eng, val, sem=None):
        self.eng, self.val, self.sem = eng, val, sem


class Prog:
    def __init__(self, nc, ctx):
        self.nc = nc
        self.ctx = ctx
        self.steps = []
        self.esem = {e: ctx.enter_context(nc.semaphore("es_" + e)) for e in ENGINES}
        self.ecnt = {e: 0 for e in ENGINES}
        self.dsems = {}
        self.guard = []

    def _dsem(self, name):
        if name not in self.dsems:
            self.dsems[name] = [self.ctx.enter_context(self.nc.semaphore("ds_" + name)), 0]
        return self.dsems[name]

    def do(self, eng, fn, after=()):
        self.ecnt[eng] += 1
        tok = Tok(eng, self.ecnt[eng])
        deps = tuple(t for t in after if t is not None) + tuple(self.guard)
        self.steps.append((eng, fn, deps, None))
        return tok

    def dma(self, eng, sem_name, fn, after=()):
        rec = self._dsem(sem_name)
        prev = rec[1]
        rec[1] += 16
        tok = Tok(None, rec[1], sem_name)
        deps = tuple(t for t in after if t is not None) + tuple(self.guard)
        if prev > 0:
            deps = deps + (Tok(None, prev, sem_name),)
        self.steps.append((eng, fn, deps, sem_name))
        return tok

    def set_guard(self):
        self.guard = [Tok(e, self.ecnt[e]) for e in ENGINES if self.ecnt[e] > 0] + \
                     [Tok(None, rec[1], n) for n, rec in self.dsems.items()]

    def emit(self, block):
        seen = {e: {} for e in ENGINES}

        def emit_for(eng):
            def body(E_):
                for s_eng, fn, deps, dsem in self.steps:
                    if s_eng != eng:
                        continue
                    for d in deps:
                        if d.sem is None:
                            if d.eng == eng and d.val >= self.ecnt[eng]:
                                continue
                            tgt = self.esem[d.eng]
                        else:
                            tgt = self.dsems[d.sem][0]
                        key = id(tgt)
                        if seen[eng].get(key, -1) < d.val:
                            E_.wait_ge(tgt, d.val)
                            seen[eng][key] = d.val
                    instr = fn(E_)
                    if dsem is not None:
                        instr.then_inc(self.dsems[dsem][0], 16)
                    else:
                        instr.then_inc(self.esem[eng], 1)
            return body

        used = {s[0] for s in self.steps}
        for eng in ENGINES:
            if eng in used:
                getattr(block, eng)(emit_for(eng))

    def n_steps(self):
        return len(self.steps)



class Build:
    def __init__(self):
        self.ctx = contextlib.ExitStack()
        self.nc = bass.Bass("TRN2", target_bir_lowering=False)
        self.p = Prog(self.nc, self.ctx)
        self.consts = {}
        self.const_ap = {}
        self.phase_ctx = None
        self.com = {}

    def dram_in(self, name, shape, dtype=F32):
        return self.nc.dram_tensor(name, list(shape), dtype, kind="ExternalInput").ap()

    def dram_out(self, name, shape, dtype=F32):
        return self.nc.dram_tensor(name, list(shape), dtype, kind="ExternalOutput").ap()

    def dram_tmp(self, name, shape, dtype=F32):
        return self.nc.dram_tensor(name, list(shape), dtype, kind="Internal").ap()

    def sb(self, name, shape, dtype):
        stk = self.phase_ctx if self.phase_ctx is not None else self.ctx
        return stk.enter_context(self.nc.sbuf_tensor(name, list(shape), dtype))

    def sbp(self, name, shape, dtype):
        return self.ctx.enter_context(
            self.nc.sbuf_tensor(name, list(shape), dtype, side="right"))

    def ps(self, name, shape, dtype=F32):
        stk = self.phase_ctx if self.phase_ctx is not None else self.ctx
        return stk.enter_context(self.nc.psum_tensor(name, list(shape), dtype))

    def const(self, name, arr):
        if name not in self.const_ap:
            self.consts["cst_" + name] = np.ascontiguousarray(arr)
            dt = {"float32": F32, "bfloat16": BF16}[str(arr.dtype)]
            self.const_ap[name] = self.dram_in("cst_" + name, arr.shape, dt)
        return self.const_ap[name]

    def const_tile(self, name, arr):
        key = "ct_" + name
        if key not in self.com:
            ap = self.const(name, arr)
            dt = {"float32": F32, "bfloat16": BF16}[str(arr.dtype)]
            sbt = self.ctx.enter_context(
                self.nc.sbuf_tensor(key, list(arr.shape), dt, side="right"))
            tok = self.p.dma(SYNC, "cst",
                             lambda e, ap=ap, sbt=sbt: e.dma_start(out=sbt[:], in_=ap))
            self.p.guard = list(self.p.guard) + [tok]
            self.com[key] = sbt
        return self.com[key]

    @contextlib.contextmanager
    def phase(self, name):
        assert self.phase_ctx is None
        self.phase_ctx = contextlib.ExitStack()
        try:
            yield
        finally:
            self.phase_ctx.close()
            self.phase_ctx = None
            self.p.set_guard()


def bf16c(b):
    import ml_dtypes
    return ml_dtypes.bfloat16



def emit_ln(b, xsrc, xln, pfx):
    p = b.p
    bf = bf16c(b)
    ones_b = b.const_tile("ones128_b", np.ones((128, 1), bf))
    ones_f = b.const_tile("ones128_f", np.ones((1, 128), np.float32))
    assert b.phase_ctx is None, "emit_ln manages its own phases"
    outer = contextlib.ExitStack()
    r1 = outer.enter_context(b.nc.sbuf_tensor(pfx + "r1", [1, 2048], F32))
    r2 = outer.enter_context(b.nc.sbuf_tensor(pfx + "r2", [1, 2048], F32))
    b.phase_ctx = contextlib.ExitStack()
    stats = b.ps(pfx + "st", [128, 4096], F32)
    xsq = b.sb(pfx + "xsq", [128, 1024], BF16)

    ld = [p.dma(POOL, pfx + "x%d" % (c % 2),
                lambda e, c=c: e.dma_start(out=xln[:, c * T:(c + 1) * T],
                                           in_=xsrc[c * 128:(c + 1) * 128, :]))
          for c in range(NCH)]

    NNB = T // 512
    sq, mmt = {}, []
    for c in range(NCH):
        for nb in range(NNB):
            i = c * NNB + nb
            deps = [ld[c]] + ([mmt[i - 2]] if i >= 2 else [])
            sq[i] = p.do(SCAL, lambda e, c=c, nb=nb, i=i: e.activation(
                out=xsq[:, (i % 2) * 512:(i % 2) * 512 + 512],
                in_=xln[:, c * T + nb * 512:c * T + nb * 512 + 512],
                func=AF.Square), after=deps)
            def mm(e, c=c, nb=nb, i=i):
                e.matmul(stats[0:1, nb * 512:nb * 512 + 512], ones_b[:],
                         xln[:, c * T + nb * 512:c * T + nb * 512 + 512],
                         start=(c == 0), stop=(c == NCH - 1))
                return e.matmul(stats[0:1, 2048 + nb * 512:2048 + nb * 512 + 512], ones_b[:],
                                xsq[:, (i % 2) * 512:(i % 2) * 512 + 512],
                                start=(c == 0), stop=(c == NCH - 1))
            mmt.append(p.do(TENS, mm, after=[ld[c], sq[i]]))

    t0 = None
    for base, dstr in ((0, r1), (2048, r2)):
        for q in range(T // 512):
            t0 = p.do(VEC, lambda e, q=q, base=base, dstr=dstr: e.tensor_copy(
                out=dstr[0:1, q * 512:(q + 1) * 512],
                in_=stats[0:1, base + q * 512:base + (q + 1) * 512]),
                after=[mmt[-1]])
    t1 = p.do(VEC, lambda e: e.tensor_scalar(out=r1[0:1, 0:T], in0=r1[0:1, 0:T],
                                             scalar1=1.0 / D, scalar2=None, op0=ALU.mult),
              after=[t0])
    t2 = p.do(VEC, lambda e: e.tensor_scalar(out=r2[0:1, 0:T], in0=r2[0:1, 0:T],
                                             scalar1=1.0 / D, scalar2=None, op0=ALU.mult),
              after=[t0])
    t3 = p.do(VEC, lambda e: e.tensor_tensor(out=stats[0:1, 0:T], in0=r1[0:1, 0:T],
                                             in1=r1[0:1, 0:T], op=ALU.mult), after=[t1])
    t4 = p.do(VEC, lambda e: e.tensor_tensor(out=r2[0:1, 0:T], in0=r2[0:1, 0:T],
                                             in1=stats[0:1, 0:T], op=ALU.subtract),
              after=[t2, t3])
    t4b = p.do(VEC, lambda e: e.tensor_scalar(out=r2[0:1, 0:T], in0=r2[0:1, 0:T],
                                              scalar1=1e-5, scalar2=None, op0=ALU.add),
               after=[t4])
    t5 = p.do(SCAL, lambda e: e.activation(out=stats[0:1, 0:T], in_=r2[0:1, 0:T],
                                           func=AF.Sqrt), after=[t4b])
    t6 = p.do(VEC, lambda e: e.reciprocal(out=r2[0:1, 0:T], in_=stats[0:1, 0:T]),
              after=[t5])
    t7 = p.do(VEC, lambda e: e.tensor_tensor(out=r1[0:1, 0:T], in0=r1[0:1, 0:T],
                                             in1=r2[0:1, 0:T], op=ALU.mult), after=[t6])
    t8 = p.do(VEC, lambda e: e.tensor_scalar(out=r1[0:1, 0:T], in0=r1[0:1, 0:T],
                                             scalar1=-1.0, scalar2=None, op0=ALU.mult),
              after=[t7])
    t9 = t8
    b.phase_ctx.close(); b.phase_ctx = None; p.set_guard()
    b.phase_ctx = contextlib.ExitStack()
    bcAB = b.ps(pfx + "bc", [128, 4096], F32)

    bc = []
    for nb in range(T // 512):
        def mm(e, nb=nb):
            e.matmul(bcAB[:, nb * 512:nb * 512 + 512], ones_f[:],
                     r2[0:1, nb * 512:nb * 512 + 512], start=True, stop=True)
            return e.matmul(bcAB[:, 2048 + nb * 512:2048 + nb * 512 + 512], ones_f[:],
                            r1[0:1, nb * 512:nb * 512 + 512],
                            start=True, stop=True)
        bc.append(p.do(TENS, mm, after=[t8, t9]))

    out_toks = []
    for c in range(NCH):
        for nb in range(T // 512):
            sl = slice(c * T + nb * 512, c * T + nb * 512 + 512)
            a = p.do(VEC, lambda e, sl=sl, nb=nb: e.tensor_tensor(
                out=xln[:, sl], in0=xln[:, sl],
                in1=bcAB[:, nb * 512:nb * 512 + 512], op=ALU.mult), after=[bc[nb], mmt[-1]])
            out_toks.append(p.do(VEC, lambda e, sl=sl, nb=nb: e.tensor_tensor(
                out=xln[:, sl], in0=xln[:, sl],
                in1=bcAB[:, 2048 + nb * 512:2048 + nb * 512 + 512], op=ALU.add), after=[a]))
    b.phase_ctx.close(); b.phase_ctx = None
    outer.close()
    p.set_guard()
    return out_toks



def emit_ffn(b, xsrc, dst, W, pfx):
    p = b.p
    bf = bf16c(b)
    ffn_outer = contextlib.ExitStack()
    cT = ffn_outer.enter_context(b.nc.sbuf_tensor(pfx + "cT", [16, 2048], F32))
    if "xln" not in b.com:
        b.com["xln"] = b.sbp("g_xln", [128, NCH * T], BF16)
    xln = b.com["xln"]
    ln_done = emit_ln(b, xsrc, xln, pfx + "l")

    esel_np = np.zeros((16, 16 * 128), np.float32)
    for e_ in range(16):
        esel_np[e_, e_ * 128:(e_ + 1) * 128] = 1.0
    with b.phase(pfx + "gate"):
        gwT = b.sb(pfx + "gw", [128, NCH * 16], BF16)
        ldg = p.dma(POOL, pfx + "gw", lambda e: e.dma_start(out=gwT[:], in_=W["f_gate"]))
        logit = b.ps(pfx + "lg", [128, 2048], F32)
        cps = b.ps(pfx + "cps", [128, 1024], F32)
        comb = b.sb(pfx + "cmb", [128, NTT * 16], F32)
        srow = b.sb(pfx + "srw", [128, 48], F32)
        mrow = b.sb(pfx + "mrw", [128, 2], F32)
        identf = b.const_tile("identf", np.eye(128, dtype=np.float32))

        slot_free = {}
        comb_done = {}
        for tt in range(NTT):
            deps = [ldg] + ([slot_free[tt - 4]] if tt >= 4 else [])
            def mm(e, tt=tt):
                for kc in range(NCH):
                    last = e.matmul(logit[:, (tt % 4) * 512:(tt % 4) * 512 + 16],
                                    xln[:, kc * T + tt * 128:kc * T + tt * 128 + 128],
                                    gwT[:, kc * 16:(kc + 1) * 16],
                                    start=(kc == 0), stop=(kc == NCH - 1))
                return last
            g0 = p.do(TENS, mm, after=deps)
            sl = logit[:, (tt % 4) * 512:(tt % 4) * 512 + 16]
            a0 = p.do(VEC, lambda e, sl=sl: e.reduce_max(out=mrow[:, 0:1], in_=sl, axis=AX.X),
                      after=[g0])
            a1 = p.do(VEC, lambda e: e.tensor_scalar(out=mrow[:, 1:2], in0=mrow[:, 0:1],
                                                     scalar1=-1.0, scalar2=None, op0=ALU.mult),
                      after=[a0])
            a2 = p.do(SCAL, lambda e, sl=sl: e.activation(out=srow[:, 0:16], in_=sl,
                                                          func=AF.Exp, bias=mrow[:, 1:2]),
                      after=[a1])
            slot_free[tt] = a2
            a3 = p.do(VEC, lambda e: e.reduce_sum(out=mrow[:, 0:1], in_=srow[:, 0:16],
                                                  axis=AX.X), after=[a2])
            a4 = p.do(VEC, lambda e: e.reciprocal(out=mrow[:, 1:2], in_=mrow[:, 0:1]),
                      after=[a3])
            cur = p.do(VEC, lambda e: e.tensor_scalar(out=srow[:, 0:16], in0=srow[:, 0:16],
                                                      scalar1=mrow[:, 1:2], scalar2=None,
                                                      op0=ALU.mult), after=[a4])
            cur = p.do(VEC, lambda e: e.tensor_copy(out=srow[:, 16:32], in_=srow[:, 0:16]),
                       after=[cur])
            for it in range(TOPK - 1):
                m0 = p.do(VEC, lambda e: e.reduce_max(out=mrow[:, 0:1], in_=srow[:, 16:32],
                                                      axis=AX.X), after=[cur])
                m1 = p.do(VEC, lambda e: e.tensor_scalar(out=srow[:, 32:48],
                                                         in0=srow[:, 16:32],
                                                         scalar1=mrow[:, 0:1], scalar2=None,
                                                         op0=ALU.is_lt), after=[m0])
                cur = p.do(VEC, lambda e: e.tensor_tensor(out=srow[:, 16:32],
                                                          in0=srow[:, 16:32],
                                                          in1=srow[:, 32:48], op=ALU.mult),
                           after=[m1])
            m0 = p.do(VEC, lambda e: e.reduce_max(out=mrow[:, 0:1], in_=srow[:, 16:32],
                                                  axis=AX.X), after=[cur])
            m1 = p.do(VEC, lambda e: e.tensor_scalar(out=srow[:, 32:48], in0=srow[:, 0:16],
                                                     scalar1=mrow[:, 0:1], scalar2=None,
                                                     op0=ALU.is_ge), after=[m0])
            m2 = p.do(VEC, lambda e: e.tensor_tensor(out=srow[:, 16:32], in0=srow[:, 0:16],
                                                     in1=srow[:, 32:48], op=ALU.mult),
                      after=[m1])
            m3 = p.do(VEC, lambda e: e.reduce_sum(out=mrow[:, 0:1], in_=srow[:, 16:32],
                                                  axis=AX.X), after=[m2])
            m4 = p.do(VEC, lambda e: e.tensor_scalar(out=mrow[:, 0:1], in0=mrow[:, 0:1],
                                                     scalar1=1e-20, scalar2=None, op0=ALU.add),
                      after=[m3])
            m5 = p.do(VEC, lambda e: e.reciprocal(out=mrow[:, 1:2], in_=mrow[:, 0:1]),
                      after=[m4])
            comb_done[tt] = p.do(VEC, lambda e, tt=tt: e.tensor_scalar(
                out=comb[:, tt * 16:(tt + 1) * 16], in0=srow[:, 16:32],
                scalar1=mrow[:, 1:2], scalar2=None, op0=ALU.mult), after=[m5])

        prev_par = {}
        for tt in range(NTT):
            def tr(e, tt=tt):
                return e.transpose(cps[0:16, (tt % 2) * 512:(tt % 2) * 512 + 128],
                                   comb[:, tt * 16:(tt + 1) * 16], identf[:])
            trt = p.do(TENS, tr, after=[comb_done[tt], prev_par.get(tt % 2)])
            prev_par[tt % 2] = p.do(VEC, lambda e, tt=tt: e.tensor_copy(
                out=cT[:, tt * 128:(tt + 1) * 128],
                in_=cps[0:16, (tt % 2) * 512:(tt % 2) * 512 + 128]), after=[trt])

    with b.phase(pfx + "ex"):
        acc = b.sb(pfx + "acc", [128, NCH * TBF], F32)
        nsil = max(11 * TBF, 22 * 512)
        sil = b.sb(pfx + "sil", [128, 2 * TBF], BF16)
        hbuf = b.sb(pfx + "h", [128, nsil], BF16)
        w13b = b.sb(pfx + "w13", [128, 2 * 2048], BF16)
        w2b = b.sb(pfx + "w2", [128, 2 * SI], BF16)
        xrt = b.sb(pfx + "xr", [128, TBF], F32)
        mps = b.ps(pfx + "mps", [128, 3072], F32)
        cbps = b.ps(pfx + "cb", [128, 1024], F32)
        eselt = b.const_tile("esel", esel_np)

        g = {
            "w13_user": {0: None, 1: None},
            "w2_user": {0: None, 1: None},
            "slot": {0: None, 1: None, 2: None},
            "cb_readers": [],
            "sil_guard": [],
            "h_guard": None,
            "acc_writer": {m: None for m in range(NCH)},
            "acc_store": {m: None for m in range(NCH)},
            "xrt": {0: None, 1: None},
        }

        def expert_pass(tb, kind, e_idx):
            if kind == "e":
                w13_src, w2_src = W["f_w13"][e_idx], W["f_w2"][e_idx]
                nm13, nk13, nm2, nk2, ni = 22, NCH, 16, 11, 11
                toff, tlen = tb * TBF, TBF
            else:
                w13_src, w2_src = W["f_sw13"], W["f_sw2"]
                nm13, nk13, nm2, nk2, ni = 44, NCH, 16, 22, 22
                toff, tlen = tb * TBF + e_idx * 512, 512
            nnb = tlen // 512
            acc_off = toff - tb * TBF

            cb_tok = None
            if kind == "e":
                def cbmm(e):
                    for nb in range(nnb):
                        last = e.matmul(cbps[:, nb * 512:nb * 512 + 512],
                                        eselt[:, e_idx * 128:(e_idx + 1) * 128],
                                        cT[:, toff + nb * 512:toff + nb * 512 + 512],
                                        start=True, stop=True)
                    return last
                cb_tok = p.do(TENS, cbmm, after=g["cb_readers"])
                g["cb_readers"] = []

            mm13, ev13 = {}, {}
            sil_consumers = []
            m_order = [x for i in range(ni) for x in (i, ni + i)]
            for mi, m in enumerate(m_order):
                par, slot = mi % 2, mi % 3
                wt = p.dma(POOL, pfx + "w13p%d" % par,
                           lambda e, m=m, par=par: e.dma_start(
                               out=w13b[:, par * 2048:par * 2048 + nk13 * 128],
                               in_=w13_src[m]),
                           after=[g["w13_user"][par]])
                def mmf(e, m=m, par=par, slot=slot, nnb=nnb, nk13=nk13, toff=toff):
                    for nb in range(nnb):
                        for kc in range(nk13):
                            last = e.matmul(
                                mps[:, slot * 1024 + nb * 512:slot * 1024 + nb * 512 + 512],
                                w13b[:, par * 2048 + kc * 128:par * 2048 + kc * 128 + 128],
                                xln[:, kc * T + toff + nb * 512:
                                     kc * T + toff + nb * 512 + 512],
                                start=(kc == 0), stop=(kc == nk13 - 1))
                    return last
                mm13[m] = p.do(TENS, mmf, after=[wt, g["slot"][slot]])
                g["w13_user"][par] = mm13[m]
                if m < ni:
                    deps = [mm13[m]] + ([ev13[ni + m - 2]] if m >= 2 else g["sil_guard"])
                    for q in range(tlen // 512):
                        ev13[m] = p.do(SCAL, lambda e, m=m, slot=slot, tlen=tlen, q=q:
                            e.activation(
                                out=sil[:, (m % 2) * tlen + q * 512:
                                         (m % 2) * tlen + q * 512 + 512],
                                in_=mps[:, slot * 1024 + q * 512:slot * 1024 + q * 512 + 512],
                                func=AF.Silu),
                            after=deps)
                else:
                    gm = m - ni
                    deps = [mm13[m], ev13[gm], g["h_guard"]]
                    t1 = None
                    for q in range(tlen // 512):
                        t1 = p.do(VEC, lambda e, gm=gm, slot=slot, tlen=tlen, q=q:
                            e.tensor_tensor(
                                out=hbuf[:, gm * tlen + q * 512:gm * tlen + q * 512 + 512],
                                in0=sil[:, (gm % 2) * tlen + q * 512:
                                        (gm % 2) * tlen + q * 512 + 512],
                                in1=mps[:, slot * 1024 + q * 512:slot * 1024 + q * 512 + 512],
                                op=ALU.mult),
                            after=deps + ([t1] if t1 else []))
                        if kind == "e":
                            t1 = p.do(VEC, lambda e, gm=gm, tlen=tlen, q=q: e.tensor_tensor(
                                out=hbuf[:, gm * tlen + q * 512:gm * tlen + q * 512 + 512],
                                in0=hbuf[:, gm * tlen + q * 512:gm * tlen + q * 512 + 512],
                                in1=cbps[:, q * 512:q * 512 + 512], op=ALU.mult),
                                after=[t1, cb_tok])
                            g["cb_readers"].append(t1)
                    ev13[m] = t1
                    sil_consumers.append(t1)
                g["slot"][slot] = ev13[m]

            h_all = [ev13[ni + i] for i in range(ni)]
            mm2, ev2 = {}, {}
            for m in range(nm2):
                par, slot = m % 2, m % 3
                wt = p.dma(POOL, pfx + "w2p%d" % par,
                           lambda e, m=m, par=par, nk2=nk2: e.dma_start(
                               out=w2b[:, par * SI:par * SI + nk2 * 128],
                               in_=w2_src[m]),
                           after=[g["w2_user"][par]])
                deps = [wt, g["slot"][slot]] + (h_all if m == 0 else [])
                def mmf(e, m=m, par=par, slot=slot, nnb=nnb, nk2=nk2, tlen=tlen):
                    for nb in range(nnb):
                        for kc in range(nk2):
                            last = e.matmul(
                                mps[:, slot * 1024 + nb * 512:slot * 1024 + nb * 512 + 512],
                                w2b[:, par * SI + kc * 128:par * SI + kc * 128 + 128],
                                hbuf[:, kc * tlen + nb * 512:kc * tlen + nb * 512 + 512],
                                start=(kc == 0), stop=(kc == nk2 - 1))
                    return last
                mm2[m] = p.do(TENS, mmf, after=deps)
                g["w2_user"][par] = mm2[m]
                first = (kind == "e" and e_idx == 0)
                evt = None
                for q in range(tlen // 512):
                    def evf(e, m=m, slot=slot, first=first, acc_off=acc_off, q=q):
                        dst_sl = acc[:, m * TBF + acc_off + q * 512:
                                     m * TBF + acc_off + q * 512 + 512]
                        src_sl = mps[:, slot * 1024 + q * 512:slot * 1024 + q * 512 + 512]
                        if first:
                            return e.tensor_copy(out=dst_sl, in_=src_sl)
                        return e.tensor_tensor(out=dst_sl, in0=dst_sl, in1=src_sl,
                                               op=ALU.add)
                    evt = p.do(VEC, evf,
                               after=[mm2[m], g["acc_writer"][m], g["acc_store"][m]] +
                                     ([evt] if evt else []))
                ev2[m] = evt
                g["acc_writer"][m] = ev2[m]
                g["acc_store"][m] = None
                g["slot"][slot] = ev2[m]
            g["sil_guard"] = sil_consumers
            g["h_guard"] = mm2[nm2 - 1]

        for tb in range(NBF):
            for e_idx in range(E):
                expert_pass(tb, "e", e_idx)
            for sb_i in range(TBF // 512):
                expert_pass(tb, "s", sb_i)
            for c in range(NCH):
                par = c % 2
                lt = p.dma(SYNC, pfx + "rx",
                           lambda e, c=c, tb=tb: e.dma_start(
                               out=xrt[:, :],
                               in_=xsrc[c * 128:(c + 1) * 128, tb * TBF:(tb + 1) * TBF]),
                           after=[g["xrt"][0]])
                adt = None
                for q in range(TBF // 512):
                    adt = p.do(VEC, lambda e, c=c, q=q: e.tensor_tensor(
                        out=acc[:, c * TBF + q * 512:c * TBF + q * 512 + 512],
                        in0=acc[:, c * TBF + q * 512:c * TBF + q * 512 + 512],
                        in1=xrt[:, q * 512:q * 512 + 512], op=ALU.add),
                        after=[lt, g["acc_writer"][c]] + ([adt] if adt else []))
                ad = adt
                g["xrt"][0] = ad
                st = p.dma(SYNC, pfx + "sx%d" % par,
                           lambda e, c=c, tb=tb: e.dma_start(
                               out=dst[c * 128:(c + 1) * 128, tb * TBF:(tb + 1) * TBF],
                               in_=acc[:, c * TBF:(c + 1) * TBF]), after=[ad])
                g["acc_writer"][c] = None
                g["acc_store"][c] = st



def prep_ffn_weights(I):
    f32 = np.float32
    w13 = np.asarray(I["L0_ffn_w13"], f32)
    w2 = np.asarray(I["L0_ffn_w2"], f32)
    return {
        "f_w13": np.stack([_relayout(np.ascontiguousarray(w13[e].T)) for e in range(E)]),
        "f_w2": np.stack([_relayout(np.ascontiguousarray(w2[e].T)) for e in range(E)]),
        "f_sw13": _relayout(np.ascontiguousarray(np.asarray(I["L0_ffn_sw13"], f32).T)),
        "f_sw2": _relayout(np.ascontiguousarray(np.asarray(I["L0_ffn_sw2"], f32).T)),
        "f_gate": np.ascontiguousarray(np.asarray(I["L0_ffn_gate_w"], f32).T
                                       .reshape(NCH, 128, 16).transpose(1, 0, 2)
                                       .reshape(128, NCH * 16)),
    }


def declare_ffn_weights(b, wnp):
    return {k: b.dram_in(k, v.shape, F32) for k, v in wnp.items()}


def build_ffn_only():
    b = Build()
    xin = b.dram_in("x_in", [D, T], F32)
    out = b.dram_out("x_out", [D, T], F32)
    wshapes = {
        "f_w13": (E, 22, 128, 2048), "f_w2": (E, 16, 128, FI),
        "f_sw13": (44, 128, 2048), "f_sw2": (16, 128, SI), "f_gate": (128, NCH * 16),
    }
    W = {k: b.dram_in(k, list(s), F32) for k, s in wshapes.items()}
    emit_ffn(b, xin, out, W, "f1")
    with b.nc.Block() as blk:
        b.p.emit(blk)
        @blk.sync
        def _(eng):
            for name, rec in b.p.dsems.items():
                eng.wait_ge(rec[0], rec[1])
    return b


def run_ffn_only(x_fm, wnp, cores=1):
    b = build_ffn_only()
    inmap = {"x_in": x_fm, **{k: wnp[k] for k in
                              ["f_w13", "f_w2", "f_sw13", "f_sw2", "f_gate"]},
             **b.consts}
    nc_obj = b.nc
    res = run_bass_kernel_spmd(nc_obj, [inmap] * cores, core_ids=list(range(cores)))
    b.ctx.close()
    return res.results[0]["x_out"]



WSHAPES = {
    "f_w13": (E, 22, 128, 2048), "f_w2": (E, 16, 128, FI),
    "f_sw13": (44, 128, 2048), "f_sw2": (16, 128, SI), "f_gate": (128, NCH * 16),
    "b_wqa": (4, 128, 2048), "b_wkva": (5, 128, 2048),
    "b_wqb": (16, 128, 768), "b_wkvb": (16, 128, 1024),
    "b_wo": (16, 128, 2048), "b_bqa": (128, 4), "b_bkva": (128, 5),
    "b_bqbn": (128, 16), "b_bqbp": (64, 16), "b_bkbn": (128, 16),
    "b_bkbv": (128, 16), "b_bwo": (128, 16),
    "a_wr": (16, 128, 2048), "a_wk": (16, 128, 2048), "a_wv": (16, 128, 2048),
    "a_wo": (16, 128, 2048), "a_w1": (128, 1024), "a_a1": (128, 1024),
    "a_g1": (128, 2048), "a_w2": (64, 2048), "a_a2": (64, 2048),
    "a_g2": (128, 2048), "a_vecs": (NCH, 128, 13),
}


def build_full():
    b = Build()
    xin = b.dram_in("x_in", [D, T], F32)
    out = b.dram_out("x_out", [D, T], F32)
    x1 = b.dram_tmp("x1", [D, T], F32)
    x2 = b.dram_tmp("x2", [D, T], F32)
    x3 = b.dram_tmp("x3", [D, T], F32)
    W = {k: b.dram_in(k, list(sh), F32) for k, sh in WSHAPES.items()}
    emit_attn_a(b, xin, x1, W, "r1")
    emit_ffn(b, x1, x2, W, "f1")
    emit_attn_b(b, x2, x3, W, "a1")
    emit_ffn(b, x3, out, W, "f2")
    with b.nc.Block() as blk:
        b.p.emit(blk)
        @blk.sync
        def _(eng):
            for name, rec in b.p.dsems.items():
                eng.wait_ge(rec[0], rec[1])
    return b


def prep_all(I):
    w = {}
    w.update(prep_ffn_weights(I))
    w.update(prep_attnb_weights(I))
    w.update(prep_attna_weights(I))
    return w


def kernel(**inputs):
    x = np.asarray(inputs["L0_attn_a_x"], np.float32)
    wnp = prep_all(inputs)
    b = build_full()
    base = {**wnp, **b.consts}
    in_maps = []
    for c in range(NCORES):
        m = dict(base)
        m["x_in"] = np.ascontiguousarray(x[c].T)
        in_maps.append(m)
    res = run_bass_kernel_spmd(b.nc, in_maps, core_ids=list(range(NCORES)))
    out = np.stack([np.ascontiguousarray(res.results[c]["x_out"].T)
                    for c in range(NCORES)])
    return out.astype(np.float32)



def prep_attnb_weights(I):
    f32 = np.float32
    wq_a = np.asarray(I["L0_attn_b_wq_a_w"], f32)
    wq_b = np.asarray(I["L0_attn_b_wq_b_w"], f32)
    wkv_a = _pad(np.asarray(I["L0_attn_b_wkv_a_w"], f32), m=640)
    wkv_b = np.asarray(I["L0_attn_b_wkv_b_w"], f32)
    wo = np.asarray(I["L0_attn_b_wo_w"], f32)
    out = {
        "b_wqa": _relayout(wq_a),
        "b_wkva": _relayout(wkv_a),
        "b_wqb": np.ascontiguousarray(
            wq_b.reshape(4, 128, 16, 192).transpose(2, 1, 0, 3).reshape(16, 128, 768)),
        "b_wkvb": np.ascontiguousarray(
            wkv_b.reshape(4, 128, 16, 256).transpose(2, 1, 0, 3).reshape(16, 128, 1024)),
        "b_wo": _relayout(wo),
        "b_bqa": np.ascontiguousarray(
            np.asarray(I["L0_attn_b_wq_a_b"], f32).reshape(4, 128).T),
        "b_bkva": np.ascontiguousarray(
            _pad(np.asarray(I["L0_attn_b_wkv_a_b"], f32)[None], m=640)[0]
            .reshape(5, 128).T),
        "b_bqbn": np.ascontiguousarray(
            np.asarray(I["L0_attn_b_wq_b_b"], f32).reshape(16, 192)[:, :128].T),
        "b_bqbp": np.ascontiguousarray(
            np.asarray(I["L0_attn_b_wq_b_b"], f32).reshape(16, 192)[:, 128:].T),
        "b_bkbn": np.ascontiguousarray(
            np.asarray(I["L0_attn_b_wkv_b_b"], f32).reshape(16, 256)[:, :128].T),
        "b_bkbv": np.ascontiguousarray(
            np.asarray(I["L0_attn_b_wkv_b_b"], f32).reshape(16, 256)[:, 128:].T),
        "b_bwo": np.ascontiguousarray(
            np.asarray(I["L0_attn_b_wo_b"], f32).reshape(16, 128).T),
    }
    return out


def rope_tables_np(Tn):
    inv = 1.0 / 1e6 ** (np.arange(0, ROPE, 2, dtype=np.float64) / ROPE)
    t = np.arange(Tn, dtype=np.float64)
    fr = np.outer(t, inv)
    cos, sin = np.cos(fr), np.sin(fr)
    Cf = np.empty((ROPE, Tn)); Sf = np.empty((ROPE, Tn))
    Cf[0::2] = cos.T; Cf[1::2] = cos.T
    Sf[0::2] = -sin.T; Sf[1::2] = sin.T
    return Cf.astype(np.float32), Sf.astype(np.float32)


def emit_attn_b(b, xsrc, dst, W, pfx):
    p = b.p
    bf = bf16c(b)
    if "xln" not in b.com:
        b.com["xln"] = b.sbp("g_xln", [128, NCH * T], BF16)
    xln = b.com["xln"]
    ln_done = emit_ln(b, xsrc, xln, pfx + "l")
    ones_b = b.com["ct_ones128_b"]
    ones_f = b.com["ct_ones128_f"]

    with b.phase(pfx + "m"):
        Cf_np, Sf_np = rope_tables_np(T)
        swp = np.zeros((64, 64), bf)
        for i in range(32):
            swp[2 * i + 1, 2 * i] = 1.0
            swp[2 * i, 2 * i + 1] = 1.0
        Cf = b.sb(pfx + "Cf", [ROPE, T], BF16)
        Sf = b.sb(pfx + "Sf", [ROPE, T], BF16)
        swap64 = b.sb(pfx + "swp", [64, 64], BF16)
        c1 = p.dma(SYNC, pfx + "c", lambda e: e.dma_start(
            out=Cf[:], in_=b.const("ropeC", Cf_np.astype(bf))))
        c2 = p.dma(SYNC, pfx + "c", lambda e: e.dma_start(
            out=Sf[:], in_=b.const("ropeS", Sf_np.astype(bf))))
        c3 = p.dma(SYNC, pfx + "c", lambda e: e.dma_start(out=swap64[:], in_=b.const("swap64", swp)))
        bias_t = {}
        for nm, shp in (("b_bqa", [128, 4]), ("b_bkva", [128, 5]), ("b_bqbn", [128, 16]),
                        ("b_bqbp", [64, 16]), ("b_bkbn", [128, 16]), ("b_bkbv", [128, 16])):
            tl = b.sb(pfx + nm, shp, F32)
            bias_t[nm] = tl
            c3 = p.dma(SYNC, pfx + "c", lambda e, tl=tl, nm=nm: e.dma_start(out=tl[:], in_=W[nm]))
        cst = [c1, c2, c3]

        qa = b.sb(pfx + "qa", [128, 4 * T], BF16)
        ckv = b.sb(pfx + "ckv", [128, 4 * T], BF16)
        kpe = b.sb(pfx + "kpe", [65, T], BF16)
        rows3 = b.sb(pfx + "rows3", [33, 2 * T], BF16)
        kpe_sq = b.sb(pfx + "kpesq", [64, T], BF16)
        qh = [b.sb(pfx + "qh%d" % i, [128, T], BF16) for i in range(2)]
        qpe = [b.sb(pfx + "qpe%d" % i, [65, T], BF16) for i in range(2)]
        kn = [b.sb(pfx + "kn%d" % i, [128, T], BF16) for i in range(2)]
        Vtm = [b.sb(pfx + "V%d" % i, [128, 16 * 128], BF16) for i in range(2)]
        def pr_slice64(i):
            return pp[0:64, i * 512:(i + 1) * 512]
        sq = b.sb(pfx + "sq", [128, T], BF16)
        PT = b.sb(pfx + "PT", [128, 8 * 512], BF16)
        rown = rows3[32:33, 0:T]
        rowq = rows3[32:33, T:2 * T]
        b2t = b.sb(pfx + "b2", [33, 2], F32)
        one33 = b.const_tile("one33", np.ones((33, 1), bf16c(b)))
        dsb = b.sb(pfx + "dsb", [1, 512], F32)
        rbsb = b.sb(pfx + "rb", [128, 512], F32)
        wslab = b.sb(pfx + "ws", [128, 2 * 2048], BF16)
        hq = b.sb(pfx + "hq", [128, 2 * 768], BF16)
        hk = b.sb(pfx + "hk", [128, 2 * 1024], BF16)
        pp = b.ps(pfx + "pp", [128, 3072], F32)
        pt = b.ps(pfx + "pt", [128, 2048], BF16)

        def bank(i):
            return pp[:, i * 512:(i + 1) * 512]

        wuser = {0: None, 1: None}
        bank_guard = {i: None for i in range(7)}
        NB = T // 512

        def proj(dst_sb, src_slabs, nm_tiles, bias_tile, name):
            toks = []
            for m in range(nm_tiles):
                par = m % 2
                wt = p.dma(POOL, pfx + name + "%d" % par,
                           lambda e, m=m, par=par: e.dma_start(
                               out=wslab[:, par * 2048:(par + 1) * 2048],
                               in_=src_slabs[m]), after=[wuser[par]])
                def mmf(e, m=m, par=par):
                    for nb in range(NB):
                        for kc in range(NCH):
                            last = e.matmul(bank(nb % 4),
                                            wslab[:, par * 2048 + kc * 128:
                                                  par * 2048 + kc * 128 + 128],
                                            xln[:, kc * T + nb * 512:kc * T + nb * 512 + 512],
                                            start=(kc == 0), stop=(kc == NCH - 1))
                            if kc == NCH - 1 and nb < NB - 1:
                                pass
                    return last
                for nb in range(NB):
                    def mmnb(e, m=m, par=par, nb=nb):
                        for kc in range(NCH):
                            last = e.matmul(bank(nb % 4),
                                            wslab[:, par * 2048 + kc * 128:
                                                  par * 2048 + kc * 128 + 128],
                                            xln[:, kc * T + nb * 512:kc * T + nb * 512 + 512],
                                            start=(kc == 0), stop=(kc == NCH - 1))
                        return last
                    mt = p.do(TENS, mmnb, after=[wt, bank_guard[nb % 4]] + (ln_done if m == 0 and nb == 0 else []))
                    wuser[par] = mt
                    ev = p.do(SCAL, lambda e, m=m, nb=nb, dst_sb=dst_sb, bias_tile=bias_tile:
                              e.activation(out=dst_sb[:, m * T + nb * 512:m * T + nb * 512 + 512],
                                           in_=bank(nb % 4), func=AF.Identity,
                                           bias=bias_tile[:, m:m + 1]),
                              after=[mt] + cst)
                    bank_guard[nb % 4] = ev
                    toks.append(ev)
            return toks

        qa_done = proj(qa, W["b_wqa"], 4, bias_t["b_bqa"], "wqa")
        ckv_done = proj(ckv, W["b_wkva"], 4, bias_t["b_bkva"], "wkva")
        kpe_raw_done = []
        for nb in range(NB):
            par5 = 4 % 2
            wt5 = p.dma(POOL, pfx + "wkva%d" % par5,
                        lambda e: e.dma_start(out=wslab[:, par5 * 2048:(par5 + 1) * 2048],
                                              in_=W["b_wkva"][4]),
                        after=[wuser[par5]]) if nb == 0 else wt5
            bk5 = nb % 4
            mt5 = p.do(TENS, lambda e, nb=nb, bk5=bk5: [
                e.matmul(bank(bk5),
                         wslab[:, par5 * 2048 + kc * 128:par5 * 2048 + kc * 128 + 128],
                         xln[:, kc * T + nb * 512:kc * T + nb * 512 + 512],
                         start=(kc == 0), stop=(kc == NCH - 1))
                for kc in range(NCH)][-1], after=[wt5, bank_guard[bk5]])
            if nb == 0:
                wuser[par5] = mt5
            ev5 = p.do(SCAL, lambda e, nb=nb, bk5=bk5: e.activation(
                out=kpe[0:64, nb * 512:nb * 512 + 512],
                in_=pp[0:64, bk5 * 512:bk5 * 512 + 512], func=AF.Identity,
                bias=bias_t["b_bkva"][0:64, 4:5]), after=[mt5] + cst)
            bank_guard[bk5] = ev5
            kpe_raw_done.append(ev5)

        kp_raw = ckv
        rope_toks = []
        ms = p.do(VEC, lambda e: e.memset(kpe[64:65, :], -1.0), after=cst)
        rope_toks.append(ms)
        for nb in range(NB):
            sw = p.do(TENS, lambda e, nb=nb: e.matmul(
                pp[0:64, 2048:2048 + 512], swap64[:],
                kpe[0:64, nb * 512:nb * 512 + 512], start=True, stop=True),
                after=[kpe_raw_done[nb]] + cst + [rope_toks[-1], bank_guard[4]])
            v1 = p.do(VEC, lambda e, nb=nb: e.tensor_tensor(
                out=sq[0:64, nb * 512:nb * 512 + 512],
                in0=pp[0:64, 2048:2048 + 512],
                in1=Sf[:, nb * 512:nb * 512 + 512], op=ALU.mult), after=[sw] + cst)
            bank_guard[4] = v1
            v2 = p.do(VEC, lambda e, nb=nb: e.tensor_tensor(
                out=kpe[0:64, nb * 512:nb * 512 + 512],
                in0=kpe[0:64, nb * 512:nb * 512 + 512],
                in1=Cf[:, nb * 512:nb * 512 + 512], op=ALU.mult), after=[sw])
            v3 = p.do(VEC, lambda e, nb=nb: e.tensor_tensor(
                out=kpe[0:64, nb * 512:nb * 512 + 512],
                in0=kpe[0:64, nb * 512:nb * 512 + 512],
                in1=sq[0:64, nb * 512:nb * 512 + 512], op=ALU.add), after=[v1, v2])
            rope_toks.append(v3)

        kq_toks = []
        for nb in range(NB):
            kq_toks.append(p.do(SCAL, lambda e, nb=nb: e.activation(
                out=kpe_sq[0:64, nb * 512:nb * 512 + 512],
                in_=kpe[0:64, nb * 512:nb * 512 + 512], func=AF.Square),
                after=rope_toks))

        identb = b.const_tile("identb", np.eye(128, dtype=bf))
        tok_proj_last = Tok(TENS, p.ecnt[TENS])
        ptg = {0: None, 1: None}
        hquser = {0: None, 1: None}
        hkuser = {0: None, 1: None}

        abk = [0]
        def a_bank():
            i = 4 + abk[0] % 2
            abk[0] += 1
            return i
        bank_guard47 = bank_guard

        st_A = {}
        bdone = {0: None, 1: None}
        bdone_s = {0: None, 1: None}
        bdone_v = {0: None, 1: None}
        a_prev = {"sq": None, "rows": None, "vT": None}

        def stageA(h):
            par = h % 2
            qh_, qpe_, kn_, Vtm_ = qh[par], qpe[par], kn[par], Vtm[par]
            wq_t = p.dma(POOL, pfx + "hq%d" % par,
                         lambda e, h=h, par=par: e.dma_start(
                             out=hq[:, par * 768:(par + 1) * 768], in_=W["b_wqb"][h]),
                         after=[hquser[par]])
            wk_t = p.dma(POOL, pfx + "hk%d" % par,
                         lambda e, h=h, par=par: e.dma_start(
                             out=hk[:, par * 1024:(par + 1) * 1024], in_=W["b_wkvb"][h]),
                         after=[hkuser[par]])
            qn_ev, qp_ev, kn_ev, v_ev = [], [], [], []
            for nb in range(NB):
                bk = a_bank()
                mt = p.do(TENS, lambda e, par=par, nb=nb, bk=bk: [
                    e.matmul(bank(bk),
                             hq[:, par * 768 + kc * 192:par * 768 + kc * 192 + 128],
                             qa[:, kc * T + nb * 512:kc * T + nb * 512 + 512],
                             start=(kc == 0), stop=(kc == 3))
                    for kc in range(4)][-1], after=[wq_t, bank_guard47[bk]])
                ev = p.do(SCAL, lambda e, h=h, nb=nb, bk=bk, qh_=qh_: e.activation(
                    out=qh_[:, nb * 512:nb * 512 + 512], in_=bank(bk), func=AF.Identity,
                    bias=bias_t["b_bqbn"][:, h:h + 1]),
                    after=[mt, bdone[par], bdone_s[par]])
                bank_guard47[bk] = ev
                qn_ev.append(ev)
                bk2 = a_bank()
                mt2 = p.do(TENS, lambda e, par=par, nb=nb, bk2=bk2: [
                    e.matmul(pr_slice64(bk2),
                             hq[:, par * 768 + kc * 192 + 128:par * 768 + kc * 192 + 192],
                             qa[:, kc * T + nb * 512:kc * T + nb * 512 + 512],
                             start=(kc == 0), stop=(kc == 3))
                    for kc in range(4)][-1], after=[wq_t, bank_guard47[bk2]])
                ev2 = p.do(SCAL, lambda e, h=h, nb=nb, bk2=bk2, qpe_=qpe_: e.activation(
                    out=qpe_[0:64, nb * 512:nb * 512 + 512],
                    in_=pr_slice64(bk2), func=AF.Identity,
                    bias=bias_t["b_bqbp"][:, h:h + 1]),
                    after=[mt2, bdone[par], bdone_s[par], bdone_v[par]])
                bank_guard47[bk2] = ev2
                qp_ev.append(ev2)
            qn2_ev = []
            for nb in range(NB):
                s1 = p.do(SCAL, lambda e, nb=nb, qh_=qh_: e.activation(
                    out=sq[:, nb * 512:nb * 512 + 512], in_=qh_[:, nb * 512:nb * 512 + 512],
                    func=AF.Square), after=[qn_ev[nb], a_prev["sq"]])
                bk = a_bank()
                mta = p.do(TENS, lambda e, nb=nb, bk=bk: e.matmul(
                    pp[32:33, bk * 512:bk * 512 + 512], ones_b[:],
                    sq[:, nb * 512:nb * 512 + 512], start=True, stop=False),
                    after=[s1, bank_guard47[bk]])
                s2 = p.do(SCAL, lambda e, nb=nb, qpe_=qpe_: e.activation(
                    out=sq[0:64, nb * 512:nb * 512 + 512],
                    in_=qpe_[0:64, nb * 512:nb * 512 + 512],
                    func=AF.Square), after=[qp_ev[nb], mta])
                mt = p.do(TENS, lambda e, nb=nb, bk=bk: e.matmul(
                    pp[32:33, bk * 512:bk * 512 + 512], ones_b[0:64, :],
                    sq[0:64, nb * 512:nb * 512 + 512], start=False, stop=True),
                    after=[s2])
                ev = p.do(VEC, lambda e, nb=nb, bk=bk: e.tensor_copy(
                    out=rowq[0:1, nb * 512:nb * 512 + 512],
                    in_=pp[32:33, bk * 512:bk * 512 + 512]),
                    after=[mt, a_prev["rows"]])
                bank_guard47[bk] = ev
                qn2_ev.append(ev)
            rp_toks = []
            for nb in range(NB):
                bk = a_bank()
                sw = p.do(TENS, lambda e, nb=nb, bk=bk, qpe_=qpe_: e.matmul(
                    pr_slice64(bk), swap64[:],
                    qpe_[0:64, nb * 512:nb * 512 + 512], start=True, stop=True),
                    after=[qp_ev[nb], qn2_ev[nb], bank_guard47[bk]])
                v1 = p.do(VEC, lambda e, nb=nb, bk=bk: e.tensor_tensor(
                    out=sq[0:64, nb * 512:nb * 512 + 512],
                    in0=pr_slice64(bk),
                    in1=Sf[:, nb * 512:nb * 512 + 512], op=ALU.mult),
                    after=[sw, qn2_ev[nb]])
                v2 = p.do(VEC, lambda e, nb=nb, qpe_=qpe_: e.tensor_tensor(
                    out=qpe_[0:64, nb * 512:nb * 512 + 512],
                    in0=qpe_[0:64, nb * 512:nb * 512 + 512],
                    in1=Cf[:, nb * 512:nb * 512 + 512], op=ALU.mult),
                    after=[sw, qn2_ev[nb]])
                v3 = p.do(VEC, lambda e, nb=nb, qpe_=qpe_: e.tensor_tensor(
                    out=qpe_[0:64, nb * 512:nb * 512 + 512],
                    in0=qpe_[0:64, nb * 512:nb * 512 + 512],
                    in1=sq[0:64, nb * 512:nb * 512 + 512], op=ALU.add), after=[v1, v2])
                bank_guard47[bk] = v1
                rp_toks.append(v3)
            for nb in range(NB):
                bk = a_bank()
                mt = p.do(TENS, lambda e, par=par, nb=nb, bk=bk: [
                    e.matmul(bank(bk),
                             hk[:, par * 1024 + kc * 256:par * 1024 + kc * 256 + 128],
                             ckv[:, kc * T + nb * 512:kc * T + nb * 512 + 512],
                             start=(kc == 0), stop=(kc == 3))
                    for kc in range(4)][-1], after=[wk_t, bank_guard47[bk]])
                ev = p.do(SCAL, lambda e, h=h, nb=nb, bk=bk, kn_=kn_: e.activation(
                    out=kn_[:, nb * 512:nb * 512 + 512], in_=bank(bk), func=AF.Identity,
                    bias=bias_t["b_bkbn"][:, h:h + 1]),
                    after=[mt, bdone[par], bdone_s[par]])
                bank_guard47[bk] = ev
                kn_ev.append(ev)
                bk2 = a_bank()
                mt2 = p.do(TENS, lambda e, par=par, nb=nb, bk2=bk2: [
                    e.matmul(bank(bk2),
                             hk[:, par * 1024 + kc * 256 + 128:par * 1024 + kc * 256 + 256],
                             ckv[:, kc * T + nb * 512:kc * T + nb * 512 + 512],
                             start=(kc == 0), stop=(kc == 3))
                    for kc in range(4)][-1], after=[wk_t, bank_guard47[bk2]])
                ev2 = p.do(SCAL, lambda e, h=h, nb=nb, bk2=bk2: e.activation(
                    out=sq[:, nb * 512:nb * 512 + 512],
                    in_=bank(bk2), func=AF.Identity,
                    bias=bias_t["b_bkbv"][:, h:h + 1]),
                    after=[mt2, a_prev["vT"], rp_toks[nb], qn2_ev[nb]])
                bank_guard47[bk2] = ev2
                v_ev.append(ev2)
            vt_toks = []
            for tt in range(NTT):
                tr = p.do(TENS, lambda e, tt=tt: e.transpose(
                    pt[0:128, (tt % 2) * 1024:(tt % 2) * 1024 + 128],
                    sq[:, tt * 128:(tt + 1) * 128], identb[:]),
                    after=[v_ev[(tt * 128) // 512], ptg[tt % 2]])
                cp = p.do(VEC, lambda e, tt=tt, Vtm_=Vtm_: e.tensor_copy(
                    out=Vtm_[:, tt * 128:(tt + 1) * 128],
                    in_=pt[0:128, (tt % 2) * 1024:(tt % 2) * 1024 + 128]),
                    after=[tr, bdone[par] if tt < 2 else None])
                ptg[tt % 2] = cp
                vt_toks.append(cp)
            def emit_aug():
                kn2_ev = []
                for nb in range(NB):
                    s1 = p.do(SCAL, lambda e, nb=nb, kn_=kn_: e.activation(
                        out=sq[:, nb * 512:nb * 512 + 512], in_=kn_[:, nb * 512:nb * 512 + 512],
                        func=AF.Square), after=[kn_ev[nb], qn2_ev[nb]] + vt_toks)
                    bk = a_bank()
                    mt = p.do(TENS, lambda e, nb=nb, bk=bk: [
                        e.matmul(pp[32:33, bk * 512:bk * 512 + 512], ones_b[:],
                                 sq[:, nb * 512:nb * 512 + 512], start=True, stop=False),
                        e.matmul(pp[32:33, bk * 512:bk * 512 + 512], ones_b[0:64, :],
                                 kpe_sq[0:64, nb * 512:nb * 512 + 512],
                                 start=False, stop=True)][-1],
                        after=[s1, bank_guard47[bk]] + (kq_toks if h == 0 and nb == 0 else []))
                    ev = p.do(VEC, lambda e, nb=nb, bk=bk: e.tensor_copy(
                        out=rown[0:1, nb * 512:nb * 512 + 512],
                        in_=pp[32:33, bk * 512:bk * 512 + 512]),
                        after=[mt, a_prev["rows"]])
                    bank_guard47[bk] = ev
                    kn2_ev.append(ev)
                a2 = p.do(VEC, lambda e: e.reduce_max(out=b2t[32:33, 0:1], in_=rown[0:1, :],
                                                      axis=AX.X), after=kn2_ev)
                a3 = p.do(VEC, lambda e: e.tensor_scalar(out=rowq[0:1, :], in0=rowq[0:1, :],
                                                         scalar1=b2t[32:33, 0:1], scalar2=None,
                                                         op0=ALU.mult), after=[a2] + qn2_ev)
                a4l = None
                for nb in range(NB):
                    bk = a_bank()
                    m4 = p.do(TENS, lambda e, nb=nb, bk=bk: e.matmul(
                        pp[64:65, bk * 512:bk * 512 + 512], one33[32:33, 0:1],
                        rowq[0:1, nb * 512:nb * 512 + 512], start=True, stop=True),
                        after=[a3, bank_guard47[bk]])
                    a4l = p.do(SCAL, lambda e, nb=nb, bk=bk, qpe_=qpe_: e.activation(
                        out=qpe_[64:65, nb * 512:nb * 512 + 512],
                        in_=pp[64:65, bk * 512:bk * 512 + 512], func=AF.Sqrt),
                        after=[m4, bdone[par], bdone_v[par]])
                    bank_guard47[bk] = a4l
                a_prev["sq"] = kn2_ev[-1]
                a_prev["rows"] = a4l
                st_A[par]["aug"] = a4l
            a_prev["vT"] = vt_toks[-1]
            hquser[par] = mt2
            hkuser[par] = mt2
            st_A[par] = {"aug": None, "rope": rp_toks, "kn_ev": kn_ev,
                         "vt": vt_toks, "qh": qh_, "qpe": qpe_, "kn": kn_,
                         "Vtm": Vtm_}
            return emit_aug

        def stageB(h, mid=None):
            par = h % 2
            A = st_A[par]
            qh_, qpe_, kn_, Vtm_ = A["qh"], A["qpe"], A["kn"], A["Vtm"]
            st_guard = {0: bank_guard[0], 1: bank_guard[1]}
            last_pv = None
            for qb in range(NB):
                exp_toks = []
                den_last = None
                pv_last = None
                for kt in range(NTT):
                    stt = p.do(TENS, lambda e, kt=kt, qb=qb, kn_=kn_, qh_=qh_,
                               qpe_=qpe_: [
                        e.matmul(bank(kt % 2), kn_[:, kt * 128:kt * 128 + 128],
                                 qh_[:, qb * 512:qb * 512 + 512], start=True, stop=False),
                        e.matmul(bank(kt % 2), kpe[0:65, kt * 128:kt * 128 + 128],
                                 qpe_[0:65, qb * 512:qb * 512 + 512], start=False,
                                 stop=True)][-1],
                        after=[st_guard[kt % 2], A["aug"], A["rope"][qb]] +
                              (A["kn_ev"] + A["vt"] if kt == 0 and qb == 0 else []))
                    ex = p.do(SCAL, lambda e, kt=kt: e.activation(
                        out=PT[:, (kt % 8) * 512:(kt % 8) * 512 + 512], in_=bank(kt % 2),
                        func=AF.Exp, scale=float(1.0 / np.sqrt(128.0))),
                        after=[stt] + ([exp_toks[kt - 8][1]] if kt >= 8 else []))
                    st_guard[kt % 2] = ex
                    den_last = p.do(TENS, lambda e, kt=kt: e.matmul(
                        pp[0:1, 1024:1024 + 512], ones_b[:],
                        PT[:, (kt % 8) * 512:(kt % 8) * 512 + 512],
                        start=(kt == 0), stop=(kt == NTT - 1)),
                        after=[ex, bank_guard[2] if kt == 0 else None])
                    pv_last = p.do(TENS, lambda e, kt=kt, Vtm_=Vtm_: e.matmul(
                        pp[:, 1536:1536 + 512], Vtm_[:, kt * 128:kt * 128 + 128],
                        PT[:, (kt % 8) * 512:(kt % 8) * 512 + 512],
                        start=(kt == 0), stop=(kt == NTT - 1)),
                        after=[ex, bank_guard[3] if kt == 0 else None])
                    exp_toks.append((ex, pv_last))
                d1 = p.do(VEC, lambda e: e.reciprocal(out=dsb[0:1, :],
                                                      in_=pp[0:1, 1024:1024 + 512]),
                          after=[den_last])
                bank_guard[2] = d1
                r1_ = p.do(TENS, lambda e: e.matmul(pp[:, 0:512], ones_f[:],
                                                    dsb[0:1, :], start=True, stop=True),
                           after=[d1, st_guard[0]])
                r2_ = p.do(VEC, lambda e: e.tensor_copy(out=rbsb[:, :],
                                                        in_=pp[:, 0:512]),
                           after=[r1_])
                st_guard[0] = r2_
                ao = p.do(VEC, lambda e, h=h, qb=qb: e.tensor_tensor(
                    out=xln[:, h * T + qb * 512:h * T + qb * 512 + 512],
                    in0=pp[:, 1536:1536 + 512], in1=rbsb[:, :], op=ALU.mult),
                    after=[pv_last, r2_, tok_proj_last])
                bank_guard[3] = ao
                last_pv = pv_last
                bdone_v[par] = ao
                if qb == 0 and mid is not None:
                    mid()
            bank_guard[0], bank_guard[1] = st_guard[0], st_guard[1]
            bdone[par] = last_pv
            bdone_s[par] = Tok(SCAL, p.ecnt[SCAL])

        pend = stageA(0)
        pend()
        for h in range(HB):
            pend = stageA(h + 1) if h + 1 < HB else None
            stageB(h, mid=pend)

    with b.phase(pfx + "w"):
        wslab_w = b.sb(pfx + "ws2", [128, 2 * 2048], BF16)
        xo_w = b.sb(pfx + "xo", [128, T], F32)
        xrt_w = b.sb(pfx + "xr", [128, T], F32)
        bwo = b.sb(pfx + "bwo", [128, 16], F32)
        pp_w = b.ps(pfx + "pp2", [128, 3072], F32)
        bankg_w = {i: None for i in range(6)}
        bwo_t = p.dma(SYNC, pfx + "c2", lambda e: e.dma_start(out=bwo[:], in_=W["b_bwo"]))
        wuser2 = {0: None, 1: None}
        xrt_g = None
        xo_g = None
        for m in range(NCH):
            par = m % 2
            wt = p.dma(POOL, pfx + "wo%d" % par,
                       lambda e, m=m, par=par: e.dma_start(
                           out=wslab_w[:, par * 2048:(par + 1) * 2048], in_=W["b_wo"][m]),
                       after=[wuser2[par]])
            lt = p.dma(SYNC, pfx + "rx",
                       lambda e, m=m: e.dma_start(
                           out=xrt_w[:, :], in_=xsrc[m * 128:(m + 1) * 128, :]),
                       after=[xrt_g])
            evs = []
            for nb in range(T // 512):
                bk = nb % 2
                mt = p.do(TENS, lambda e, par=par, nb=nb: [
                    e.matmul(pp_w[:, (nb % 2) * 512:(nb % 2) * 512 + 512],
                             wslab_w[:, par * 2048 + kc * 128:par * 2048 + kc * 128 + 128],
                             xln[:, kc * T + nb * 512:kc * T + nb * 512 + 512],
                             start=(kc == 0), stop=(kc == NCH - 1))
                    for kc in range(NCH)][-1],
                    after=[wt, bankg_w[bk]])
                ev = p.do(SCAL, lambda e, m=m, nb=nb: e.activation(
                    out=xo_w[:, nb * 512:nb * 512 + 512],
                    in_=pp_w[:, (nb % 2) * 512:(nb % 2) * 512 + 512], func=AF.Identity,
                    bias=bwo[:, m:m + 1]), after=[mt, xo_g, bwo_t])
                bankg_w[bk] = ev
                wuser2[par] = mt
                ad = p.do(VEC, lambda e, nb=nb: e.tensor_tensor(
                    out=xo_w[:, nb * 512:nb * 512 + 512],
                    in0=xo_w[:, nb * 512:nb * 512 + 512],
                    in1=xrt_w[:, nb * 512:nb * 512 + 512], op=ALU.add), after=[ev, lt])
                evs.append(ad)
            st = p.dma(SYNC, pfx + "sx",
                       lambda e, m=m: e.dma_start(
                           out=dst[m * 128:(m + 1) * 128, :], in_=xo_w[:]), after=evs)
            xrt_g = st
            xo_g = st


def build_attnb_only():
    b = Build()
    xin = b.dram_in("x_in", [D, T], F32)
    out = b.dram_out("x_out", [D, T], F32)
    wshapes = {"b_wqa": (4, 128, 2048), "b_wkva": (5, 128, 2048),
               "b_wqb": (16, 128, 768), "b_wkvb": (16, 128, 1024),
               "b_wo": (16, 128, 2048), "b_bqa": (128, 4), "b_bkva": (128, 5),
               "b_bqbn": (128, 16), "b_bqbp": (64, 16), "b_bkbn": (128, 16),
               "b_bkbv": (128, 16), "b_bwo": (128, 16)}
    W = {k: b.dram_in(k, list(s), F32) for k, s in wshapes.items()}
    emit_attn_b(b, xin, out, W, "a1")
    with b.nc.Block() as blk:
        b.p.emit(blk)
        @blk.sync
        def _(eng):
            for name, rec in b.p.dsems.items():
                eng.wait_ge(rec[0], rec[1])
    return b



def prep_attna_weights(I):
    f32 = np.float32
    out = {
        "a_wr": _relayout(np.asarray(I["L0_attn_a_W_r"], f32)),
        "a_wk": _relayout(np.asarray(I["L0_attn_a_W_k"], f32)),
        "a_wv": _relayout(np.asarray(I["L0_attn_a_W_v"], f32)),
        "a_wo": _relayout(np.asarray(I["L0_attn_a_W_o"], f32)),
        "a_w1": np.ascontiguousarray(np.asarray(I["L0_attn_a_w1"], f32)
                                     .reshape(16, 128, 64).transpose(1, 0, 2).reshape(128, 1024)),
        "a_a1": np.ascontiguousarray(np.asarray(I["L0_attn_a_a1"], f32)
                                     .reshape(16, 128, 64).transpose(1, 0, 2).reshape(128, 1024)),
        "a_g1": np.ascontiguousarray(np.asarray(I["L0_attn_a_g1"], f32)
                                     .reshape(16, 128, 128).transpose(1, 0, 2).reshape(128, 2048)),
        "a_w2": np.ascontiguousarray(np.asarray(I["L0_attn_a_w2"], f32)),
        "a_a2": np.ascontiguousarray(np.asarray(I["L0_attn_a_a2"], f32)),
        "a_g2": np.ascontiguousarray(np.asarray(I["L0_attn_a_g2"], f32)),
    }
    names = ["x_r", "x_w", "x_k", "x_v", "x_a", "x_g", "w0", "a0", "k_k", "k_a"]
    vs = [np.asarray(I["L0_attn_a_" + n], f32).reshape(-1) for n in names]
    vs.append(np.asarray(I["L0_attn_a_r_k"], f32).reshape(-1))
    vs.append(np.asarray(I["L0_attn_a_gn_scale"], f32).reshape(-1))
    vs.append(np.asarray(I["L0_attn_a_gn_offset"], f32).reshape(-1))
    V = np.stack(vs, axis=1)
    out["a_vecs"] = np.ascontiguousarray(V.reshape(NCH, 128, 13))
    return out


VI = {n: i for i, n in enumerate(
    ["x_r", "x_w", "x_k", "x_v", "x_a", "x_g", "w0", "a0", "k_k", "k_a",
     "r_k", "gn_s", "gn_o"])}


def emit_attn_a(b, xsrc, dst, W, pfx):
    p = b.p
    bf = bf16c(b)
    if "xln" not in b.com:
        b.com["xln"] = b.sbp("g_xln", [128, NCH * T], BF16)
    xln = b.com["xln"]
    ln_done = emit_ln(b, xsrc, xln, pfx + "l")
    ones_b = b.com["ct_ones128_b"]
    ones_f = b.com["ct_ones128_f"]
    identb = b.const_tile("identb", np.eye(128, dtype=bf))

    NB = T // 512
    NCC = T // LCH

    streams = {n: b.dram_tmp(pfx + "s_" + n, [D, T], BF16)
               for n in ["r", "k2", "kk", "bb", "v", "sz", "gg", "bo"]}
    y_hbm = b.dram_tmp(pfx + "s_y", [NCC, 128, 16 * LCH], BF16)

    with b.phase(pfx + "p"):
        vecs = b.sbp(pfx + "vecs", [128, NCH * 13], F32)
        vld = p.dma(SYNC, pfx + "c", lambda e: e.dma_start(
            out=vecs[:].rearrange("p (c v) -> p c v", v=13), in_=W["a_vecs"].rearrange("c p v -> p c v")))
        omc = b.sbp(pfx + "omc", [128, NCH * 7], F32)
        omtok = []
        for c in range(NCH):
            t1 = p.do(VEC, lambda e, c=c: e.tensor_scalar(
                out=omc[:, c * 7:c * 7 + 6], in0=vecs[:, c * 13:c * 13 + 6],
                scalar1=-1.0, scalar2=1.0, op0=ALU.mult, op1=ALU.add), after=[vld])
            t2 = p.do(VEC, lambda e, c=c: e.tensor_scalar(
                out=omc[:, c * 7 + 6:c * 7 + 7], in0=vecs[:, c * 13 + 9:c * 13 + 10],
                scalar1=-1.0, scalar2=1.0, op0=ALU.mult, op1=ALU.add), after=[vld])
            omtok += [t1, t2]

        mix = b.sb(pfx + "mix", [128, NCH * T], BF16)
        wslab = b.sb(pfx + "ws", [128, 2 * 2048], BF16)
        stg = b.sb(pfx + "stg", [128, 2 * T], BF16)
        t64 = b.sb(pfx + "t64", [64, T], BF16)
        lora1 = b.sb(pfx + "lo1", [128, 2048], BF16)
        lora2 = b.sb(pfx + "lo2", [128, 2048], BF16)
        t128 = b.sb(pfx + "t128", [128, T], BF16)
        pp = b.ps(pfx + "pp", [128, 3072], F32)
        def bank(i):
            return pp[:, i * 512:(i + 1) * 512]
        bank_g = {i: None for i in range(6)}
        wuser = {0: None, 1: None}
        stg_g = {0: None, 1: None}

        mix_built = []
        mix_users = []

        def build_mix(vi):
            nonlocal mix_built, mix_users
            toks = []
            for c in range(NCH):
                ms = p.do(VEC, lambda e, c=c: e.memset(mix[:, c * T:c * T + 1], 0.0),
                          after=mix_users + omtok)
                t1 = p.do(VEC, lambda e, c=c, vi=vi: e.tensor_scalar(
                    out=mix[:, c * T + 1:c * T + T],
                    in0=xln[:, c * T:c * T + T - 1],
                    scalar1=vecs[:, c * 13 + vi:c * 13 + vi + 1], scalar2=None,
                    op0=ALU.mult), after=mix_users + omtok + ln_done)
                t2 = p.do(VEC, lambda e, c=c, vi=vi: e.scalar_tensor_tensor(
                    out=mix[:, c * T:c * T + T], in0=xln[:, c * T:c * T + T],
                    scalar=omc[:, c * 7 + vi:c * 7 + vi + 1],
                    in1=mix[:, c * T:c * T + T], op0=ALU.mult, op1=ALU.add),
                    after=[ms, t1])
                toks.append(t2)
            mix_built = toks
            mix_users = []

        def spill(stream, c, par, src_tok):
            st = p.dma(SYNC, pfx + "sp%d" % par,
                       lambda e, stream=stream, c=c, par=par: e.dma_start(
                           out=streams[stream][c * 128:(c + 1) * 128, :],
                           in_=stg[:, par * T:(par + 1) * T]), after=[src_tok])
            stg_g[par] = st
            return st

        def proj_big(wkey, stream, vi):
            build_mix(vi)
            for m in range(NCH):
                par = m % 2
                wt = p.dma(POOL, pfx + "w%d" % par,
                           lambda e, m=m, par=par, wkey=wkey: e.dma_start(
                               out=wslab[:, par * 2048:(par + 1) * 2048],
                               in_=W[wkey][m]), after=[wuser[par]])
                evs = []
                for nb in range(NB):
                    bk = nb % 2
                    mt = p.do(TENS, lambda e, m=m, par=par, nb=nb: [
                        e.matmul(bank(nb % 2),
                                 wslab[:, par * 2048 + kc * 128:par * 2048 + kc * 128 + 128],
                                 mix[:, kc * T + nb * 512:kc * T + nb * 512 + 512],
                                 start=(kc == 0), stop=(kc == NCH - 1))
                        for kc in range(NCH)][-1],
                        after=[wt, bank_g[bk]] + mix_built)
                    mix_users.append(mt)
                    wuser[par] = mt
                    ev = p.do(SCAL, lambda e, m=m, nb=nb: e.activation(
                        out=stg[:, (m % 2) * T + nb * 512:(m % 2) * T + nb * 512 + 512],
                        in_=bank(nb % 2), func=AF.Identity),
                        after=[mt, stg_g[m % 2]])
                    bank_g[bk] = ev
                    evs.append(ev)
                spill(stream, m, m % 2, evs[-1])

        def proj_lora(w1key, w2key, midfunc, outfunc, bias_vi, stream, vi, mid_m=64):
            build_mix(vi)
            l1 = p.dma(POOL, pfx + "l1", lambda e, w1key=w1key: e.dma_start(
                out=lora1[:, 0:NCH * mid_m], in_=W[w1key]), after=[Tok(TENS, p.ecnt[TENS])])
            l2 = p.dma(POOL, pfx + "l2", lambda e, w2key=w2key: e.dma_start(
                out=lora2[0:mid_m, :], in_=W[w2key]), after=[Tok(TENS, p.ecnt[TENS])])
            mid = t64 if mid_m == 64 else t128
            for nb in range(NB):
                bk = nb % 2
                mt = p.do(TENS, lambda e, nb=nb, mid_m=mid_m: [
                    e.matmul(pp[0:mid_m, (nb % 2) * 512:(nb % 2) * 512 + 512],
                             lora1[:, kc * mid_m:(kc + 1) * mid_m],
                             mix[:, kc * T + nb * 512:kc * T + nb * 512 + 512],
                             start=(kc == 0), stop=(kc == NCH - 1))
                    for kc in range(NCH)][-1],
                    after=[l1, bank_g[bk]] + mix_built)
                mix_users.append(mt)
                ev = p.do(SCAL, lambda e, nb=nb, mid=mid, mid_m=mid_m, midfunc=midfunc:
                          e.activation(out=mid[0:mid_m, nb * 512:nb * 512 + 512],
                                       in_=pp[0:mid_m, (nb % 2) * 512:(nb % 2) * 512 + 512],
                                       func=midfunc), after=[mt])
                bank_g[bk] = ev
            mid_done = Tok(SCAL, p.ecnt[SCAL])
            for m in range(NCH):
                evs = []
                for nb in range(NB):
                    bk = 2 + nb % 2
                    mt = p.do(TENS, lambda e, m=m, nb=nb, mid=mid, mid_m=mid_m: e.matmul(
                        bank(2 + nb % 2), lora2[0:mid_m, m * 128:(m + 1) * 128],
                        mid[0:mid_m, nb * 512:nb * 512 + 512], start=True, stop=True),
                        after=[l2, bank_g[bk], mid_done])
                    def evf(e, m=m, nb=nb, outfunc=outfunc, bias_vi=bias_vi):
                        kw = {}
                        if bias_vi is not None:
                            kw["bias"] = vecs[:, m * 13 + bias_vi:m * 13 + bias_vi + 1]
                        return e.activation(
                            out=stg[:, (m % 2) * T + nb * 512:(m % 2) * T + nb * 512 + 512],
                            in_=bank(2 + nb % 2), func=outfunc, **kw)
                    ev = p.do(SCAL, evf, after=[mt, stg_g[m % 2], vld])
                    bank_g[bk] = ev
                    evs.append(ev)
                spill(stream, m, m % 2, evs[-1])

        streams["kraw"] = b.dram_tmp(pfx + "s_kraw", [D, T], BF16)
        streams["as"] = b.dram_tmp(pfx + "s_as", [D, T], BF16)

        proj_big("a_wr", "r", VI["x_r"])
        proj_big("a_wk", "kraw", VI["x_k"])
        proj_big("a_wv", "v", VI["x_v"])
        proj_lora("a_w1", "a_w2", AF.Tanh, AF.Sigmoid, VI["w0"], "sz", VI["x_w"], 64)
        proj_lora("a_a1", "a_a2", AF.Identity, AF.Sigmoid, VI["a0"], "as", VI["x_a"], 64)
        proj_lora("a_g1", "a_g2", AF.Sigmoid, AF.Identity, None, "gg", VI["x_g"], 128)
        vecs_keep = vecs
        omc_keep = omc

    with b.phase(pfx + "q"):
        vecs = vecs_keep
        omc = omc_keep
        hsel2_np = np.zeros((128, 2), bf16c(b))
        hsel2_np[0:64, 0] = 1.0
        hsel2_np[64:128, 1] = 1.0
        bsel2_np = np.zeros((2, 128), np.float32)
        bsel2_np[0, 0:64] = 1.0
        bsel2_np[1, 64:128] = 1.0
        hsel2 = b.const_tile("hsel2", hsel2_np)
        bsel2 = b.const_tile("bsel2", bsel2_np)
        pk = b.sb(pfx + "pk", [128, T], BF16)
        pa = b.sb(pfx + "pa", [128, T], BF16)
        rrt = b.sb(pfx + "rrt", [128, T], BF16)
        pv = b.sb(pfx + "pv", [128, T], BF16)
        kkt = b.sb(pfx + "kkt", [128, T], BF16)
        k2t = b.sb(pfx + "k2t", [128, T], BF16)
        btl = b.sb(pfx + "btl", [128, T], BF16)
        bont = b.sb(pfx + "bont", [128, T], BF16)
        sqs = b.sb(pfx + "sqs", [128, 1024], BF16)
        rows2 = b.sb(pfx + "rws", [2, T], F32)
        pq = b.ps(pfx + "pq", [128, 3072], F32)
        gq = {"pk": None, "pa": None, "pr": None, "pv": None,
              "kkt": None, "k2t": None, "btl": None, "bont": None}
        for c in range(NCH):
            lk = p.dma(SYNC, pfx + "qk", lambda e, c=c: e.dma_start(
                out=pk[:], in_=streams["kraw"][c * 128:(c + 1) * 128, :]), after=[gq["pk"]])
            la = p.dma(SYNC, pfx + "qa", lambda e, c=c: e.dma_start(
                out=pa[:], in_=streams["as"][c * 128:(c + 1) * 128, :]), after=[gq["pa"]])
            lr = p.dma(SYNC, pfx + "qr", lambda e, c=c: e.dma_start(
                out=rrt[:], in_=streams["r"][c * 128:(c + 1) * 128, :]), after=[gq["pr"]])
            lv = p.dma(SYNC, pfx + "qv", lambda e, c=c: e.dma_start(
                out=pv[:], in_=streams["v"][c * 128:(c + 1) * 128, :]), after=[gq["pv"]])
            k1 = p.do(VEC, lambda e, c=c: e.tensor_scalar(
                out=kkt[:], in0=pk[:], scalar1=vecs[:, c * 13 + 8:c * 13 + 9],
                scalar2=None, op0=ALU.mult), after=[lk, gq["kkt"]])
            nrm_ev = []
            for nb in range(NB):
                s1 = p.do(SCAL, lambda e, nb=nb: e.activation(
                    out=sqs[:, (nb % 2) * 512:(nb % 2) * 512 + 512],
                    in_=kkt[:, nb * 512:nb * 512 + 512], func=AF.Square),
                    after=[k1] + ([nrm_ev[nb - 2]] if nb >= 2 else []))
                m1 = p.do(TENS, lambda e, nb=nb: e.matmul(
                    pq[0:2, 2048:2048 + 512], hsel2[:],
                    sqs[:, (nb % 2) * 512:(nb % 2) * 512 + 512], start=True, stop=True),
                    after=[s1, nrm_ev[-1] if nrm_ev else None])
                e1 = p.do(VEC, lambda e, nb=nb: e.tensor_copy(
                    out=rows2[0:2, nb * 512:nb * 512 + 512], in_=pq[0:2, 2048:2048 + 512]),
                    after=[m1])
                nrm_ev.append(e1)
            s2 = p.do(SCAL, lambda e: e.activation(out=rows2[0:2, :], in_=rows2[0:2, :],
                                                   func=AF.Sqrt), after=nrm_ev)
            v2 = p.do(VEC, lambda e: e.tensor_scalar(out=rows2[0:2, :], in0=rows2[0:2, :],
                                                     scalar1=1e-12, scalar2=None,
                                                     op0=ALU.max), after=[s2])
            v3 = p.do(VEC, lambda e: e.reciprocal(out=rows2[0:2, :], in_=rows2[0:2, :]),
                      after=[v2])
            kk_done = []
            for nb in range(NB):
                m2 = p.do(TENS, lambda e, nb=nb: e.matmul(
                    pq[:, 2560:2560 + 512], bsel2[:],
                    rows2[0:2, nb * 512:nb * 512 + 512], start=True, stop=True),
                    after=[v3, kk_done[-1] if kk_done else None])
                v4 = p.do(VEC, lambda e, nb=nb: e.tensor_tensor(
                    out=kkt[:, nb * 512:nb * 512 + 512], in0=kkt[:, nb * 512:nb * 512 + 512],
                    in1=pq[:, 2560:2560 + 512], op=ALU.mult), after=[m2])
                kk_done.append(v4)
            b1 = p.do(VEC, lambda e, c=c: e.tensor_scalar(
                out=btl[:], in0=pa[:], scalar1=vecs[:, c * 13 + 9:c * 13 + 10],
                scalar2=omc[:, c * 7 + 6:c * 7 + 7], op0=ALU.mult, op1=ALU.add),
                after=[la, gq["btl"]])
            b2 = p.do(VEC, lambda e: e.tensor_tensor(out=k2t[:], in0=pk[:], in1=btl[:],
                                                     op=ALU.mult), after=[b1, k1, gq["k2t"]])
            b3 = p.do(VEC, lambda e: e.tensor_tensor(out=btl[:], in0=kkt[:], in1=pa[:],
                                                     op=ALU.mult), after=kk_done + [b1, b2])
            z1 = p.do(VEC, lambda e: e.tensor_tensor(out=bont[:], in0=rrt[:], in1=k2t[:],
                                                     op=ALU.mult), after=[lr, b2, gq["bont"]])
            z2 = p.do(VEC, lambda e, c=c: e.tensor_scalar(
                out=bont[:], in0=bont[:], scalar1=vecs[:, c * 13 + 10:c * 13 + 11],
                scalar2=None, op0=ALU.mult), after=[z1])
            bon_done = []
            for nb in range(NB):
                m3 = p.do(TENS, lambda e, nb=nb: e.matmul(
                    pq[0:2, 2048:2048 + 512], hsel2[:],
                    bont[:, nb * 512:nb * 512 + 512], start=True, stop=True),
                    after=[z2, bon_done[-1] if bon_done else None] + ([nrm_ev[-1]] if nb == 0 else []))
                e3 = p.do(VEC, lambda e, nb=nb: e.tensor_copy(
                    out=rows2[0:2, nb * 512:nb * 512 + 512], in_=pq[0:2, 2048:2048 + 512]),
                    after=[m3, kk_done[-1] if nb == 0 else None])
                m4 = p.do(TENS, lambda e, nb=nb: e.matmul(
                    pq[:, 2560:2560 + 512], bsel2[:],
                    rows2[0:2, nb * 512:nb * 512 + 512], start=True, stop=True),
                    after=[e3, kk_done[-1] if nb == 0 else None])
                v5 = p.do(VEC, lambda e, nb=nb: e.tensor_tensor(
                    out=bont[:, nb * 512:nb * 512 + 512], in0=pv[:, nb * 512:nb * 512 + 512],
                    in1=pq[:, 2560:2560 + 512], op=ALU.mult), after=[m4, lv])
                bon_done.append(v5)
            sp1 = p.dma(SYNC, pfx + "qs1", lambda e, c=c: e.dma_start(
                out=streams["kk"][c * 128:(c + 1) * 128, :], in_=kkt[:]), after=kk_done + [b3])
            sp2 = p.dma(SYNC, pfx + "qs2", lambda e, c=c: e.dma_start(
                out=streams["k2"][c * 128:(c + 1) * 128, :], in_=k2t[:]), after=[b2, z1])
            sp3 = p.dma(SYNC, pfx + "qs3", lambda e, c=c: e.dma_start(
                out=streams["bb"][c * 128:(c + 1) * 128, :], in_=btl[:]), after=[b3])
            sp4 = p.dma(SYNC, pfx + "qs4", lambda e, c=c: e.dma_start(
                out=streams["bo"][c * 128:(c + 1) * 128, :], in_=bont[:]), after=bon_done)
            gq["pk"] = b2
            gq["pa"] = b3
            gq["pr"] = z1
            gq["pv"] = bon_done[-1]
            gq["kkt"] = sp1
            gq["k2t"] = sp2
            gq["btl"] = sp3
            gq["bont"] = sp4

    with b.phase(pfx + "r"):
        bfdt = bf16c(b)
        tri_i = b.const_tile("tri_i", np.triu(np.ones((128, 128), bfdt), 0))
        tri_s = b.const_tile("tri_s", np.triu(np.ones((128, 128), bfdt), 1))
        msk_su = b.const_tile("msk_su", np.triu(np.ones((128, 128), np.float32), 1))
        msk_iu = b.const_tile("msk_iu", np.triu(np.ones((128, 128), np.float32), 0))
        msk_sl = b.const_tile("msk_sl", np.tril(np.ones((128, 128), np.float32), -1))

        SDE = 0.60653065971263342
        nst = 6
        sn = ["r", "k2", "kk", "bb", "v", "sz"]
        stt = {n: b.sb(pfx + "st_" + n, [128, 2 * 2048], BF16) for n in sn}
        lamT = b.sb(pfx + "lamT", [128, 6144], BF16)
        lamL32 = b.sb(pfx + "lamL", [128, NCH], F32)
        ltm = b.sb(pfx + "ltm", [128, 2048], BF16)
        sztm = b.sb(pfx + "sztm", [128, 2048], BF16)
        ktm = b.sb(pfx + "ktm", [128, 2048], BF16)
        btm = b.sb(pfx + "btm", [128, 2048], BF16)
        vtm = b.sb(pfx + "vtm", [128, 2048], BF16)
        AR = b.sb(pfx + "AR", [128, NCH * 256], BF16)
        BK = b.sb(pfx + "BK", [128, NCH * 256], BF16)
        ysb = b.sb(pfx + "ysb", [128, 2048], BF16)
        stateT = b.sb(pfx + "stT", [128, 1024], BF16)
        hw8 = [{n: b.sb(pfx + "h%d_%s" % (q, n), [128, 128], BF16)
                for n in ["gt", "pt", "ht", "qt", "gm", "y1", "g1", "m0", "m1",
                          "cb", "ub"]} for q in range(8)]
        pr = b.ps(pfx + "prw", [128, 3072], F32)
        pt2 = b.ps(pfx + "pt2", [128, 2048], BF16)
        def rbank(i):
            return pr[:, i * 512:(i + 1) * 512]

        st0 = p.do(VEC, lambda e: e.memset(stateT[:], 0.0))
        ld_g = {}
        rb_g = {i: None for i in range(6)}
        ptg2 = {0: None, 1: None}
        prev_cc = {"tens": None, "vec": None, "scal": None, "ysp": None,
                   "state": [st0] * HA}

        for cc in range(NCC):
            par = cc % 2
            lds = {}
            for n in sn:
                lds[n] = p.dma(SYNC, pfx + "r_" + n,
                               lambda e, n=n, cc=cc, par=par: e.dma_start(
                                   out=stt[n][:, par * 2048:(par + 1) * 2048]
                                       .rearrange("p (c t) -> p c t", t=128),
                                   in_=streams[n].rearrange("(c p) t -> p c t", p=128)
                                       [:, :, cc * 128:(cc + 1) * 128]),
                               after=[ld_g.get((n, par))])
            cur = {n: stt[n][:, par * 2048:(par + 1) * 2048] for n in sn}

            tr_toks = []
            for fc in range(NCH):
                tr = p.do(TENS, lambda e, fc=fc, src=cur["sz"]: e.transpose(
                    pt2[:, (fc % 2) * 1024:(fc % 2) * 1024 + 128],
                    src[:, fc * 128:(fc + 1) * 128], identb[:]),
                    after=[lds["sz"], ptg2[fc % 2]])
                cp = p.do(VEC, lambda e, fc=fc: e.tensor_copy(
                    out=sztm[:, fc * 128:(fc + 1) * 128],
                    in_=pt2[:, (fc % 2) * 1024:(fc % 2) * 1024 + 128]),
                    after=[tr, prev_cc["vec"] if fc == 0 else None])
                ptg2[fc % 2] = cp
                tr_toks.append(cp)
            cs_tm = []
            for q in range(4):
                mt = p.do(TENS, lambda e, q=q: e.matmul(
                    rbank(q), tri_i[:], sztm[:, q * 512:q * 512 + 512],
                    start=True, stop=True), after=tr_toks + [rb_g[q]])
                cs_tm.append(mt)
            ex1 = []
            for q in range(4):
                e1 = p.do(SCAL, lambda e, q=q: e.activation(
                    out=ltm[:, q * 512:q * 512 + 512], in_=rbank(q), func=AF.Exp,
                    scale=SDE), after=[cs_tm[q]] +
                    ([prev_cc["scal"], prev_cc["vec"]] if q == 0 else []))
                rb_g[q] = e1
                ex1.append(e1)
            lam_toks = []
            for fc in range(NCH):
                bki = 4 + (fc % 2)
                mi = p.do(TENS, lambda e, fc=fc, bki=bki: [
                    e.matmul(pr[:, bki * 512:bki * 512 + 128],
                             sztm[:, fc * 128:(fc + 1) * 128], tri_i[:],
                             start=True, stop=True),
                    e.matmul(pr[:, bki * 512 + 128:bki * 512 + 256],
                             sztm[:, fc * 128:(fc + 1) * 128], tri_s[:],
                             start=True, stop=True)][-1],
                    after=tr_toks + [rb_g[bki]])
                e2 = p.do(SCAL, lambda e, fc=fc, bki=bki: [
                    e.activation(out=lamT[:, fc * 128:fc * 128 + 128],
                                 in_=pr[:, bki * 512:bki * 512 + 128],
                                 func=AF.Exp, scale=-SDE),
                    e.activation(out=lamT[:, 2048 + fc * 128:2048 + fc * 128 + 128],
                                 in_=pr[:, bki * 512 + 128:bki * 512 + 256],
                                 func=AF.Exp, scale=-SDE),
                    e.activation(out=lamT[:, 4096 + fc * 128:4096 + fc * 128 + 128],
                                 in_=pr[:, bki * 512:bki * 512 + 128],
                                 func=AF.Exp, scale=SDE),
                    e.activation(out=lamL32[:, fc:fc + 1],
                                 in_=pr[:, bki * 512 + 127:bki * 512 + 128],
                                 func=AF.Exp, scale=-SDE)][-1],
                    after=[mi] + ([prev_cc["vec"], prev_cc["scal"]] if fc == 0 else []))
                rb_g[bki] = e2
                e3 = p.do(VEC, lambda e, fc=fc: e.tensor_scalar(
                    out=lamT[:, 2048 + fc * 128:2048 + fc * 128 + 128],
                    in0=lamT[:, 2048 + fc * 128:2048 + fc * 128 + 128],
                    scalar1=-1.0, scalar2=None, op0=ALU.mult), after=[e2])
                lam_toks.append(e3)

            fm_toks = []
            for fc in range(NCH):
                a1_ = p.do(VEC, lambda e, fc=fc, cur=cur: e.tensor_tensor(
                    out=AR[:, fc * 256:fc * 256 + 128],
                    in0=cur["kk"][:, fc * 128:(fc + 1) * 128],
                    in1=lamT[:, 2048 + fc * 128:2048 + fc * 128 + 128], op=ALU.mult),
                    after=[lam_toks[fc], lds["kk"], prev_cc["tens"]])
                a2_ = p.do(VEC, lambda e, fc=fc, cur=cur: e.tensor_tensor(
                    out=AR[:, fc * 256 + 128:fc * 256 + 256],
                    in0=cur["r"][:, fc * 128:(fc + 1) * 128],
                    in1=lamT[:, fc * 128:fc * 128 + 128], op=ALU.mult),
                    after=[lam_toks[fc], lds["r"], prev_cc["tens"]])
                a3_ = p.do(VEC, lambda e, fc=fc, cur=cur: e.tensor_tensor(
                    out=BK[:, fc * 256:fc * 256 + 128],
                    in0=cur["bb"][:, fc * 128:(fc + 1) * 128],
                    in1=lamT[:, 4096 + fc * 128:4096 + fc * 128 + 128], op=ALU.mult),
                    after=[lam_toks[fc], lds["bb"], prev_cc["tens"]])
                a4_ = p.do(VEC, lambda e, fc=fc, cur=cur: e.tensor_tensor(
                    out=BK[:, fc * 256 + 128:fc * 256 + 256],
                    in0=cur["k2"][:, fc * 128:(fc + 1) * 128],
                    in1=lamT[:, 4096 + fc * 128:4096 + fc * 128 + 128], op=ALU.mult),
                    after=[lam_toks[fc], lds["k2"], prev_cc["tens"]])
                fm_toks += [a1_, a2_, a3_, a4_]

            tm_done = {}
            for n, dsttile in (("k2", ktm), ("bb", btm), ("v", vtm)):
                tts = []
                for fc in range(NCH):
                    tr = p.do(TENS, lambda e, fc=fc, src=cur[n]: e.transpose(
                        pt2[:, (fc % 2) * 1024:(fc % 2) * 1024 + 128],
                        src[:, fc * 128:(fc + 1) * 128], identb[:]),
                        after=[lds[n], ptg2[fc % 2]])
                    cp = p.do(VEC, lambda e, n=n, fc=fc, dsttile=dsttile: e.tensor_copy(
                        out=dsttile[:, fc * 128:(fc + 1) * 128],
                        in_=pt2[:, (fc % 2) * 1024:(fc % 2) * 1024 + 128]),
                        after=[tr, prev_cc["tens"] if fc == 0 else None])
                    ptg2[fc % 2] = cp
                    tts.append(cp)
                tm_done[n] = tts
            ksc = []
            for q in range(4):
                k_ = p.do(VEC, lambda e, q=q: e.tensor_tensor(
                    out=ktm[:, q * 512:q * 512 + 512], in0=ktm[:, q * 512:q * 512 + 512],
                    in1=ltm[:, q * 512:q * 512 + 512], op=ALU.mult),
                    after=tm_done["k2"] + [ex1[q]])
                b_ = p.do(VEC, lambda e, q=q: e.tensor_tensor(
                    out=btm[:, q * 512:q * 512 + 512], in0=btm[:, q * 512:q * 512 + 512],
                    in1=ltm[:, q * 512:q * 512 + 512], op=ALU.mult),
                    after=tm_done["bb"] + [ex1[q]])
                ksc += [k_, b_]
            ld_g[("sz", par)] = lam_toks[-1]
            ld_g[("kk", par)] = fm_toks[-1]
            ld_g[("r", par)] = fm_toks[-1]
            ld_g[("k2", par)] = tm_done["k2"][-1]
            ld_g[("bb", par)] = tm_done["bb"][-1]
            ld_g[("v", par)] = tm_done["v"][-1]

            head_toks = [None] * HA
            ysb_toks = []
            rbc = [0]
            def nxt_bank():
                i = rbc[0] % 6
                rbc[0] += 1
                return i
            for hg in range(0, HA, 8):
                H8 = list(range(hg, hg + 8))
                X = {}
                for h in H8:
                    q = h % 8
                    fc, po = h // 2, (h % 2) * 64
                    X[h] = {
                        "a": AR[po:po + 64, fc * 256:fc * 256 + 128],
                        "r": AR[po:po + 64, fc * 256 + 128:fc * 256 + 256],
                        "b": BK[po:po + 64, fc * 256:fc * 256 + 128],
                        "k": BK[po:po + 64, fc * 256 + 128:fc * 256 + 256],
                        "ar": AR[po:po + 64, fc * 256:fc * 256 + 256],
                        "Ktm": ktm[:, fc * 128 + po:fc * 128 + po + 64],
                        "Btm": btm[:, fc * 128 + po:fc * 128 + po + 64],
                        "Vtm": vtm[:, fc * 128 + po:fc * 128 + po + 64],
                        "stT": stateT[po:po + 64, fc * 64:fc * 64 + 64],
                        "lamL": lamL32[po:po + 64, fc:fc + 1],
                        "po": po, "fc": fc,
                        "eng": VEC if (h % 2 == 0) else SCAL,
                        "w": hw8[q],
                    }
                dep0 = fm_toks + ksc + tm_done["v"] + [t for t in prev_cc["state"] if t]

                def ev_step(eng, dst, src_ap, after):
                    if eng == SCAL:
                        return p.do(SCAL, lambda e, dst=dst, src_ap=src_ap: e.activation(
                            out=dst, in_=src_ap, func=AF.Identity), after=after)
                    return p.do(VEC, lambda e, dst=dst, src_ap=src_ap: e.tensor_copy(
                        out=dst, in_=src_ap), after=after)

                for h in H8:
                    x = X[h]
                    w = x["w"]
                    b0 = nxt_bank(); b1 = nxt_bank(); b2 = nxt_bank()
                    pa_ = p.do(TENS, lambda e, x=x, b0=b0: e.matmul(
                        pr[:, b0 * 512:b0 * 512 + 256], x["b"], x["ar"],
                        start=True, stop=True), after=dep0 + [rb_g[b0]])
                    pb_ = p.do(TENS, lambda e, x=x, b1=b1: e.matmul(
                        pr[:, b1 * 512:b1 * 512 + 256], x["k"], x["ar"],
                        start=True, stop=True), after=dep0 + [rb_g[b1]])
                    pc_ = p.do(TENS, lambda e, x=x, b2=b2: e.matmul(
                        pr[:, b2 * 512:b2 * 512 + 128], x["a"], x["b"],
                        start=True, stop=True), after=dep0 + [rb_g[b2]])
                    ma = p.do(VEC, lambda e, w=w, b0=b0: e.tensor_tensor(
                        out=w["gt"][:], in0=pr[:, b0 * 512:b0 * 512 + 128],
                        in1=msk_su[:], op=ALU.mult), after=[pa_])
                    ma2 = p.do(VEC, lambda e, w=w, b0=b0: e.tensor_tensor(
                        out=w["pt"][:], in0=pr[:, b0 * 512 + 128:b0 * 512 + 256],
                        in1=msk_iu[:], op=ALU.mult), after=[pa_])
                    mb = p.do(SCAL if h % 2 else VEC, None, after=[pb_]) if False else None
                    mb = p.do(VEC, lambda e, w=w, b1=b1: e.tensor_tensor(
                        out=w["ht"][:], in0=pr[:, b1 * 512:b1 * 512 + 128],
                        in1=msk_su[:], op=ALU.mult), after=[pb_])
                    mb2 = p.do(VEC, lambda e, w=w, b1=b1: e.tensor_tensor(
                        out=w["qt"][:], in0=pr[:, b1 * 512:b1 * 512 + 256]
                        [:, 128:256] if False else pr[:, b1 * 512 + 128:b1 * 512 + 256],
                        in1=msk_iu[:], op=ALU.mult), after=[pb_])
                    mc2 = p.do(VEC, lambda e, w=w, b2=b2: e.tensor_tensor(
                        out=w["gm"][:], in0=pr[:, b2 * 512:b2 * 512 + 128],
                        in1=msk_sl[:], op=ALU.mult), after=[pc_])
                    m0t = p.do(VEC, lambda e, w=w: e.tensor_tensor(
                        out=w["m0"][:], in0=w["gt"][:], in1=identb[:], op=ALU.add),
                        after=[ma])
                    rb_g[b0], rb_g[b1], rb_g[b2] = ma2, mb2, mc2
                    x["masks"] = (ma, ma2, mb, mb2, mc2)
                    x["y_tile"], x["y_alt"] = w["gt"], w["y1"]
                    x["g_tile"], x["g_alt"] = w["gm"], w["g1"]
                    x["m_tile"], x["m_alt"] = w["m0"], w["m1"]
                    x["y_tok"], x["g_tok"], x["m_tok"] = ma, mc2, m0t

                for lev in range(1, 7):
                    for h in H8:
                        x = X[h]
                        bk_ = nxt_bank()
                        gt2 = p.do(TENS, lambda e, yt=x["y_tile"], gtl=x["g_tile"],
                                   bk_=bk_: e.matmul(
                            pr[:, bk_ * 512:bk_ * 512 + 128], yt[:], gtl[:],
                            start=True, stop=True),
                            after=[x["y_tok"], x["g_tok"], rb_g[bk_]])
                        ge = ev_step(x["eng"], x["g_alt"][:],
                                     pr[:, bk_ * 512:bk_ * 512 + 128], [gt2])
                        rb_g[bk_] = ge
                        x["g_new_tok"] = ge
                    if lev < 6:
                        for h in H8:
                            x = X[h]
                            bk_ = nxt_bank()
                            yt2 = p.do(TENS, lambda e, yt=x["y_tile"], gtl=x["g_tile"],
                                       bk_=bk_: e.matmul(
                                pr[:, bk_ * 512:bk_ * 512 + 128], gtl[:], yt[:],
                                start=True, stop=True),
                                after=[x["y_tok"], x["g_tok"], rb_g[bk_]])
                            ye = ev_step(x["eng"], x["y_alt"][:],
                                         pr[:, bk_ * 512:bk_ * 512 + 128], [yt2])
                            rb_g[bk_] = ye
                            x["y_new_tok"] = ye
                    for h in H8:
                        x = X[h]
                        bk_ = nxt_bank()
                        mm1 = p.do(TENS, lambda e, mt_=x["m_tile"], ga=x["g_alt"],
                                   bk_=bk_: [
                            e.matmul(pr[:, bk_ * 512:bk_ * 512 + 128], identb[:],
                                     mt_[:], start=True, stop=False),
                            e.matmul(pr[:, bk_ * 512:bk_ * 512 + 128], ga[:],
                                     mt_[:], start=False, stop=True)][-1],
                            after=[x["m_tok"], x["g_new_tok"], rb_g[bk_]])
                        me = ev_step(x["eng"], x["m_alt"][:],
                                     pr[:, bk_ * 512:bk_ * 512 + 128], [mm1])
                        rb_g[bk_] = me
                        x["m_tok"] = me
                        x["m_tile"], x["m_alt"] = x["m_alt"], x["m_tile"]
                        if lev < 6:
                            x["y_tile"], x["y_alt"] = x["y_alt"], x["y_tile"]
                            x["y_tok"] = x["y_new_tok"]
                        x["g_tile"], x["g_alt"] = x["g_alt"], x["g_tile"]
                        x["g_tok"] = x["g_new_tok"]

                for h in H8:
                    x = X[h]
                    bk_ = nxt_bank()
                    cmm = p.do(TENS, lambda e, x=x, bk_=bk_: [
                        e.matmul(pr[:, bk_ * 512:bk_ * 512 + 64], x["a"], x["stT"],
                                 start=True, stop=False),
                        e.matmul(pr[:, bk_ * 512:bk_ * 512 + 64], x["w"]["ht"][:],
                                 x["Vtm"], start=False, stop=True)][-1],
                        after=[rb_g[bk_], x["masks"][2],
                               prev_cc["state"][h] if prev_cc["state"][h] else prev_cc["state"][0] or st0])
                    cev = ev_step(x["eng"], x["w"]["cb"][:, 0:64],
                                  pr[:, bk_ * 512:bk_ * 512 + 64], [cmm])
                    rb_g[bk_] = cev
                    x["cmm"], x["cev"] = cmm, cev
                for h in H8:
                    x = X[h]
                    bk_ = nxt_bank()
                    umm = p.do(TENS, lambda e, mt_=x["m_tile"], cb_=x["w"]["cb"],
                               bk_=bk_: e.matmul(
                        pr[:, bk_ * 512:bk_ * 512 + 64], mt_[:],
                        cb_[:, 0:64], start=True, stop=True),
                        after=[x["m_tok"], x["cev"], rb_g[bk_]])
                    uev = ev_step(x["eng"], x["w"]["ub"][:, 0:64],
                                  pr[:, bk_ * 512:bk_ * 512 + 64], [umm])
                    rb_g[bk_] = uev
                    x["umm"], x["uev"] = umm, uev
                for h in H8:
                    x = X[h]
                    bk_ = nxt_bank()
                    ymm = p.do(TENS, lambda e, x=x, bk_=bk_: [
                        e.matmul(pr[0:64, bk_ * 512:bk_ * 512 + 128], x["stT"], x["r"],
                                 start=True, stop=False),
                        e.matmul(pr[0:64, bk_ * 512:bk_ * 512 + 128], x["w"]["ub"][:, 0:64],
                                 x["w"]["pt"][:], start=False, stop=False),
                        e.matmul(pr[0:64, bk_ * 512:bk_ * 512 + 128], x["Vtm"],
                                 x["w"]["qt"][:], start=False, stop=True)][-1],
                        after=[x["uev"], x["masks"][1], x["masks"][3], rb_g[bk_]])
                    yev = p.do(VEC, lambda e, x=x, bk_=bk_: e.tensor_copy(
                        out=ysb[x["po"]:x["po"] + 64, x["fc"] * 128:x["fc"] * 128 + 128],
                        in_=pr[0:64, bk_ * 512:bk_ * 512 + 128]),
                        after=[ymm, prev_cc["ysp"]])
                    rb_g[bk_] = yev
                    x["ymm"] = ymm
                    ysb_toks.append(yev)
                for h in H8:
                    x = X[h]
                    bk_ = nxt_bank()
                    smm = p.do(TENS, lambda e, x=x, bk_=bk_: [
                        e.matmul(pr[0:64, bk_ * 512:bk_ * 512 + 64],
                                 identb[x["po"]:x["po"] + 64, x["po"]:x["po"] + 64],
                                 x["stT"], start=True, stop=False),
                        e.matmul(pr[0:64, bk_ * 512:bk_ * 512 + 64], x["Btm"],
                                 x["w"]["ub"][:, 0:64], start=False, stop=False),
                        e.matmul(pr[0:64, bk_ * 512:bk_ * 512 + 64], x["Ktm"],
                                 x["Vtm"], start=False, stop=True)][-1],
                        after=[x["uev"], x["ymm"], x["cmm"], rb_g[bk_]])
                    sev = p.do(VEC, lambda e, x=x, bk_=bk_: e.tensor_scalar(
                        out=x["stT"], in0=pr[0:64, bk_ * 512:bk_ * 512 + 64],
                        scalar1=x["lamL"], scalar2=None, op0=ALU.mult), after=[smm])
                    rb_g[bk_] = sev
                    head_toks[h] = sev
            prev_cc["state"] = head_toks
            prev_cc["state"] = head_toks

            ysp = p.dma(SYNC, pfx + "ry", lambda e, cc=cc: e.dma_start(
                out=y_hbm[cc], in_=ysb[:]), after=ysb_toks)
            prev_cc["ysp"] = ysp
            prev_cc["tens"] = Tok(TENS, p.ecnt[TENS])
            prev_cc["vec"] = Tok(VEC, p.ecnt[VEC])
            prev_cc["scal"] = Tok(SCAL, p.ecnt[SCAL])

    with b.phase(pfx + "e"):
        vecs = vecs_keep
        obuf = b.sb(pfx + "ob", [128, NCH * T], BF16)
        yt = b.sb(pfx + "yt", [128, T], BF16)
        gt_ = b.sb(pfx + "gt", [128, T], BF16)
        bot = b.sb(pfx + "bot", [128, T], BF16)
        wslab2 = b.sb(pfx + "ws2", [128, 2 * 2048], BF16)
        xo = b.sb(pfx + "xo", [128, T], F32)
        xrt = b.sb(pfx + "xr", [128, T], F32)
        pe_ = b.ps(pfx + "pe", [128, 3072], F32)
        eg = {"done": None}
        ob_toks = []
        for c in range(NCH):
            ly = p.dma(SYNC, pfx + "ey", lambda e, c=c: e.dma_start(
                out=yt[:].rearrange("p (n t) -> p n t", t=128),
                in_=y_hbm.rearrange("n p f -> p n f")[:, :, c * 128:(c + 1) * 128]),
                after=[eg["done"]])
            lg_ = p.dma(SYNC, pfx + "eg", lambda e, c=c: e.dma_start(
                out=gt_[:], in_=streams["gg"][c * 128:(c + 1) * 128, :]),
                after=[eg["done"]])
            lb = p.dma(SYNC, pfx + "eb", lambda e, c=c: e.dma_start(
                out=bot[:], in_=streams["bo"][c * 128:(c + 1) * 128, :]),
                after=[eg["done"]])
            t1 = p.do(VEC, lambda e, c=c: e.tensor_scalar(
                out=yt[:], in0=yt[:], scalar1=vecs[:, c * 13 + 11:c * 13 + 12],
                scalar2=vecs[:, c * 13 + 12:c * 13 + 13], op0=ALU.mult, op1=ALU.add),
                after=[ly, eg["done"]])
            t2 = p.do(VEC, lambda e: e.tensor_tensor(out=yt[:], in0=yt[:], in1=bot[:],
                                                     op=ALU.add), after=[t1, lb])
            t3 = p.do(VEC, lambda e, c=c: e.tensor_tensor(
                out=obuf[:, c * T:(c + 1) * T], in0=yt[:], in1=gt_[:], op=ALU.mult),
                after=[t2, lg_])
            eg["done"] = t3
            ob_toks.append(t3)
        wuser3 = {0: None, 1: None}
        bank_g3 = {i: None for i in range(6)}
        xrt_g = None
        xo_g = None
        for m in range(NCH):
            par = m % 2
            wt = p.dma(POOL, pfx + "ew%d" % par,
                       lambda e, m=m, par=par: e.dma_start(
                           out=wslab2[:, par * 2048:(par + 1) * 2048],
                           in_=W["a_wo"][m]), after=[wuser3[par]])
            lt = p.dma(SYNC, pfx + "erx",
                       lambda e, m=m: e.dma_start(
                           out=xrt[:], in_=xsrc[m * 128:(m + 1) * 128, :]), after=[xrt_g])
            evs = []
            for nb in range(T // 512):
                bk = nb % 2
                mt = p.do(TENS, lambda e, m=m, par=par, nb=nb: [
                    e.matmul(pe_[:, (nb % 2) * 512:(nb % 2) * 512 + 512],
                             wslab2[:, par * 2048 + kc * 128:par * 2048 + kc * 128 + 128],
                             obuf[:, kc * T + nb * 512:kc * T + nb * 512 + 512],
                             start=(kc == 0), stop=(kc == NCH - 1))
                    for kc in range(NCH)][-1],
                    after=[wt, bank_g3[bk]] + (ob_toks if m == 0 and nb == 0 else []))
                wuser3[par] = mt
                ev = p.do(SCAL, lambda e, nb=nb: e.activation(
                    out=xo[:, nb * 512:nb * 512 + 512],
                    in_=pe_[:, (nb % 2) * 512:(nb % 2) * 512 + 512], func=AF.Identity),
                    after=[mt, xo_g])
                bank_g3[bk] = ev
                ad = p.do(VEC, lambda e, nb=nb: e.tensor_tensor(
                    out=xo[:, nb * 512:nb * 512 + 512], in0=xo[:, nb * 512:nb * 512 + 512],
                    in1=xrt[:, nb * 512:nb * 512 + 512], op=ALU.add), after=[ev, lt])
                evs.append(ad)
            st = p.dma(SYNC, pfx + "esx",
                       lambda e, m=m: e.dma_start(
                           out=dst[m * 128:(m + 1) * 128, :], in_=xo[:]), after=evs)
            xrt_g = st
            xo_g = st


def build_attna_only():
    b = Build()
    xin = b.dram_in("x_in", [D, T], F32)
    out = b.dram_out("x_out", [D, T], F32)
    wshapes = {"a_wr": (16, 128, 2048), "a_wk": (16, 128, 2048), "a_wv": (16, 128, 2048),
               "a_wo": (16, 128, 2048), "a_w1": (128, 1024), "a_a1": (128, 1024),
               "a_g1": (128, 2048), "a_w2": (64, 2048), "a_a2": (64, 2048),
               "a_g2": (128, 2048), "a_vecs": (NCH, 128, 13)}
    W = {k: b.dram_in(k, list(s), F32) for k, s in wshapes.items()}
    emit_attn_a(b, xin, out, W, "r1")
    with b.nc.Block() as blk:
        b.p.emit(blk)
        @blk.sync
        def _(eng):
            for name, rec in b.p.dsems.items():
                eng.wait_ge(rec[0], rec[1])
    return b
